# version 09
# baseline (speedup 1.0000x reference)
import numpy as np
from contextlib import ExitStack

import concourse.bass as bass
import concourse.mybir as mybir
from concourse.bass_utils import run_bass_kernel_spmd

F32 = mybir.dt.float32
BF16 = mybir.dt.bfloat16
U32 = mybir.dt.uint32
I32 = mybir.dt.int32
AF = mybir.ActivationFunctionType
ALU = mybir.AluOpType
AX = mybir.AxisListType

C = 1024
NF = C // 128

ENGINES = ("pe", "dve", "act", "pool", "sp")
DMAQ = ("dsp", "dact", "dpool")
NPOOL = 8


class Buf:
    __slots__ = ("name", "ap", "writer", "readers")

    def __init__(self, name, ap):
        self.name = name
        self.ap = ap
        self.writer = None
        self.readers = []


class Prog:
    def __init__(self, name="prog"):
        self.nc = bass.Bass("TRN2", target_bir_lowering=False)
        self.es = ExitStack()
        self.ops = {e: [] for e in ENGINES}
        self.sems = {}
        self.sem_count = {}
        for e in ENGINES:
            self.sems[e] = self.es.enter_context(self.nc.semaphore("sem_" + e))
            self.sem_count[e] = 0
        self.dma_pool = {}
        self.dma_idx = {}
        for q in DMAQ:
            self.dma_pool[q] = []
            self.dma_idx[q] = 0
            for j in range(NPOOL):
                nm = f"{q}{j}"
                self.sems[nm] = self.es.enter_context(
                    self.nc.semaphore("sem_" + nm))
                self.sem_count[nm] = 0
                self.dma_pool[q].append(nm)
        self.waited = {e: {} for e in ENGINES}
        self.fence = {}
        self.scope_bufs = None
        self.uid = 0
        self._ps_arena = self.es.enter_context(
            self.nc.psum_tensor([128, 4096], F32))
        self._ps_free = [True] * 8
        self.scope_banks = None

    def unq(self, tag):
        self.uid += 1
        return f"{tag}_{self.uid}"

    def shared(self, name, shape, dtype):
        key = (name, tuple(shape), dtype)
        if not hasattr(self, "_shared"):
            self._shared = {}
        if key not in self._shared:
            self._shared[key] = self.sbuf(name, shape, dtype)
        return self._shared[key]

    def sbuf(self, name, shape, dtype, scope=None):
        t = (scope or self.es).enter_context(
            self.nc.sbuf_tensor(self.unq(name), list(shape), dtype))
        b = Buf(name, t.ap() if hasattr(t, "ap") else t)
        if scope is not None:
            self.scope_bufs.append(b)
        return b

    def psum(self, name, shape, dtype=F32, scope=None):
        shape = list(shape)
        nelem = 1
        for s in shape[1:]:
            nelem *= s
        per = 512 if dtype == F32 else 1024
        nbank = (nelem + per - 1) // per
        run = -1
        for i in range(8 - nbank + 1):
            if all(self._ps_free[i:i + nbank]):
                run = i
                break
        if run < 0:
            raise RuntimeError(
                f"psum arena full for {name} ({nbank} banks; "
                f"free={self._ps_free})")
        for i in range(run, run + nbank):
            self._ps_free[i] = False
        base = self._ps_arena.ap() if hasattr(self._ps_arena, "ap") else \
            self._ps_arena
        ap = base[:, run * 512:(run + nbank) * 512]
        if dtype != F32:
            ap = ap.bitcast(dtype)
        ap = ap[:, 0:nelem]
        if len(shape) > 2:
            spec = " ".join(f"d{i}" for i in range(1, len(shape)))
            ap = ap.rearrange(f"p ({spec}) -> p {spec}",
                              **{f"d{i}": shape[i]
                                 for i in range(1, len(shape))})
        elif shape[0] != 128:
            ap = ap[0:shape[0], :]
        b = Buf(name, ap)
        banks = (run, nbank)
        if scope is not None:
            self.scope_bufs.append(b)
            self.scope_banks.append(banks)
        else:
            pass
        return b

    def dram_in(self, name, shape, dtype):
        return Buf(name, self.nc.dram_tensor(name, list(shape), dtype,
                                             kind="ExternalInput").ap())

    def dram_out(self, name, shape, dtype):
        return Buf(name, self.nc.dram_tensor(name, list(shape), dtype,
                                             kind="ExternalOutput").ap())

    def dram_tmp(self, name, shape, dtype):
        return Buf(name, self.nc.dram_tensor(name, list(shape), dtype,
                                             kind="Internal").ap())

    class _Scope:
        def __init__(self, p):
            self.p = p
            self.es = ExitStack()
            self.prev = None

        def __enter__(self):
            self.prev = self.p.scope_bufs
            self.prev_banks = self.p.scope_banks
            self.p.scope_bufs = []
            self.p.scope_banks = []
            return self.es

        def __exit__(self, *a):
            for b in self.p.scope_bufs:
                toks = list(b.readers)
                if b.writer is not None:
                    toks.append(b.writer)
                for t in toks:
                    if t is not None:
                        s, v = t
                        if self.p.fence.get(s, 0) < v:
                            self.p.fence[s] = v
            for run, nbank in self.p.scope_banks:
                for i in range(run, run + nbank):
                    self.p._ps_free[i] = True
            self.p.scope_bufs = self.prev
            self.p.scope_banks = self.prev_banks
            self.es.close()
            return False

    def scope(self):
        return Prog._Scope(self)

    def emit(self, eng, fn, reads=(), writes=(), war=(), dma_queue=None,
             inc=True):
        waits = {}

        def add_wait(tok):
            if tok is None:
                return
            s, v = tok
            if waits.get(s, 0) < v:
                waits[s] = v

        for b in reads:
            add_wait(b.writer)
        for b in writes:
            add_wait(b.writer)
            for t in b.readers:
                add_wait(t)
            for s, v in self.fence.items():
                if waits.get(s, 0) < v:
                    waits[s] = v
        for b in war:
            add_wait(b.writer)
            for t in b.readers:
                add_wait(t)
            for s, v in self.fence.items():
                if waits.get(s, 0) < v:
                    waits[s] = v

        if dma_queue is not None:
            nm = self.dma_pool[dma_queue][self.dma_idx[dma_queue] % NPOOL]
            self.dma_idx[dma_queue] += 1
            prev = self.sem_count[nm]
            if prev and waits.get(nm, 0) < prev:
                waits[nm] = prev
            self.sem_count[nm] = prev + 16
            tok = (nm, prev + 16)
            sem, amt = self.sems[nm], 16
        elif inc:
            self.sem_count[eng] += 1
            tok = (eng, self.sem_count[eng])
            sem, amt = self.sems[eng], 1
        else:
            tok, sem, amt = None, None, 0

        self.ops[eng].append((waits, fn, sem, amt))

        for b in reads:
            if tok is not None:
                b.readers.append(tok)
        for b in writes:
            b.writer = tok
            b.readers = []
        return tok

    def finalize(self):
        final_waits = {}
        for q in DMAQ:
            for nm in self.dma_pool[q]:
                if self.sem_count[nm]:
                    final_waits[nm] = self.sem_count[nm]
        self.ops["sp"].append((final_waits, lambda e: None, None, 0))
        nc = self.nc
        with nc.Block() as block:
            for eng in ENGINES:
                opslist = self.ops[eng]
                waited = self.waited[eng]
                sems = self.sems

                def replay(engobj, opslist=opslist, waited=waited):
                    for waits, fn, sem, amt in opslist:
                        for s, v in waits.items():
                            if waited.get(s, 0) >= v:
                                continue
                            engobj.wait_ge(sems[s], v)
                            waited[s] = v
                        inst = fn(engobj)
                        if sem is not None:
                            if inst is None:
                                raise RuntimeError("op returned no inst")
                            inst.then_inc(sem, amt)

                {"pe": block.tensor, "dve": block.vector, "act": block.scalar,
                 "pool": block.gpsimd, "sp": block.sync}[eng](replay)
        nc.finalize()
        return nc



def mvs(T, w=512):
    w = min(w, T)
    return [(i * w, w) for i in range(T // w)]


def setup_consts(p, T=None, big_pool=False, wkv_pool=False):
    if T is not None:
        p.shared("sh_wraw", (128, 128), BF16)
        p.shared("sh_wlhs", (128, 128), BF16)
        for j in range(3):
            p.shared(f"sh_wt{j}", (128, 128), BF16)
            if big_pool:
                p.shared(f"sh_ld{j}", (128, T), BF16)
        pass
    onesf = p.sbuf("c_onesf", (128, 128), F32)
    p.emit("dve", lambda e: e.memset(onesf.ap[:], 1.0), writes=(onesf,))
    p.onesf = onesf
    p.idn32 = p.sbuf("c_idn32", (128, 128), F32)
    p.emit("pool", lambda e: e.affine_select(
        p.idn32.ap[:], onesf.ap[:], pattern=[[-1, 128]],
        compare_op=ALU.is_equal, fill=0.0, base=0, channel_multiplier=1),
        reads=(onesf,), writes=(p.idn32,))
    p.idn = p.sbuf("c_idn", (128, 128), BF16)
    p.emit("dve", lambda e: e.tensor_copy(p.idn.ap[:], p.idn32.ap[:]),
           reads=(p.idn32,), writes=(p.idn,))
    p.ones2 = p.sbuf("c_ones2", (128, 2), BF16)
    p.emit("dve", lambda e: e.memset(p.ones2.ap[:], 0.0), writes=(p.ones2,))
    p.emit("dve", lambda e: e.memset(p.ones2.ap[0:64, 0:1], 1.0),
           writes=(p.ones2,))
    p.emit("dve", lambda e: e.memset(p.ones2.ap[64:128, 1:2], 1.0),
           writes=(p.ones2,))
    p.rep2 = p.sbuf("c_rep2", (2, 128), BF16)
    with p.scope() as sc:
        ps = p.psum("c_psr2", (2, 128), BF16, scope=sc)
        p.emit("pe", lambda e: e.transpose(ps.ap[:], p.ones2.ap[:],
                                           p.idn.ap[:]),
               reads=(p.ones2, p.idn), writes=(ps,))
        p.emit("act", lambda e: e.copy(p.rep2.ap[:], ps.ap[:]),
               reads=(ps,), writes=(p.rep2,))
    p.ones_bf = p.sbuf("c_ones_bf", (128, 128), BF16)
    p.emit("dve", lambda e: e.memset(p.ones_bf.ap[:], 1.0), writes=(p.ones_bf,))
    p.mask8 = p.sbuf("c_mask8", (128, 64), F32)
    p.emit("dve", lambda e: e.memset(p.mask8.ap[:], 0.0), writes=(p.mask8,))
    p.emit("pool", lambda e: e.affine_select(
        p.mask8.ap[:].rearrange("p (t j) -> p t j", t=8),
        p.mask8.ap[:].rearrange("p (t j) -> p t j", t=8),
        pattern=[[1, 8], [-1, 8]], compare_op=ALU.is_ge, fill=-1e9,
        base=-1, channel_multiplier=0),
        reads=(p.mask8,), writes=(p.mask8,))
    p.iota8 = p.sbuf("c_iota8", (128, 8), F32)
    p.emit("pool", lambda e: e.iota(p.iota8.ap[:], pattern=[[1, 8]], base=0,
                                    channel_multiplier=0,
                                    allow_small_or_imprecise_dtypes=True),
           writes=(p.iota8,))
    p.ones2f = p.sbuf("c_ones2f", (128, 2), F32)
    p.emit("dve", lambda e: e.tensor_copy(p.ones2f.ap[:], p.ones2.ap[:]),
           reads=(p.ones2,), writes=(p.ones2f,))
    p.rep2f = p.sbuf("c_rep2f", (2, 128), F32)
    p.emit("dve", lambda e: e.tensor_copy(p.rep2f.ap[:], p.rep2.ap[:]),
           reads=(p.rep2,), writes=(p.rep2f,))


def load_featvec(p, name, src_ap_1d, n, scope=None, dtype=F32, out=None,
                 outsl=None):
    nt = n // 128
    if out is None:
        out = p.sbuf(name + "_fv", (128, nt), dtype, scope=scope)
        outsl = (slice(None), slice(None))
    with p.scope() as sc:
        stage = p.sbuf(name + "_st", (nt, 128), F32, scope=sc)
        srcbuf = Buf(name + "_src", src_ap_1d)

        def f_ld(e):
            return e.dma_start(out=stage.ap[:],
                               in_=src_ap_1d.rearrange("(a b) -> a b", b=128))
        p.emit("sp", f_ld, reads=(srcbuf,), writes=(stage,), dma_queue="dsp")
        ps = p.psum(name + "_ps", (128, nt), F32, scope=sc)
        p.emit("pe", lambda e: e.transpose(ps.ap[:], stage.ap[:],
                                           p.idn32.ap[0:nt, 0:nt]),
               reads=(stage, p.idn32), writes=(ps,))
        p.emit("act", lambda e: e.copy(out.ap[outsl], ps.ap[:]),
               reads=(ps,), writes=(out,))
    return out


def dump_fm(p, tiles, T, name, dtype=F32):
    nt = len(tiles)
    out = p.dram_out(name, (nt * 128, T), dtype)
    for fi in range(nt):
        def f(e, fi=fi):
            return e.dma_start(out=out.ap[fi * 128:(fi + 1) * 128, :],
                               in_=tiles[fi].ap[:])
        p.emit("sp", f, reads=(tiles[fi],), writes=(out,), dma_queue="dsp")
    return out


def dump_tile(p, buf, name, shape, dtype=F32):
    out = p.dram_out(name, list(shape), dtype)
    p.emit("sp", lambda e: e.dma_start(out=out.ap[:], in_=buf.ap[:]),
           reads=(buf,), writes=(out,), dma_queue="dsp")
    return out



def b0_entry(p, T, x_in, xn_fm):
    with p.scope() as sc:
        stage = [p.sbuf(f"e_stage{j}", (128, C), F32, scope=sc)
                 for j in range(2)]
        xnt = [p.sbuf(f"e_xnt{j}", (128, C), BF16, scope=sc) for j in range(2)]
        ssum = p.sbuf("e_ssum", (128, 8), F32, scope=sc)
        ps_tr = [p.psum(f"e_pstr{j}", (128, 128), BF16, scope=sc)
                 for j in range(2)]
        for tt in range(T // 128):
            st = stage[tt % 2]
            xn_t = xnt[tt % 2]

            def f_ld(e, tt=tt, st=st):
                return e.dma_start(out=st.ap[:],
                                   in_=x_in.ap[tt * 128:(tt + 1) * 128, :])
            p.emit("sp", f_ld, reads=(x_in,), writes=(st,), dma_queue="dsp")

            def f_sq2(e, st=st, xn_t=xn_t):
                return e.activation(xn_t.ap[:], st.ap[:], AF.Square,
                                    accum_out=ssum.ap[:, 0:1])
            p.emit("act", f_sq2, reads=(st,), writes=(ssum,), war=(xn_t,))

            def f_sum(e, st=st):
                return e.tensor_reduce(ssum.ap[:, 1:2], st.ap[:], AX.X, ALU.add)
            p.emit("dve", f_sum, reads=(st,), writes=(ssum,))

            def f_mean(e):
                return e.tensor_scalar(ssum.ap[:, 2:3], ssum.ap[:, 1:2],
                                       1.0 / C, None, ALU.mult)
            p.emit("dve", f_mean, reads=(ssum,), writes=(ssum,))

            def f_msq(e):
                return e.tensor_tensor(ssum.ap[:, 3:4], ssum.ap[:, 2:3],
                                       ssum.ap[:, 2:3], ALU.mult)
            p.emit("dve", f_msq, reads=(ssum,), writes=(ssum,))

            def f_var(e):
                return e.scalar_tensor_tensor(ssum.ap[:, 4:5], ssum.ap[:, 0:1],
                                              1.0 / C, ssum.ap[:, 3:4],
                                              ALU.mult, ALU.subtract)
            p.emit("dve", f_var, reads=(ssum,), writes=(ssum,))

            def f_vep(e):
                return e.tensor_scalar(ssum.ap[:, 5:6], ssum.ap[:, 4:5],
                                       1e-5, None, ALU.add)
            p.emit("dve", f_vep, reads=(ssum,), writes=(ssum,))

            def f_rec(e):
                return e.reciprocal(ssum.ap[:, 6:7], ssum.ap[:, 5:6])
            p.emit("dve", f_rec, reads=(ssum,), writes=(ssum,))

            def f_rstd(e):
                return e.activation(ssum.ap[:, 7:8], ssum.ap[:, 6:7], AF.Sqrt)
            p.emit("act", f_rstd, reads=(ssum,), writes=(ssum,))

            def f_xn(e, st=st, xn_t=xn_t):
                return e.tensor_scalar(xn_t.ap[:], st.ap[:], ssum.ap[:, 2:3],
                                       ssum.ap[:, 7:8], ALU.subtract, ALU.mult)
            p.emit("dve", f_xn, reads=(st, ssum), writes=(xn_t,))

            for fi in range(NF):
                pst = ps_tr[fi % 2]

                def f_t1(e, fi=fi, xn_t=xn_t, pst=pst):
                    return e.transpose(pst.ap[:],
                                       xn_t.ap[:, fi * 128:(fi + 1) * 128],
                                       p.idn.ap[:])
                p.emit("pe", f_t1, reads=(xn_t, p.idn), writes=(pst,))

                def f_e1(e, fi=fi, tt=tt, pst=pst):
                    return e.copy(xn_fm[fi].ap[:, tt * 128:(tt + 1) * 128],
                                  pst.ap[:])
                p.emit("act", f_e1, reads=(pst,), writes=(xn_fm[fi],))



def b0_mixer(p, T, xn_fm, W, maa, dbg, xk_fm, xv_fm, xr_fm):
    with p.scope() as sc:
        xx_t = p.sbuf("xx_t", (128, T), BF16, scope=sc)

        def build_xx(fi):
            xnf = xn_fm[fi]

            def f_xx(e, xnf=xnf):
                return e.tensor_tensor(xx_t.ap[:, 1:T], xnf.ap[:, 0:T - 1],
                                       xnf.ap[:, 1:T], ALU.subtract)
            p.emit("dve", f_xx, reads=(xnf,), writes=(xx_t,))

            def f_xx0(e, xnf=xnf):
                return e.tensor_scalar(xx_t.ap[:, 0:1], xnf.ap[:, 0:1],
                                       -1.0, None, ALU.mult)
            p.emit("dve", f_xx0, reads=(xnf,), writes=(xx_t,))
        w1sb = p.sbuf("w1sb", (128, NF, 96), BF16, scope=sc)
        p.emit("pool", lambda e: e.dma_start(
            out=w1sb.ap[:],
            in_=W["w1"].ap.rearrange("(a p) m -> p a m", p=128)),
            reads=(W["w1"],), writes=(w1sb,), dma_queue="dpool")
        w2sb = p.sbuf("w2sb", (96, C), BF16, scope=sc)
        p.emit("pool", lambda e: e.dma_start(
            out=w2sb.ap[:], in_=W["w2"].ap.rearrange("a b c -> (a b) c")),
            reads=(W["w2"],), writes=(w2sb,), dma_queue="dpool")

        ps_m = p.psum("ps_m", (96, T), F32, scope=sc)
        xxx = p.sbuf("xxx_t", (128, T), BF16, scope=sc)
        for fi in range(NF):
            xnf = xn_fm[fi]
            build_xx(fi)

            def f_xxx(e, fi=fi, xnf=xnf):
                return e.scalar_tensor_tensor(
                    xxx.ap[:], xx_t.ap[:], maa["maa_x"].ap[:, fi:fi + 1],
                    xnf.ap[:], ALU.mult, ALU.add)
            p.emit("dve", f_xxx, reads=(xx_t, xnf, maa["maa_x"]),
                   writes=(xxx,))

            sl = mvs(T)
            for mi, (m0, mw) in enumerate(sl):
                def f_mm(e, fi=fi, m0=m0, mw=mw):
                    return e.matmul(ps_m.ap[:, m0:m0 + mw], w1sb.ap[:, fi, :],
                                    xxx.ap[:, m0:m0 + mw],
                                    start=(fi == 0), stop=(fi == NF - 1))
                last = (mi == len(sl) - 1)
                p.emit("pe", f_mm, reads=(w1sb, xxx),
                       writes=(ps_m,) if (fi == NF - 1 and last) else (),
                       war=(ps_m,) if (fi == 0 and mi == 0) else (), inc=last)

        m_tm = p.sbuf("m_tm", (96, T), BF16, scope=sc)
        p.emit("act", lambda e: e.activation(m_tm.ap[:], ps_m.ap[:], AF.Tanh),
               reads=(ps_m,), writes=(m_tm,))

        ps_mk = p.psum("ps_mk", (128, T), F32, scope=sc)
        tmp1 = p.sbuf("mix_tmp1", (128, T), F32, scope=sc)
        branch = [("maa_k", xk_fm), ("maa_v", xv_fm), ("maa_r", xr_fm)]
        for l in range(3):
            maa_nm, dst = branch[l]
            for fi in range(NF):
                build_xx(fi)
                sl = mvs(T)
                for mi, (m0, mw) in enumerate(sl):
                    def f_mm(e, l=l, fi=fi, m0=m0, mw=mw):
                        return e.matmul(
                            ps_mk.ap[:, m0:m0 + mw],
                            w2sb.ap[l * 32:(l + 1) * 32,
                                    fi * 128:(fi + 1) * 128],
                            m_tm.ap[l * 32:(l + 1) * 32, m0:m0 + mw],
                            start=True, stop=True)
                    p.emit("pe", f_mm, reads=(w2sb, m_tm),
                           writes=(ps_mk,) if mi == len(sl) - 1 else (),
                           war=(ps_mk,) if mi == 0 else (),
                           inc=(mi == len(sl) - 1))

                def f_t1(e, maa_nm=maa_nm, fi=fi):
                    return e.tensor_scalar(tmp1.ap[:], ps_mk.ap[:],
                                           maa[maa_nm].ap[:, fi:fi + 1],
                                           None, ALU.add)
                p.emit("dve", f_t1, reads=(ps_mk, maa[maa_nm]), writes=(tmp1,))

                def f_t2(e, fi=fi):
                    return e.tensor_tensor(tmp1.ap[:], tmp1.ap[:],
                                           xx_t.ap[:], ALU.mult)
                p.emit("dve", f_t2, reads=(tmp1, xx_t), writes=(tmp1,))

                def f_t3(e, fi=fi, dst=dst):
                    return e.tensor_tensor(dst[fi].ap[:], tmp1.ap[:],
                                           xn_fm[fi].ap[:], ALU.add)
                p.emit("dve", f_t3, reads=(tmp1, xn_fm[fi]), writes=(dst[fi],))
    return xk_fm, xv_fm, xr_fm



def wkv_hp(p, T, hp, r, k, v, ew, u_fm, o_hp):
    LCH = 32
    nch = T // LCH

    with p.scope() as sc0:
        Lpad = p.sbuf("wk_Lpad", (128, T + 1), F32, scope=sc0)
        negone = p.sbuf("wk_neg1", (128, 1), F32, scope=sc0)
        p.emit("dve", lambda e: e.memset(negone.ap[:], -1.0), writes=(negone,))
        p.emit("dve", lambda e: e.memset(Lpad.ap[:, 0:1], 0.0), writes=(Lpad,))

        def f_scan(e):
            return e.tensor_tensor_scan(
                Lpad.ap[:, 1:T + 1], ew.ap[:],
                negone.ap[:].broadcast_to((128, T)), 0.0,
                ALU.subtract, ALU.mult)
        p.emit("dve", f_scan, reads=(ew, negone), writes=(Lpad,))

        R32 = p.sbuf("wk_R32", (128, T), BF16, scope=sc0)
        K32 = p.sbuf("wk_K32", (128, T), BF16, scope=sc0)
        R16 = p.sbuf("wk_R16", (128, T), BF16, scope=sc0)
        K16 = p.sbuf("wk_K16", (128, T), BF16, scope=sc0)
        scdk = p.scope()
        scdk_es = scdk.__enter__()
        dtmp = p.sbuf("wk_dtmp", (128, T), F32, scope=scdk_es)

        def decay_build(out, base_buf, kind, blk):
            nb = T // blk
            if kind == "r":
                i0 = Lpad.ap[:, 0:T].rearrange("p (b t) -> p b t", t=blk)
                i1 = Lpad.ap[:, 0:T].rearrange(
                    "p (b t) -> p b t", t=blk)[:, :, 0:1].broadcast_to(
                    (128, nb, blk))
            else:
                i0 = Lpad.ap[:, 1:T + 1].rearrange(
                    "p (b t) -> p b t", t=blk)[:, :, blk - 1:blk].broadcast_to(
                    (128, nb, blk))
                i1 = Lpad.ap[:, 1:T + 1].rearrange("p (b t) -> p b t", t=blk)

            def f_sub(e):
                return e.tensor_tensor(
                    dtmp.ap[:].rearrange("p (b t) -> p b t", t=blk), i0, i1,
                    ALU.subtract)
            p.emit("dve", f_sub, reads=(Lpad,), writes=(dtmp,))
            p.emit("act", lambda e: e.activation(dtmp.ap[:], dtmp.ap[:],
                                                 AF.Exp),
                   reads=(dtmp,), writes=(dtmp,))

            def f_mul(e):
                return e.tensor_tensor(out.ap[:], dtmp.ap[:], base_buf.ap[:],
                                       ALU.mult)
            p.emit("dve", f_mul, reads=(dtmp, base_buf), writes=(out,))
            return out

        decay_build(R32, r, "r", 32)
        decay_build(K32, k, "k", 32)
        decay_build(R16, r, "r", 16)
        decay_build(K16, k, "k", 16)
        scdk.__exit__(None, None, None)

        sdec = p.sbuf("wk_sdec", (128, nch), F32, scope=sc0)

        def f_sd(e):
            i0 = Lpad.ap[:, 1:T + 1].rearrange(
                "p (b t) -> p b t", t=LCH)[:, :, LCH - 1:LCH]
            i1 = Lpad.ap[:, 0:T].rearrange(
                "p (b t) -> p b t", t=LCH)[:, :, 0:1]
            return e.tensor_tensor(sdec.ap[:].unsqueeze(2), i0, i1,
                                   ALU.subtract)
        p.emit("dve", f_sd, reads=(Lpad,), writes=(sdec,))
        p.emit("act", lambda e: e.activation(sdec.ap[:], sdec.ap[:], AF.Exp),
               reads=(sdec,), writes=(sdec,))

        if getattr(p, "wkv_stop", 99) <= 0:
            def f_dbg(e):
                return e.tensor_copy(o_hp.ap[:], R32.ap[:])
            p.emit("dve", f_dbg, reads=(R32,), writes=(o_hp,))
            return

        vT = p.sbuf("wk_vT", (128, T), BF16, scope=sc0)
        K32T = p.sbuf("wk_K32T", (128, T), BF16, scope=sc0)
        with p.scope() as sctr:
            ps_tr = [p.psum(f"wk_pstr{j}", (128, 128), BF16, scope=sctr)
                     for j in range(2)]
            i = 0
            for bi in range(T // 128):
                for src, dst in ((v, vT), (K32, K32T)):
                    pst = ps_tr[i % 2]
                    i += 1

                    def f_tr(e, bi=bi, src=src, pst=pst):
                        return e.transpose(pst.ap[:],
                                           src.ap[:, bi * 128:(bi + 1) * 128],
                                           p.idn.ap[:])
                    p.emit("pe", f_tr, reads=(src, p.idn), writes=(pst,))

                    def f_ev(e, bi=bi, dst=dst, pst=pst):
                        return e.copy(dst.ap[:, bi * 128:(bi + 1) * 128],
                                      pst.ap[:])
                    p.emit("act", f_ev, reads=(pst,), writes=(dst,))

        if getattr(p, "wkv_stop", 99) <= 1:
            def f_dbg(e):
                return e.tensor_copy(o_hp.ap[:], vT.ap[:])
            p.emit("dve", f_dbg, reads=(vT,), writes=(o_hp,))
            return

        pc = p.sbuf("wk_pc", (128, T), F32, scope=sc0)
        with p.scope() as scdu:
            ruk = p.sbuf("wk_ruk", (128, T), BF16, scope=scdu)

            def f_ru(e):
                return e.tensor_scalar(ruk.ap[:], r.ap[:],
                                       u_fm.ap[:, hp:hp + 1], None, ALU.mult)
            p.emit("dve", f_ru, reads=(r, u_fm), writes=(ruk,))
            p.emit("dve", lambda e: e.tensor_tensor(ruk.ap[:], ruk.ap[:],
                                                    k.ap[:], ALU.mult),
                   reads=(ruk, k), writes=(ruk,))

            ps_du = p.psum("wk_psdu", (2, T), F32, scope=scdu)
            sl = mvs(T)
            for mi, (m0, mw) in enumerate(sl):
                def f_mm(e, m0=m0, mw=mw):
                    return e.matmul(ps_du.ap[:, m0:m0 + mw], p.ones2.ap[:],
                                    ruk.ap[:, m0:m0 + mw],
                                    start=True, stop=True)
                p.emit("pe", f_mm, reads=(p.ones2, ruk),
                       writes=(ps_du,) if mi == len(sl) - 1 else (),
                       war=(ps_du,) if mi == 0 else (), inc=(mi == len(sl) - 1))
            du_sb = p.sbuf("wk_du", (2, T), BF16, scope=scdu)
            p.emit("act", lambda e: e.copy(du_sb.ap[:], ps_du.ap[:]),
                   reads=(ps_du,), writes=(du_sb,))

            ps_duB = p.psum("wk_psduB", (128, T), F32, scope=scdu)
            for mi, (m0, mw) in enumerate(sl):
                def f_mm(e, m0=m0, mw=mw):
                    return e.matmul(ps_duB.ap[:, m0:m0 + mw], p.rep2.ap[:],
                                    du_sb.ap[:, m0:m0 + mw],
                                    start=True, stop=True)
                p.emit("pe", f_mm, reads=(p.rep2, du_sb),
                       writes=(ps_duB,) if mi == len(sl) - 1 else (),
                       war=(ps_duB,) if mi == 0 else (),
                       inc=(mi == len(sl) - 1))
            p.emit("dve", lambda e: e.tensor_tensor(pc.ap[:], ps_duB.ap[:],
                                                    v.ap[:], ALU.mult),
                   reads=(ps_duB, v), writes=(pc,))

        if getattr(p, "wkv_stop", 99) <= 2:
            def f_dbg(e):
                return e.tensor_copy(o_hp.ap[:], pc.ap[:])
            p.emit("dve", f_dbg, reads=(pc,), writes=(o_hp,))
            return

        SEG = min(256, T)
        QCH = 1024
        with p.scope() as scpw:
            nbA = SEG // 8
            P0 = p.wkv_bufs["P0"]
            E = p.wkv_bufs["E"]
            RK = p.wkv_bufs["RK"]
            OIt = p.wkv_bufs["OIt"]
            OIr = p.wkv_bufs["OIr"]
            ps_M = p.psum("wk_psM", (2, QCH), F32, scope=scpw)
            ps_R = p.psum("wk_psR", (128, QCH), F32, scope=scpw)
            M_sb = p.wkv_bufs["Msb"]

            def pairwise(s0, nb, tb, span, off_t, mask):
                npair = nb * tb * tb

                def tview(ap_src, off, bcast_ax):
                    if off + tb <= span:
                        a = ap_src[:, s0: s0 + span * nb].rearrange(
                            "p (b x) -> p b x", x=span)[:, :, off:off + tb]
                    else:
                        a = ap_src[:, s0 + off: s0 + off + span * nb].rearrange(
                            "p (b x) -> p b x", x=span)[:, :, 0:tb]
                    return a.unsqueeze(bcast_ax).broadcast_to(
                        (128, nb, tb, tb))

                def f_sub(e):
                    return e.tensor_tensor(
                        P0.ap[:, 0:npair].rearrange(
                            "p (b t j) -> p b t j", t=tb, j=tb),
                        tview(Lpad.ap, off_t, 3), tview(Lpad.ap, 1, 2),
                        ALU.subtract)
                p.emit("dve", f_sub, reads=(Lpad,), writes=(P0,))

                if mask:
                    def f_mask(e):
                        mb = p.mask8.ap[:].rearrange(
                            "p (t j) -> p t j", t=tb).unsqueeze(
                            1).broadcast_to((128, nb, tb, tb))
                        return e.tensor_tensor(
                            P0.ap[:, 0:npair].rearrange(
                                "p (b t j) -> p b t j", t=tb, j=tb),
                            P0.ap[:, 0:npair].rearrange(
                                "p (b t j) -> p b t j", t=tb, j=tb),
                            mb, ALU.add)
                    p.emit("dve", f_mask, reads=(P0, p.mask8), writes=(P0,))

                p.emit("act", lambda e: e.activation(
                    E.ap[:, 0:npair], P0.ap[:, 0:npair], AF.Exp),
                    reads=(P0,), writes=(E,))

                def f_rk(e):
                    return e.tensor_tensor(
                        RK.ap[:, 0:npair].rearrange(
                            "p (b t j) -> p b t j", t=tb, j=tb),
                        tview(r.ap, off_t, 3), tview(k.ap, 0, 2), ALU.mult)
                p.emit("dve", f_rk, reads=(r, k), writes=(RK,))

                def f_p(e):
                    return e.tensor_tensor(E.ap[:, 0:npair], E.ap[:, 0:npair],
                                           RK.ap[:, 0:npair], ALU.mult)
                p.emit("dve", f_p, reads=(E, RK), writes=(E,))

                for q0 in range(0, npair, QCH):
                    qw = min(QCH, npair - q0)
                    sl = mvs(qw)
                    for mi, (m0, mw) in enumerate(sl):
                        def f_mm(e, q0=q0, m0=m0, mw=mw):
                            return e.matmul(ps_M.ap[:, m0:m0 + mw],
                                            p.ones2f.ap[:],
                                            E.ap[:, q0 + m0:q0 + m0 + mw],
                                            start=True, stop=True)
                        p.emit("pe", f_mm, reads=(p.ones2f, E),
                               writes=(ps_M,) if mi == len(sl) - 1 else (),
                               war=(ps_M,) if mi == 0 else (),
                               inc=(mi == len(sl) - 1))

                    def f_mev(e, qw=qw):
                        return e.copy(M_sb.ap[:, 0:qw], ps_M.ap[:, 0:qw])
                    p.emit("act", f_mev, reads=(ps_M,), writes=(M_sb,))

                    for mi, (m0, mw) in enumerate(sl):
                        def f_mm(e, m0=m0, mw=mw):
                            return e.matmul(ps_R.ap[:, m0:m0 + mw],
                                            p.rep2f.ap[:],
                                            M_sb.ap[:, m0:m0 + mw],
                                            start=True, stop=True)
                        p.emit("pe", f_mm, reads=(p.rep2f, M_sb),
                               writes=(ps_R,) if mi == len(sl) - 1 else (),
                               war=(ps_R,) if mi == 0 else (),
                               inc=(mi == len(sl) - 1))

                    nbq = qw // (tb * tb)
                    b0 = q0 // (tb * tb)

                    def f_oim(e, qw=qw, nbq=nbq, b0=b0):
                        vb = v.ap[:, s0 + b0 * span:
                                  s0 + b0 * span + span * nbq].rearrange(
                            "p (b x) -> p b x", x=span)[:, :, 0:tb].unsqueeze(
                            2).broadcast_to((128, nbq, tb, tb))
                        return e.tensor_tensor(
                            OIt.ap[:, 0:qw].rearrange(
                                "p (b t j) -> p b t j", t=tb, j=tb),
                            ps_R.ap[:, 0:qw].rearrange(
                                "p (b t j) -> p b t j", t=tb, j=tb),
                            vb, ALU.mult)
                    p.emit("dve", f_oim, reads=(ps_R, v), writes=(OIt,))

                    def f_oir(e, qw=qw, nbq=nbq, b0=b0):
                        return e.tensor_reduce(
                            OIr.ap[:, b0 * tb: (b0 + nbq) * tb].rearrange(
                                "p (b t) -> p b t", t=tb).unsqueeze(3),
                            OIt.ap[:, 0:qw].rearrange(
                                "p (b t j) -> p b t j", t=tb, j=tb),
                            AX.X, ALU.add)
                    p.emit("dve", f_oir, reads=(OIt,), writes=(OIr,))

                def f_acc(e):
                    if off_t + tb <= span:
                        dst = pc.ap[:, s0: s0 + span * nb].rearrange(
                            "p (b x) -> p b x", x=span)[:, :, off_t:off_t + tb]
                    else:
                        dst = pc.ap[:, s0 + off_t: s0 + off_t +
                                    span * nb].rearrange(
                            "p (b x) -> p b x", x=span)[:, :, 0:tb]
                    return e.tensor_tensor(
                        dst, dst, OIr.ap[:, 0:nb * tb].rearrange(
                            "p (b t) -> p b t", t=tb), ALU.add)
                p.emit("dve", f_acc, reads=(pc, OIr), writes=(pc,))

            for sg in range(T // SEG):
                s0 = sg * SEG
                pairwise(s0, SEG // 8, 8, 8, 0, True)
                pairwise(s0, SEG // 16, 8, 16, 8, False)

        if getattr(p, "wkv_dbg", False) and hp == 0:
            dump_tile(p, Lpad, "dbgw_Lpad", (128, T + 1), F32)
            dump_tile(p, R32, "dbgw_R32", (128, T), BF16)
            dump_tile(p, K32, "dbgw_K32", (128, T), BF16)
            dump_tile(p, R16, "dbgw_R16", (128, T), BF16)
            dump_tile(p, K16, "dbgw_K16", (128, T), BF16)
            dump_tile(p, sdec, "dbgw_sdec", (128, T // LCH), F32)
            dump_tile(p, pc, "dbgw_pc", (128, T), F32)

        if getattr(p, "wkv_stop", 99) <= 3:
            def f_dbg(e):
                return e.tensor_copy(o_hp.ap[:], pc.ap[:])
            p.emit("dve", f_dbg, reads=(pc,), writes=(o_hp,))
            return

        with p.scope() as scch:
            S_sb = p.sbuf("wk_S", (128, 128), F32, scope=scch)
            p.emit("dve", lambda e: e.memset(S_sb.ap[:], 0.0), writes=(S_sb,))
            Sbf = p.sbuf("wk_Sbf", (128, 128), BF16, scope=scch)
            ps_o = [[p.psum(f"wk_pso{j}h{h}", (128, LCH), F32, scope=scch)
                     for h in range(2)] for j in range(2)]
            ps_S = p.psum("wk_psS", (128, 128), F32, scope=scch)
            M16 = p.sbuf("wk_M16", (128, 32), BF16, scope=scch)
            ps_M16h = [p.psum(f"wk_psM16{j}", (16, 16), F32, scope=scch)
                       for j in range(2)]

            parts = getattr(p, "wkv_parts", {"inter", "cross", "supd"})
            for c in range(nch):
                t0 = c * LCH
                po = ps_o[c % 2]
                rbase = 32 * (c % 4)
                cbase = 128 * (c // 4)

                def f_sbf(e):
                    return e.tensor_copy(Sbf.ap[:], S_sb.ap[:])
                p.emit("dve", f_sbf, reads=(S_sb,), writes=(Sbf,))

                for h in range(2):
                    def f_mm(e, h=h, t0=t0, po=po):
                        return e.matmul(
                            po[h].ap[64 * h:64 * h + 64, :],
                            Sbf.ap[64 * h:64 * h + 64, 64 * h:64 * h + 64],
                            R32.ap[64 * h:64 * h + 64, t0:t0 + LCH],
                            start=True, stop=("cross" not in parts),
                            skip_group_check=True)
                    p.emit("pe", f_mm, reads=(Sbf, R32),
                           war=(po[h],), writes=(po[h],) if
                           "cross" not in parts else (),
                           inc=("cross" not in parts))

                if "cross" in parts or "crossm" in parts:
                  for h in range(2):
                    def f_mm(e, h=h, t0=t0):
                        return e.matmul(
                            ps_M16h[h].ap[:],
                            K16.ap[64 * h:64 * h + 64, t0:t0 + 16],
                            R16.ap[64 * h:64 * h + 64, t0 + 16:t0 + 32],
                            start=True, stop=True,
                            skip_group_check=True)
                    p.emit("pe", f_mm, reads=(K16, R16),
                           writes=(ps_M16h[h],), war=(ps_M16h[h],))

                if "cross" in parts or "crossm" in parts:
                  for h in range(2):
                    def f_m16ev(e, rbase=rbase, h=h):
                        return e.copy(
                            M16.ap[rbase:rbase + 16, 16 * h:16 * h + 16],
                            ps_M16h[h].ap[:])
                    p.emit("act", f_m16ev, reads=(ps_M16h[h],), writes=(M16,))

                  for h in (range(2) if "cross" in parts else ()):
                    def f_mm(e, h=h, rbase=rbase, cbase=cbase, po=po):
                        return e.matmul(
                            po[h].ap[64 * h:64 * h + 64, 16:32],
                            vT.ap[rbase:rbase + 16,
                                  cbase + 64 * h:cbase + 64 * h + 64],
                            M16.ap[rbase:rbase + 16, 16 * h:16 * h + 16],
                            start=False, stop=True,
                            tile_position=((96, 64 * h) if rbase == 96
                                           else None),
                            skip_group_check=True)
                    p.emit("pe", f_mm, reads=(vT, M16), writes=(po[h],))

                for h in range(2):
                    def f_oev(e, h=h, t0=t0, po=po):
                        return e.scalar_tensor_tensor(
                            o_hp.ap[64 * h:64 * h + 64, t0:t0 + LCH],
                            po[h].ap[64 * h:64 * h + 64, :], 1.0,
                            pc.ap[64 * h:64 * h + 64, t0:t0 + LCH],
                            ALU.mult, ALU.add)
                    p.emit("dve", f_oev, reads=(po[h], pc), writes=(o_hp,))

                if "supd" in parts:
                  def f_smm(e, rbase=rbase, cbase=cbase):
                    return e.matmul(ps_S.ap[:],
                                    K32T.ap[rbase:rbase + 32,
                                            cbase:cbase + 128],
                                    vT.ap[rbase:rbase + 32, cbase:cbase + 128],
                                    start=True, stop=True,
                                    tile_position=((96, 0) if rbase == 96
                                                   else None),
                                    skip_group_check=True)
                  p.emit("pe", f_smm, reads=(K32T, vT), writes=(ps_S,),
                         war=(ps_S,))

                  def f_supd(e, c=c):
                    return e.scalar_tensor_tensor(
                        S_sb.ap[:], S_sb.ap[:], sdec.ap[:, c:c + 1],
                        ps_S.ap[:], ALU.mult, ALU.add)
                  p.emit("dve", f_supd, reads=(S_sb, sdec, ps_S),
                         writes=(S_sb,))
                if getattr(p, "wkv_dbg", False) and hp == 0 and c < 2:
                    dump_tile(p, S_sb, f"dbgw_S{c}", (128, 128), F32)



def b0_projections(p, T, xk_fm, xv_fm, xr_fm, W, u_fm, o_hbm, ffn_hbm, dbg,
                   do_wkv=True):
    with p.scope() as sc:
        r_hp = p.sbuf("pj_r", (128, T), BF16, scope=sc)
        k_hp = p.sbuf("pj_k", (128, T), BF16, scope=sc)
        v_hp = p.sbuf("pj_v", (128, T), BF16, scope=sc)
        ew_hp = p.sbuf("pj_ew", (128, T), F32, scope=sc)
        sg_hp = p.sbuf("pj_sg", (128, T), BF16, scope=sc)
        o_hp = p.sbuf("pj_o", (128, T), BF16, scope=sc)
        wti = [0]

        def proj(dst_ps, w_hbm, col0, x_tiles, wblk):
            tl = wblk[wti[0] % 2]
            wti[0] += 1

            def f_ld(e, tl=tl, col0=col0, w_hbm=w_hbm):
                return e.dma_start(
                    out=tl.ap[:],
                    in_=w_hbm.ap[:, col0:col0 + 128].rearrange(
                        "(c p) m -> p c m", p=128))
            p.emit("pool", f_ld, reads=(w_hbm,), writes=(tl,),
                   dma_queue="dpool")
            for kc in range(NF):
                sl = mvs(T)
                for mi, (m0, mw) in enumerate(sl):
                    def f_mm(e, kc=kc, tl=tl, m0=m0, mw=mw):
                        return e.matmul(dst_ps.ap[:, m0:m0 + mw],
                                        tl.ap[:, kc, :],
                                        x_tiles[kc].ap[:, m0:m0 + mw],
                                        start=(kc == 0), stop=(kc == NF - 1))
                    last = (kc == NF - 1 and mi == len(sl) - 1)
                    p.emit("pe", f_mm, reads=(tl, x_tiles[kc]),
                           writes=(dst_ps,) if last else (),
                           war=(dst_ps,) if (kc == 0 and mi == 0) else (),
                           inc=(mi == len(sl) - 1))

        for oc in range(NF):
            with p.scope() as socp:
                ps4 = p.psum("pj_ps4", (128, T), F32, scope=socp)
                ps4b = p.psum("pj_ps4b", (128, T), F32, scope=socp)
                wblk = [p.sbuf(f"pj_wb{j}", (128, NF, 128), BF16, scope=socp)
                        for j in range(2)]
                proj(ps4, W["Wr"], oc * 128, xr_fm, wblk)
                p.emit("act", lambda e: e.copy(r_hp.ap[:], ps4.ap[:]),
                       reads=(ps4,), writes=(r_hp,))

                proj(ps4b, W["Wk"], oc * 128, xk_fm, wblk)
                p.emit("act", lambda e: e.activation(ew_hp.ap[:], ps4b.ap[:],
                                                     AF.Exp),
                       reads=(ps4b,), writes=(ew_hp,))
                p.emit("act", lambda e: e.activation(k_hp.ap[:], ew_hp.ap[:],
                                                     AF.Exp, scale=-1.0),
                       reads=(ew_hp,), writes=(k_hp,))
                p.emit("dve", lambda e: e.tensor_scalar(
                    k_hp.ap[:], k_hp.ap[:], -1.0, 1.0, ALU.mult, ALU.add),
                    reads=(k_hp,), writes=(k_hp,))

                proj(ps4, W["Wvg"], (24 + oc) * 128, xv_fm, wblk)
                p.emit("act", lambda e: e.activation(sg_hp.ap[:], ps4.ap[:],
                                                     AF.Silu),
                       reads=(ps4,), writes=(sg_hp,))
                proj(ps4b, W["Wvg"], oc * 128, xv_fm, wblk)
                p.emit("dve", lambda e: e.tensor_tensor(
                    v_hp.ap[:], ps4b.ap[:], sg_hp.ap[:], ALU.mult),
                    reads=(ps4b, sg_hp), writes=(v_hp,))

            if "rkev" in dbg:
                dump_tile(p, r_hp, f"dbg_r{oc}", (128, T), BF16)
                dump_tile(p, k_hp, f"dbg_k{oc}", (128, T), BF16)
                dump_tile(p, v_hp, f"dbg_v{oc}", (128, T), BF16)
                dump_tile(p, ew_hp, f"dbg_ew{oc}", (128, T), F32)

            if do_wkv:
                wkv_hp(p, T, oc, r_hp, k_hp, v_hp, ew_hp, u_fm, o_hp)

                def f_so(e, oc=oc):
                    return e.dma_start(
                        out=o_hbm.ap[oc * 128:(oc + 1) * 128, :],
                        in_=o_hp.ap[:])
                p.emit("sp", f_so, reads=(o_hp,), writes=(o_hbm,),
                       dma_queue="dsp")

        for oc2 in range(8, 24):
            with p.scope() as socp:
                ps4 = p.psum("pj_ps4f", (128, T), F32, scope=socp)
                ps4b = p.psum("pj_ps4fb", (128, T), F32, scope=socp)
                wblk = [p.sbuf(f"pj_wbf{j}", (128, NF, 128), BF16,
                               scope=socp) for j in range(2)]
                proj(ps4, W["Wvg"], (24 + oc2) * 128, xv_fm, wblk)
                p.emit("act", lambda e: e.activation(sg_hp.ap[:], ps4.ap[:],
                                                     AF.Silu),
                       reads=(ps4,), writes=(sg_hp,))
                proj(ps4b, W["Wvg"], oc2 * 128, xv_fm, wblk)
                p.emit("dve", lambda e: e.tensor_tensor(
                    v_hp.ap[:], ps4b.ap[:], sg_hp.ap[:], ALU.mult),
                    reads=(ps4b, sg_hp), writes=(v_hp,))

            def f_st(e, oc2=oc2):
                return e.dma_start(
                    out=ffn_hbm.ap[(oc2 - 8) * 128:(oc2 - 7) * 128, :],
                    in_=v_hp.ap[:])
            p.emit("sp", f_st, reads=(v_hp,), writes=(ffn_hbm,),
                   dma_queue="dsp")



def b0_output(p, T, xn_hbm, o_hbm, ffn_hbm, W, x_in, x1_hbm, dbg, x1t_hbm=None):
    lng = load_featvec(p, "lng", W["ln_g"].ap, C)
    lnb = load_featvec(p, "lnb", W["ln_b"].ap, C)

    s_hbm = (p.dram_out("dbg_s0", (C, T), BF16) if "s0" in dbg
             else p.dram_tmp("s_hbm", (C, T), BF16))

    scow = p.scope()
    scow_es = scow.__enter__()
    oln_fm = [p.sbuf(f"oln{i}", (128, T), BF16, scope=scow_es)
              for i in range(NF)]
    with p.scope() as sc:
        o_t = p.sbuf("ol_o", (128, T), BF16, scope=sc)
        osq = p.sbuf("ol_osq", (128, T), BF16, scope=sc)
        strow = p.sbuf("ol_st", (1, T, 6), F32, scope=sc)
        scps = p.scope()
        scps_es = scps.__enter__()
        ps_s1 = p.psum("ol_ps1", (1, T), F32, scope=scps_es)
        ps_s2 = p.psum("ol_ps2", (1, T), F32, scope=scps_es)
        o_tiles = []
        for hp in range(NF):
            ot = p.sbuf(f"ol_ot{hp}", (128, T), BF16, scope=sc)
            o_tiles.append(ot)

            def f_ld(e, hp=hp, ot=ot):
                return e.dma_start(out=ot.ap[:],
                                   in_=o_hbm.ap[hp * 128:(hp + 1) * 128, :])
            p.emit("sp", f_ld, reads=(o_hbm,), writes=(ot,), dma_queue="dsp")

            def f_sq(e, ot=ot):
                return e.activation(osq.ap[:], ot.ap[:], AF.Square)
            p.emit("act", f_sq, reads=(ot,), writes=(osq,))

            sl = mvs(T)
            for mi, (m0, mw) in enumerate(sl):
                def f_mm(e, ot=ot, m0=m0, mw=mw, hp=hp):
                    return e.matmul(ps_s1.ap[:, m0:m0 + mw], p.ones_bf.ap[:, 0:1],
                                    ot.ap[:, m0:m0 + mw],
                                    start=(hp == 0), stop=(hp == NF - 1))
                p.emit("pe", f_mm, reads=(ot, p.ones_bf),
                       writes=(ps_s1,) if hp == NF - 1 and mi == len(sl) - 1
                       else (),
                       war=(ps_s1,) if hp == 0 and mi == 0 else (),
                       inc=(mi == len(sl) - 1))

                def f_mm2(e, m0=m0, mw=mw, hp=hp):
                    return e.matmul(ps_s2.ap[:, m0:m0 + mw], p.ones_bf.ap[:, 0:1],
                                    osq.ap[:, m0:m0 + mw],
                                    start=(hp == 0), stop=(hp == NF - 1))
                p.emit("pe", f_mm2, reads=(osq, p.ones_bf),
                       writes=(ps_s2,) if hp == NF - 1 and mi == len(sl) - 1
                       else (),
                       war=(ps_s2,) if hp == 0 and mi == 0 else (),
                       inc=(mi == len(sl) - 1))

        def f_mu(e):
            return e.tensor_scalar(strow.ap[:, :, 0], ps_s1.ap[:], 1.0 / C,
                                   None, ALU.mult)
        p.emit("dve", f_mu, reads=(ps_s1,), writes=(strow,))

        def f_m2(e):
            return e.tensor_tensor(strow.ap[:, :, 1], strow.ap[:, :, 0],
                                   strow.ap[:, :, 0], ALU.mult)
        p.emit("dve", f_m2, reads=(strow,), writes=(strow,))

        def f_var(e):
            return e.scalar_tensor_tensor(strow.ap[:, :, 2], ps_s2.ap[:],
                                          1.0 / C, strow.ap[:, :, 1],
                                          ALU.mult, ALU.subtract)
        p.emit("dve", f_var, reads=(ps_s2, strow), writes=(strow,))

        def f_vep(e):
            return e.tensor_scalar(strow.ap[:, :, 2], strow.ap[:, :, 2],
                                   1e-5, None, ALU.add)
        p.emit("dve", f_vep, reads=(strow,), writes=(strow,))

        def f_rc(e):
            return e.reciprocal(strow.ap[:, :, 3], strow.ap[:, :, 2])
        p.emit("dve", f_rc, reads=(strow,), writes=(strow,))

        def f_sq2(e):
            return e.activation(strow.ap[:, :, 4], strow.ap[:, :, 3], AF.Sqrt)
        p.emit("act", f_sq2, reads=(strow,), writes=(strow,))

        def f_nb(e):
            return e.tensor_tensor(strow.ap[:, :, 5], strow.ap[:, :, 0],
                                   strow.ap[:, :, 4], ALU.mult)
        p.emit("dve", f_nb, reads=(strow,), writes=(strow,))

        def f_nb2(e):
            return e.tensor_scalar(strow.ap[:, :, 5], strow.ap[:, :, 5],
                                   -1.0, None, ALU.mult)
        p.emit("dve", f_nb2, reads=(strow,), writes=(strow,))
        scps.__exit__(None, None, None)

        ab_sb = p.sbuf("ol_ab", (128, T, 2), F32, scope=sc)
        ps_ab = p.psum("ol_psab", (128, 512), F32, scope=sc)
        for mi, (m0, mw) in enumerate(mvs(T)):
            for j, col in ((0, 4), (1, 5)):
                def f_mm(e, m0=m0, mw=mw, j=j, col=col):
                    return e.matmul(ps_ab.ap[:, 0:mw], p.onesf.ap[0:1, :],
                                    strow.ap[:, m0:m0 + mw, col],
                                    start=True, stop=True)
                p.emit("pe", f_mm, reads=(strow, p.onesf), writes=(ps_ab,),
                       war=(ps_ab,))

                def f_ev(e, m0=m0, mw=mw, j=j):
                    return e.copy(ab_sb.ap[:, m0:m0 + mw, j], ps_ab.ap[:, 0:mw])
                p.emit("act", f_ev, reads=(ps_ab,), writes=(ab_sb,))

        tmpn = p.sbuf("ol_tmpn", (128, T), F32, scope=sc)
        for hp in range(NF):
            def f_n1(e, hp=hp):
                return e.tensor_tensor(tmpn.ap[:], o_tiles[hp].ap[:],
                                       ab_sb.ap[:, :, 0], ALU.mult)
            p.emit("dve", f_n1, reads=(o_tiles[hp], ab_sb), writes=(tmpn,))

            def f_n2(e):
                return e.tensor_tensor(tmpn.ap[:], tmpn.ap[:],
                                       ab_sb.ap[:, :, 1], ALU.add)
            p.emit("dve", f_n2, reads=(tmpn, ab_sb), writes=(tmpn,))

            def f_n3(e, hp=hp):
                return e.tensor_scalar(oln_fm[hp].ap[:], tmpn.ap[:],
                                       lng.ap[:, hp:hp + 1],
                                       lnb.ap[:, hp:hp + 1],
                                       ALU.mult, ALU.add)
            p.emit("dve", f_n3, reads=(tmpn, lng, lnb), writes=(oln_fm[hp],))

    with p.scope() as sc:
        ffn_t = [p.sbuf(f"wo_ffn{j}", (128, T), BF16, scope=sc)
                 for j in range(16)]
        for j in range(16):
            def f_f(e, j=j):
                return e.dma_start(
                    out=ffn_t[j].ap[:],
                    in_=ffn_hbm.ap[j * 128:(j + 1) * 128, :])
            p.emit("sp", f_f, reads=(ffn_hbm,), writes=(ffn_t[j],),
                   dma_queue="dsp")
        wo_b = [p.sbuf(f"wo_wb{j}", (128, 24, 128), BF16, scope=sc)
                for j in range(2)]
        s_t = p.sbuf("wo_s", (128, T), BF16, scope=sc)
        psc = [p.psum(f"wo_psc{j}", (128, 512), F32, scope=sc)
               for j in range(4)]
        for m in range(NF):
            wbl = wo_b[m % 2]

            def f_w(e, m=m, wbl=wbl):
                return e.dma_start(
                    out=wbl.ap[:],
                    in_=W["Wo"].ap[:, m * 128:(m + 1) * 128].rearrange(
                        "(c p) m -> p c m", p=128))
            p.emit("pool", f_w, reads=(W["Wo"],), writes=(wbl,),
                   dma_queue="dpool")
            for ci, (m0, mw) in enumerate(mvs(T)):
                pc = psc[ci]
                for kc in range(24):
                    rhs = oln_fm[kc] if kc < 8 else ffn_t[kc - 8]

                    def f_mm(e, wbl=wbl, rhs=rhs, m0=m0, mw=mw, kc=kc, pc=pc):
                        return e.matmul(pc.ap[:, 0:mw],
                                        wbl.ap[:, kc, :],
                                        rhs.ap[:, m0:m0 + mw],
                                        start=(kc == 0), stop=(kc == 23))
                    p.emit("pe", f_mm, reads=(wbl, rhs),
                           writes=(pc,) if kc == 23 else (),
                           war=(pc,) if kc == 0 else (),
                           inc=(kc == 23))

                def f_sv(e, s_t=s_t, m0=m0, mw=mw, pc=pc):
                    return e.copy(s_t.ap[:, m0:m0 + mw], pc.ap[:, 0:mw])
                p.emit("act", f_sv, reads=(pc,), writes=(s_t,))

            def f_st(e, m=m, s_t=s_t):
                return e.dma_start(out=s_hbm.ap[m * 128:(m + 1) * 128, :],
                                   in_=s_t.ap[:])
            p.emit("sp", f_st, reads=(s_t,), writes=(s_hbm,), dma_queue="dsp")
    scow.__exit__(None, None, None)

    y_hbm = p.dram_tmp("moe_y_hbm", (C, T), BF16)
    NTT = T // 128
    with p.scope() as sc:
        xn_fm = [p.sbuf(f"moe_xn{i}", (128, T), BF16, scope=sc)
                 for i in range(NF)]
        for fi in range(NF):
            def f_lx(e, fi=fi):
                return e.dma_start(out=xn_fm[fi].ap[:],
                                   in_=xn_hbm.ap[fi * 128:(fi + 1) * 128, :])
            p.emit("sp", f_lx, reads=(xn_hbm,), writes=(xn_fm[fi],),
                   dma_queue="dsp")
        wr_sb = p.sbuf("moe_wr", (128, NF, 8), BF16, scope=sc)
        p.emit("pool", lambda e: e.dma_start(
            out=wr_sb.ap[:],
            in_=W["mWr"].ap.rearrange("(a p) m -> p a m", p=128)),
            reads=(W["mWr"],), writes=(wr_sb,), dma_queue="dpool")
        br_sb = p.sbuf("moe_br", (1, 8), BF16, scope=sc)
        p.emit("pool", lambda e: e.dma_start(
            out=br_sb.ap[:], in_=W["mbr"].ap.unsqueeze(0)),
            reads=(W["mbr"],), writes=(br_sb,), dma_queue="dpool")

        mask_hbm = p.dram_tmp("moe_mask_hbm", (8, T), F32)
        mrow = p.sbuf("moe_mrow", (1, T), F32, scope=sc)
        msk_ev = p.sbuf("moe_mskev", (8, 128), F32, scope=sc)
        ps_l = p.psum("moe_psl", (128, 8), F32, scope=sc)
        ps_mt = p.psum("moe_psmt", (8, 128), F32, scope=sc)
        lrow = p.sbuf("moe_lrow", (128, 12), F32, scope=sc)
        prob = p.sbuf("moe_prob", (128, 8), F32, scope=sc)
        midx = p.sbuf("moe_midx", (128, 8), U32, scope=sc)
        msrt = p.sbuf("moe_msrt", (128, 8), F32, scope=sc)
        l_sb = p.sbuf("moe_lsb", (128, 8), F32, scope=sc)
        msk8 = p.sbuf("moe_msk8", (128, 8), F32, scope=sc)
        for tt in range(NTT):
            for kc in range(NF):
                def f_mm(e, tt=tt, kc=kc):
                    return e.matmul(ps_l.ap[:],
                                    xn_fm[kc].ap[:, tt * 128:(tt + 1) * 128],
                                    wr_sb.ap[:, kc, :],
                                    start=(kc == 0), stop=False)
                p.emit("pe", f_mm, reads=(xn_fm[kc], wr_sb),
                       war=(ps_l,) if kc == 0 else (), inc=False)

            def f_mmb(e):
                return e.matmul(ps_l.ap[:], p.ones_bf.ap[0:1, :], br_sb.ap[:],
                                start=False, stop=True)
            p.emit("pe", f_mmb, reads=(br_sb, p.ones_bf), writes=(ps_l,))

            def f_mx(e):
                return e.tensor_reduce(lrow.ap[:, 0:1], ps_l.ap[:], AX.X,
                                       ALU.max)
            p.emit("dve", f_mx, reads=(ps_l,), writes=(lrow,))

            def f_nm(e):
                return e.tensor_scalar(lrow.ap[:, 1:2], lrow.ap[:, 0:1], -1.0,
                                       None, ALU.mult)
            p.emit("dve", f_nm, reads=(lrow,), writes=(lrow,))

            def f_ex(e):
                return e.activation(prob.ap[:], ps_l.ap[:], AF.Exp,
                                    bias=lrow.ap[:, 1:2],
                                    accum_out=lrow.ap[:, 2:3])
            p.emit("act", f_ex, reads=(ps_l, lrow), writes=(prob, lrow))

            def f_gate(e):
                return e.reciprocal(lrow.ap[:, 3:4], lrow.ap[:, 2:3])
            p.emit("dve", f_gate, reads=(lrow,), writes=(lrow,))

            def f_lsb(e):
                return e.tensor_copy(l_sb.ap[:], ps_l.ap[:])
            p.emit("dve", f_lsb, reads=(ps_l,), writes=(l_sb,))

            def f_srt(e):
                return e.max(msrt.ap[:], l_sb.ap[:])
            p.emit("dve", f_srt, reads=(l_sb,), writes=(msrt,))

            def f_idx(e):
                return e.max_index(midx.ap[:], msrt.ap[:], l_sb.ap[:])
            p.emit("dve", f_idx, reads=(msrt, l_sb), writes=(midx,))

            def f_if(e):
                return e.tensor_copy(lrow.ap[:, 4:5], midx.ap[:, 0:1])
            p.emit("dve", f_if, reads=(midx,), writes=(lrow,))

            def f_m8(e):
                return e.tensor_single_scalar(
                    msk8.ap[:], p.iota8.ap[:].broadcast_to((128, 8)),
                    lrow.ap[:, 4:5], ALU.is_equal)
            p.emit("dve", f_m8, reads=(p.iota8, lrow), writes=(msk8,))

            def f_m8g(e):
                return e.tensor_scalar(msk8.ap[:], msk8.ap[:],
                                       lrow.ap[:, 3:4], None, ALU.mult)
            p.emit("dve", f_m8g, reads=(msk8, lrow), writes=(msk8,))

            def f_tr(e):
                return e.transpose(ps_mt.ap[:], msk8.ap[:],
                                   p.idn32.ap[:])
            p.emit("pe", f_tr, reads=(msk8, p.idn32), writes=(ps_mt,))

            def f_te(e, tt=tt):
                return e.copy(msk_ev.ap[:], ps_mt.ap[:])
            p.emit("act", f_te, reads=(ps_mt,), writes=(msk_ev,))

            def f_td(e, tt=tt):
                return e.dma_start(
                    out=mask_hbm.ap[:, tt * 128:(tt + 1) * 128],
                    in_=msk_ev.ap[:])
            p.emit("sp", f_td, reads=(msk_ev,), writes=(mask_hbm,),
                   dma_queue="dsp")


        h1 = [p.sbuf(f"moe_h1_{m}", (128, T), BF16, scope=sc)
              for m in range(32)]
        wb1 = [p.sbuf(f"moe_wb1{j}", (128, 8, 128), BF16, scope=sc)
               for j in range(2)]
        wb2 = [p.sbuf(f"moe_wb2{j}", (128, 32, 128), BF16, scope=sc)
               for j in range(2)]
        maskB = p.sbuf("moe_maskB", (128, T), BF16, scope=sc)
        yacc = p.sbuf("moe_yacc", (128, T), BF16, scope=sc)
        psc = [p.psum(f"moe_psc{j}", (128, 512), F32, scope=sc)
               for j in range(4)]
        ps_mb = p.psum("moe_psmb", (128, 512), F32, scope=sc)
        tmp_y = p.sbuf("moe_tmpy", (128, T), BF16, scope=sc)
        p.emit("dve", lambda e: e.memset(tmp_y.ap[:], 0.0), writes=(tmp_y,))
        for fi in range(NF):
            def f_z(e, fi=fi):
                return e.dma_start(out=y_hbm.ap[fi * 128:(fi + 1) * 128, :],
                                   in_=tmp_y.ap[:])
            p.emit("sp", f_z, reads=(tmp_y,), writes=(y_hbm,),
                   dma_queue="dsp")
        wi = [0]
        b1_all = p.sbuf("moe_b1a", (128, 8, 32), F32, scope=sc)
        b2_all = p.sbuf("moe_b2a", (128, 8, 8), F32, scope=sc)
        for ee in range(8):
            load_featvec(p, f"b1_{ee}", W["mb1"].ap[ee, :], 4096,
                         out=b1_all, outsl=(slice(None), ee, slice(None)))
            load_featvec(p, f"b2_{ee}", W["mb2"].ap[ee, :], C,
                         out=b2_all, outsl=(slice(None), ee, slice(None)))
        for ee in range(8):
            b1e = Buf("b1e", b1_all.ap[:, ee, :])
            b1e.writer = b1_all.writer
            b2e = Buf("b2e", b2_all.ap[:, ee, :])
            b2e.writer = b2_all.writer
            def f_mr(e, ee=ee):
                return e.dma_start(out=mrow.ap[:],
                                   in_=mask_hbm.ap[ee:ee + 1, :])
            p.emit("sp", f_mr, reads=(mask_hbm,), writes=(mrow,),
                   dma_queue="dsp")
            for mi, (m0, mw) in enumerate(mvs(T)):
                def f_mm(e, ee=ee, m0=m0, mw=mw):
                    return e.matmul(ps_mb.ap[:, 0:mw], p.onesf.ap[0:1, :],
                                    mrow.ap[:, m0:m0 + mw],
                                    start=True, stop=True)
                p.emit("pe", f_mm, reads=(mrow, p.onesf), writes=(ps_mb,),
                       war=(ps_mb,))

                def f_ev(e, m0=m0, mw=mw):
                    return e.copy(maskB.ap[:, m0:m0 + mw], ps_mb.ap[:, 0:mw])
                p.emit("act", f_ev, reads=(ps_mb,), writes=(maskB,))

            for m in range(32):
                wbl = wb1[m % 2]

                def f_w(e, ee=ee, m=m, wbl=wbl):
                    return e.dma_start(
                        out=wbl.ap[:],
                        in_=W["mW1"].ap[ee, :,
                                        m * 128:(m + 1) * 128].rearrange(
                            "(c p) m -> p c m", p=128))
                p.emit("pool", f_w, reads=(W["mW1"],), writes=(wbl,),
                       dma_queue="dpool")
                for ci, (m0, mw) in enumerate(mvs(T)):
                    pc = psc[ci]
                    for kc in range(NF):
                        def f_mm(e, wbl=wbl, kc=kc, m0=m0, mw=mw, pc=pc):
                            return e.matmul(pc.ap[:, 0:mw],
                                            wbl.ap[:, kc, :],
                                            xn_fm[kc].ap[:, m0:m0 + mw],
                                            start=(kc == 0),
                                            stop=(kc == NF - 1))
                        p.emit("pe", f_mm, reads=(wbl, xn_fm[kc]),
                               writes=(pc,) if kc == NF - 1 else (),
                               war=(pc,) if kc == 0 else (),
                               inc=(kc == NF - 1))

                    def f_relu(e, m=m, b1e=b1e, m0=m0, mw=mw, pc=pc):
                        return e.activation(h1[m].ap[:, m0:m0 + mw],
                                            pc.ap[:, 0:mw], AF.Relu,
                                            bias=b1e.ap[:, m:m + 1])
                    p.emit("act", f_relu, reads=(pc, b1e), writes=(h1[m],))

            for m2 in range(NF):
                wbl = wb2[m2 % 2]

                def f_w(e, ee=ee, m2=m2, wbl=wbl):
                    return e.dma_start(
                        out=wbl.ap[:],
                        in_=W["mW2"].ap[ee, :,
                                        m2 * 128:(m2 + 1) * 128].rearrange(
                            "(c p) m -> p c m", p=128))
                p.emit("pool", f_w, reads=(W["mW2"],), writes=(wbl,),
                       dma_queue="dpool")
                for ci, (m0, mw) in enumerate(mvs(T)):
                    pc = psc[ci]
                    for kc2 in range(32):
                        def f_mm(e, wbl=wbl, kc2=kc2, m0=m0, mw=mw, pc=pc):
                            return e.matmul(pc.ap[:, 0:mw],
                                            wbl.ap[:, kc2, :],
                                            h1[kc2].ap[:, m0:m0 + mw],
                                            start=(kc2 == 0),
                                            stop=(kc2 == 31))
                        p.emit("pe", f_mm, reads=(wbl, h1[kc2]),
                               writes=(pc,) if kc2 == 31 else (),
                               war=(pc,) if kc2 == 0 else (),
                               inc=(kc2 == 31))

                    def f_b2(e, m2=m2, b2e=b2e, m0=m0, mw=mw, pc=pc):
                        return e.tensor_scalar(tmp_y.ap[:, m0:m0 + mw],
                                               pc.ap[:, 0:mw],
                                               b2e.ap[:, m2:m2 + 1], None,
                                               ALU.add)
                    p.emit("dve", f_b2, reads=(pc, b2e), writes=(tmp_y,))

                def f_msk(e):
                    return e.tensor_tensor(tmp_y.ap[:], tmp_y.ap[:],
                                           maskB.ap[:], ALU.mult)
                p.emit("dve", f_msk, reads=(tmp_y, maskB), writes=(tmp_y,))

                def f_yl(e, m2=m2):
                    return e.dma_start(
                        out=yacc.ap[:],
                        in_=y_hbm.ap[m2 * 128:(m2 + 1) * 128, :])
                p.emit("sp", f_yl, reads=(y_hbm,), writes=(yacc,),
                       dma_queue="dsp")

                def f_acc(e, m2=m2):
                    return e.tensor_tensor(yacc.ap[:], yacc.ap[:],
                                           tmp_y.ap[:], ALU.add)
                p.emit("dve", f_acc, reads=(yacc, tmp_y), writes=(yacc,))

                def f_ys(e, m2=m2):
                    return e.dma_start(
                        out=y_hbm.ap[m2 * 128:(m2 + 1) * 128, :],
                        in_=yacc.ap[:])
                p.emit("sp", f_ys, reads=(yacc,), writes=(y_hbm,),
                       dma_queue="dsp")

    if "y" in dbg:
        ydmp = p.dram_out("dbg_y", (C, T), BF16)
        yb = Buf("ydump", y_hbm.ap)
        yb.writer = y_hbm.writer
        p.emit("sp", lambda e: e.dma_start(out=ydmp.ap[:], in_=y_hbm.ap[:]),
               reads=(y_hbm,), writes=(ydmp,), dma_queue="dsp")

    with p.scope() as sc:
        stage = p.sbuf("cb_stage", (128, C), F32, scope=sc)
        ps_tr = [p.psum(f"cb_pstr{j}", (128, 128), F32, scope=sc)
                 for j in range(2)]
        s_t = p.sbuf("cb_st", (128, T), BF16, scope=sc)
        g_t = p.sbuf("cb_gt", (128, T), BF16, scope=sc)
        x_cols = [p.sbuf(f"cb_xc{i}", (128, T), F32, scope=sc)
                  for i in range(NF)]
        for tt in range(T // 128):
            def f_ld(e, tt=tt):
                return e.dma_start(out=stage.ap[:],
                                   in_=x_in.ap[tt * 128:(tt + 1) * 128, :])
            p.emit("sp", f_ld, reads=(x_in,), writes=(stage,), dma_queue="dsp")
            for fi in range(NF):
                pst = ps_tr[fi % 2]

                def f_tr(e, fi=fi, pst=pst):
                    return e.transpose(pst.ap[:],
                                       stage.ap[:, fi * 128:(fi + 1) * 128],
                                       p.idn32.ap[:])
                p.emit("pe", f_tr, reads=(stage, p.idn32), writes=(pst,))

                def f_ev(e, fi=fi, tt=tt, pst=pst):
                    return e.copy(x_cols[fi].ap[:, tt * 128:(tt + 1) * 128],
                                  pst.ap[:])
                p.emit("act", f_ev, reads=(pst,), writes=(x_cols[fi],))

        y_t = p.sbuf("cb_yt", (128, T), BF16, scope=sc)
        for fi in range(NF):
            def f_yl(e, fi=fi):
                return e.dma_start(out=y_t.ap[:],
                                   in_=y_hbm.ap[fi * 128:(fi + 1) * 128, :])
            p.emit("sp", f_yl, reads=(y_hbm,), writes=(y_t,), dma_queue="dsp")

            def f_g(e, fi=fi):
                return e.activation(g_t.ap[:], y_t.ap[:], AF.Silu)
            p.emit("act", f_g, reads=(y_t,), writes=(g_t,))

            def f_sl(e, fi=fi):
                return e.dma_start(out=s_t.ap[:],
                                   in_=s_hbm.ap[fi * 128:(fi + 1) * 128, :])
            p.emit("sp", f_sl, reads=(s_hbm,), writes=(s_t,), dma_queue="dsp")

            def f_sg(e):
                return e.tensor_tensor(g_t.ap[:], g_t.ap[:], s_t.ap[:],
                                       ALU.mult)
            p.emit("dve", f_sg, reads=(g_t, s_t), writes=(g_t,))

            def f_x1(e, fi=fi):
                return e.tensor_tensor(x_cols[fi].ap[:], x_cols[fi].ap[:],
                                       g_t.ap[:], ALU.add)
            p.emit("dve", f_x1, reads=(x_cols[fi], g_t),
                   writes=(x_cols[fi],))

            def f_st(e, fi=fi):
                return e.dma_start(out=x1_hbm.ap[fi * 128:(fi + 1) * 128, :],
                                   in_=x_cols[fi].ap[:])
            p.emit("sp", f_st, reads=(x_cols[fi],), writes=(x1_hbm,),
                   dma_queue="dsp")

        if x1t_hbm is not None:
            for tt in range(T // 128):
                for fi in range(NF):
                    pst = ps_tr[fi % 2]

                    def f_tr(e, fi=fi, tt=tt, pst=pst):
                        return e.transpose(
                            pst.ap[:],
                            x_cols[fi].ap[:, tt * 128:(tt + 1) * 128],
                            p.idn32.ap[:])
                    p.emit("pe", f_tr, reads=(x_cols[fi], p.idn32),
                           writes=(pst,))

                    def f_ev(e, fi=fi, tt=tt, pst=pst):
                        return e.copy(stage.ap[:, fi * 128:(fi + 1) * 128],
                                      pst.ap[:])
                    p.emit("act", f_ev, reads=(pst,), writes=(stage,))

                def f_so(e, tt=tt):
                    return e.dma_start(
                        out=x1t_hbm.ap[tt * 128:(tt + 1) * 128, :],
                        in_=stage.ap[:])
                p.emit("sp", f_so, reads=(stage,), writes=(x1t_hbm,),
                       dma_queue="dsp")



def declare_b1_weights(p):
    W = {}
    W["Wq"] = p.dram_in("aWq", (C, C), F32)
    W["bq"] = p.dram_in("abq", (C,), F32)
    W["Wk"] = p.dram_in("aWk", (C, C), F32)
    W["bk"] = p.dram_in("abk", (C,), F32)
    W["Wv"] = p.dram_in("aWv", (C, C), F32)
    W["bv"] = p.dram_in("abv", (C,), F32)
    W["Wo"] = p.dram_in("aWo", (C, C), F32)
    W["bo"] = p.dram_in("abo", (C,), F32)
    W["hm"] = p.dram_in("ahm", (16,), F32)
    W["gamma"] = p.dram_in("pgamma", (C,), F32)
    W["beta"] = p.dram_in("pbeta", (C,), F32)
    W["w_q"] = p.dram_in("pwq", (2 * C, C), F32)
    W["keys"] = p.dram_in("pkeys", (8, 256, 2, 128), F32)
    W["key_in"] = p.dram_in("pkin", (65536, C), F32)
    W["key_out"] = p.dram_in("pkout", (65536, C), F32)
    return W


def b1_entry(p, T, x1f_hbm, xn2_fm, xn2_tm, dbg, ncores=1):
    with p.scope() as sc:
        xt = [p.sbuf(f"n2_x{j}", (128, T), F32, scope=sc) for j in range(2)]
        x2sq = p.sbuf("n2_sq", (128, T), BF16, scope=sc)
        xbf = [p.sbuf(f"n2_xb{j}", (128, T), BF16, scope=sc) for j in range(2)]
        strow = p.sbuf("n2_st", (1, T, 6), F32, scope=sc)
        scps = p.scope()
        scps_es = scps.__enter__()
        ps_s1 = p.psum("n2_ps1", (1, T), F32, scope=scps_es)
        ps_s2 = p.psum("n2_ps2", (1, T), F32, scope=scps_es)
        for fi in range(NF):
            x_t = xt[fi % 2]

            def f_ld(e, fi=fi, x_t=x_t):
                return e.dma_start(out=x_t.ap[:],
                                   in_=x1f_hbm.ap[fi * 128:(fi + 1) * 128, :])
            p.emit("sp", f_ld, reads=(x1f_hbm,), writes=(x_t,),
                   dma_queue="dsp")

            def f_sq(e, x_t=x_t):
                return e.activation(x2sq.ap[:], x_t.ap[:], AF.Square)
            p.emit("act", f_sq, reads=(x_t,), writes=(x2sq,))

            sl = mvs(T)
            for mi, (m0, mw) in enumerate(sl):
                def f_mm(e, x_t=x_t, m0=m0, mw=mw, fi=fi):
                    return e.matmul(ps_s1.ap[:, m0:m0 + mw],
                                    p.onesf.ap[:, 0:1],
                                    x_t.ap[:, m0:m0 + mw],
                                    start=(fi == 0), stop=(fi == NF - 1))
                p.emit("pe", f_mm, reads=(x_t, p.onesf),
                       writes=(ps_s1,) if fi == NF - 1 and mi == len(sl) - 1
                       else (),
                       war=(ps_s1,) if fi == 0 and mi == 0 else (),
                       inc=(mi == len(sl) - 1))

                def f_mm2(e, m0=m0, mw=mw, fi=fi):
                    return e.matmul(ps_s2.ap[:, m0:m0 + mw],
                                    p.ones_bf.ap[:, 0:1],
                                    x2sq.ap[:, m0:m0 + mw],
                                    start=(fi == 0), stop=(fi == NF - 1))
                p.emit("pe", f_mm2, reads=(x2sq, p.ones_bf),
                       writes=(ps_s2,) if fi == NF - 1 and mi == len(sl) - 1
                       else (),
                       war=(ps_s2,) if fi == 0 and mi == 0 else (),
                       inc=(mi == len(sl) - 1))

        p.emit("dve", lambda e: e.tensor_scalar(
            strow.ap[:, :, 0], ps_s1.ap[:], 1.0 / C, None, ALU.mult),
            reads=(ps_s1,), writes=(strow,))
        p.emit("dve", lambda e: e.tensor_tensor(
            strow.ap[:, :, 1], strow.ap[:, :, 0], strow.ap[:, :, 0],
            ALU.mult), reads=(strow,), writes=(strow,))
        p.emit("dve", lambda e: e.scalar_tensor_tensor(
            strow.ap[:, :, 2], ps_s2.ap[:], 1.0 / C, strow.ap[:, :, 1],
            ALU.mult, ALU.subtract), reads=(ps_s2, strow), writes=(strow,))
        p.emit("dve", lambda e: e.tensor_scalar(
            strow.ap[:, :, 2], strow.ap[:, :, 2], 1e-5, None, ALU.add),
            reads=(strow,), writes=(strow,))
        p.emit("dve", lambda e: e.reciprocal(
            strow.ap[:, :, 3], strow.ap[:, :, 2]),
            reads=(strow,), writes=(strow,))
        p.emit("act", lambda e: e.activation(
            strow.ap[:, :, 4], strow.ap[:, :, 3], AF.Sqrt),
            reads=(strow,), writes=(strow,))
        p.emit("dve", lambda e: e.tensor_tensor(
            strow.ap[:, :, 5], strow.ap[:, :, 0], strow.ap[:, :, 4],
            ALU.mult), reads=(strow,), writes=(strow,))
        p.emit("dve", lambda e: e.tensor_scalar(
            strow.ap[:, :, 5], strow.ap[:, :, 5], -1.0, None, ALU.mult),
            reads=(strow,), writes=(strow,))
        scps.__exit__(None, None, None)

        ab_sb = p.sbuf("n2_ab", (128, T, 2), F32, scope=sc)
        ps_ab = p.psum("n2_psab", (128, 512), F32, scope=sc)
        for mi, (m0, mw) in enumerate(mvs(T)):
            for j, col in ((0, 4), (1, 5)):
                def f_mm(e, m0=m0, mw=mw, j=j, col=col):
                    return e.matmul(ps_ab.ap[:, 0:mw], p.onesf.ap[0:1, :],
                                    strow.ap[:, m0:m0 + mw, col],
                                    start=True, stop=True)
                p.emit("pe", f_mm, reads=(strow, p.onesf), writes=(ps_ab,),
                       war=(ps_ab,))

                def f_ev(e, m0=m0, mw=mw, j=j):
                    return e.copy(ab_sb.ap[:, m0:m0 + mw, j],
                                  ps_ab.ap[:, 0:mw])
                p.emit("act", f_ev, reads=(ps_ab,), writes=(ab_sb,))

        tmpn = p.sbuf("n2_tmp", (128, T), F32, scope=sc)
        ps_tr = [p.psum(f"n2_pstr{j}", (128, 128), BF16, scope=sc)
                 for j in range(2)]
        for fi in range(NF):
            x_t = xbf[fi % 2]

            def f_rl(e, fi=fi, x_t=x_t):
                return e.dma_start(out=x_t.ap[:],
                                   in_=x1f_hbm.ap[fi * 128:(fi + 1) * 128, :])
            p.emit("pool", f_rl, reads=(x1f_hbm,), writes=(x_t,),
                   dma_queue="dpool")

            def f_n1(e, fi=fi, x_t=x_t):
                return e.tensor_tensor(tmpn.ap[:], x_t.ap[:],
                                       ab_sb.ap[:, :, 0], ALU.mult)
            p.emit("dve", f_n1, reads=(x_t, ab_sb), writes=(tmpn,))

            def f_n2(e, fi=fi):
                return e.tensor_tensor(xn2_fm[fi].ap[:], tmpn.ap[:],
                                       ab_sb.ap[:, :, 1], ALU.add)
            p.emit("dve", f_n2, reads=(tmpn, ab_sb), writes=(xn2_fm[fi],))

            for tt in range(T // 128):
                pst = ps_tr[tt % 2]

                def f_tr(e, fi=fi, tt=tt, pst=pst):
                    return e.transpose(
                        pst.ap[:], xn2_fm[fi].ap[:, tt * 128:(tt + 1) * 128],
                        p.idn.ap[:])
                p.emit("pe", f_tr, reads=(xn2_fm[fi], p.idn), writes=(pst,))

                def f_ev(e, fi=fi, tt=tt, pst=pst):
                    return e.copy(
                        xn2_tm[tt].ap[:, fi * 128:(fi + 1) * 128], pst.ap[:])
                p.emit("act", f_ev, reads=(pst,), writes=(xn2_tm[tt],))


def trans_weight_lhsT(p, w_hbm, m, kc, wt_raw, wt_out, ps_tr, transT=True):
    def f_ld(e):
        return e.dma_start(out=wt_raw.ap[:],
                           in_=w_hbm.ap[m * 128:(m + 1) * 128,
                                        kc * 128:(kc + 1) * 128])
    p.emit("pool", f_ld, reads=(w_hbm,), writes=(wt_raw,), dma_queue="dpool")

    def f_tr(e):
        return e.transpose(ps_tr.ap[:], wt_raw.ap[:], p.idn.ap[:])
    p.emit("pe", f_tr, reads=(wt_raw, p.idn), writes=(ps_tr,))

    def f_ev(e):
        return e.copy(wt_out.ap[:], ps_tr.ap[:])
    p.emit("act", f_ev, reads=(ps_tr,), writes=(wt_out,))


def trans_weight_blk(p, w_hbm, m, nkc, wraw_blk, wlhs_blk, ps_tr):
    def f_ld(e):
        return e.dma_start(out=wraw_blk.ap[:, 0:nkc, :],
                           in_=w_hbm.ap[m * 128:(m + 1) * 128, :].rearrange(
                               "p (c q) -> p c q", q=128))
    p.emit("pool", f_ld, reads=(w_hbm,), writes=(wraw_blk,),
           dma_queue="dpool")
    for kc in range(nkc):
        def f_tr(e, kc=kc):
            return e.transpose(ps_tr.ap[:], wraw_blk.ap[:, kc, :],
                               p.idn.ap[:])
        p.emit("pe", f_tr, reads=(wraw_blk, p.idn), writes=(ps_tr,))

        def f_ev(e, kc=kc):
            return e.copy(wlhs_blk.ap[:, kc, :], ps_tr.ap[:])
        p.emit("act", f_ev, reads=(ps_tr,), writes=(wlhs_blk,))


def b1_attn_setup(p, T, W, sc, biases):
    NTT = T // 128
    ctx = {}
    ctx.update(biases)
    ctx["attn_hbm"] = p.dram_tmp("attn_hbm", (C, T), BF16)
    hm_sb = p.sbuf("at_hm", (1, 16), F32, scope=sc)
    p.emit("sp", lambda e: e.dma_start(out=hm_sb.ap[:],
                                       in_=W["hm"].ap.unsqueeze(0)),
           reads=(W["hm"],), writes=(hm_sb,), dma_queue="dsp")
    ctx["hm"] = hm_sb
    ctx["q_c"] = p.sbuf("at_q", (128, T), BF16, scope=sc)
    ctx["k_c"] = p.sbuf("at_k", (128, T), BF16, scope=sc)
    ctx["v_c"] = p.sbuf("at_v", (128, T), BF16, scope=sc)
    ctx["vt_c"] = p.sbuf("at_vt", (128, T), BF16, scope=sc)
    ctx["wraw"] = [p.sbuf(f"at_wr{j}", (128, NF, 128), BF16, scope=sc)
                   for j in range(2)]
    ctx["wlhs"] = [p.sbuf(f"at_wl{j}", (128, NF, 128), BF16, scope=sc)
                   for j in range(2)]
    ctx["wbi"] = [0]
    ctx["att_o"] = p.sbuf("at_ao", (128, T), BF16, scope=sc)
    ctx["rowtmp"] = p.sbuf("at_row", (1, 1, 2), F32, scope=sc)
    ctx["coltmp"] = p.sbuf("at_col", (128, 4), F32, scope=sc)
    ctx["attT"] = [p.sbuf(f"at_attT{j}", (128, 1024), BF16, scope=sc)
                   for j in range(2)]
    ctx["vth"] = [p.sbuf(f"at_vth{j}", (128, NTT, 65), BF16, scope=sc)
                  for j in range(2)]
    ctx["drow"] = p.sbuf("at_drow", (1, min(1024, T)), F32, scope=sc)
    for j in range(2):
        def f_one(e, j=j):
            return e.memset(ctx["vth"][j].ap[:, :, 64:65], 1.0)
        p.emit("dve", f_one, writes=(ctx["vth"][j],))
    return ctx


def b1_attn_chunk(p, T, ctx, xn2_fm, W, mc, dbg, pump):
    NTT = T // 128
    q_c, k_c, v_c = ctx["q_c"], ctx["k_c"], ctx["v_c"]
    att_o, rowtmp, coltmp = ctx["att_o"], ctx["rowtmp"], ctx["coltmp"]
    attTs, hm_sb = ctx["attT"], ctx["hm"]
    vth = ctx["vth"]
    with p.scope() as scp:
        psc = [p.psum(f"at_psq{j}", (128, 512), F32, scope=scp)
               for j in range(4)]
        ps_t = p.psum("at_pst", (128, 128), BF16, scope=scp)
        for dst, wname, bias in ((q_c, "Wq", ctx["bq"]),
                                 (k_c, "Wk", ctx["bk"]),
                                 (v_c, "Wv", ctx["bv"])):
            wr = ctx["wraw"][ctx["wbi"][0] % 2]
            wl = ctx["wlhs"][ctx["wbi"][0] % 2]
            ctx["wbi"][0] += 1
            trans_weight_blk(p, W[wname], mc, NF, wr, wl, ps_t)
            for ci, (m0, mw) in enumerate(mvs(T)):
                pc = psc[ci]
                for kc in range(NF):
                    def f_mm(e, kc=kc, m0=m0, mw=mw, wl=wl, pc=pc):
                        return e.matmul(
                            pc.ap[:, 0:mw], wl.ap[:, kc, :],
                            xn2_fm[kc].ap[:, m0:m0 + mw],
                            start=(kc == 0), stop=(kc == NF - 1))
                    p.emit("pe", f_mm, reads=(wl, xn2_fm[kc]),
                           writes=(pc,) if kc == NF - 1 else (),
                           war=(pc,) if kc == 0 else (),
                           inc=(kc == NF - 1))

                def f_b(e, dst=dst, bias=bias, mc=mc, m0=m0, mw=mw, pc=pc):
                    return e.activation(dst.ap[:, m0:m0 + mw],
                                        pc.ap[:, 0:mw], AF.Identity,
                                        bias=bias.ap[:, mc:mc + 1])
                p.emit("act", f_b, reads=(pc, bias), writes=(dst,))

        for tt in range(NTT):
            def f_tr(e, tt=tt, ps_t=ps_t):
                return e.transpose(
                    ps_t.ap[:], v_c.ap[:, tt * 128:(tt + 1) * 128],
                    p.idn.ap[:])
            p.emit("pe", f_tr, reads=(v_c, p.idn), writes=(ps_t,))

            for hh in range(2):
                def f_ev(e, tt=tt, hh=hh, ps_t=ps_t):
                    return e.copy(vth[hh].ap[:, tt, 0:64],
                                  ps_t.ap[:, 64 * hh:64 * hh + 64])
                p.emit("act", f_ev, reads=(ps_t,), writes=(vth[hh],))

    if "qkv" in dbg:
        dump_tile(p, q_c, f"dbg_q{mc}", (128, T), BF16)
        dump_tile(p, k_c, f"dbg_k1{mc}", (128, T), BF16)
        dump_tile(p, v_c, f"dbg_v1{mc}", (128, T), BF16)
    pump(3)
    for hh in range(2):
        h = mc * 2 + hh
        hr = slice(64 * hh, 64 * hh + 64)
        with p.scope() as sch:
            ps_sT = [p.psum(f"at_psT{j}", (128, 1024), F32, scope=sch)
                     for j in range(2)]
            ps_oT = p.psum("at_psoT", (65, 1024), F32, scope=sch)
            ps_b = p.psum("at_psb", (128, 512), F32, scope=sch)
            for qh in range(T // 1024 if T >= 1024 else 1):
                QW = min(1024, T)
                q0 = qh * QW
                for kt in range(NTT):
                    attT = attTs[kt % 2]
                    psT = ps_sT[kt % 2]
                    sl2 = mvs(QW)
                    for mi, (m0, mw) in enumerate(sl2):
                        def f_mm(e, kt=kt, m0=m0, mw=mw, hr=hr, q0=q0,
                                 psT=psT):
                            return e.matmul(
                                psT.ap[:, m0:m0 + mw],
                                k_c.ap[hr, kt * 128:(kt + 1) * 128],
                                q_c.ap[hr, q0 + m0:q0 + m0 + mw],
                                start=True, stop=True)
                        p.emit("pe", f_mm, reads=(k_c, q_c),
                               writes=(psT,) if mi == len(sl2) - 1 else (),
                               war=(psT,) if mi == 0 else (),
                               inc=(mi == len(sl2) - 1))

                    if kt == 0 and qh == 0:
                        def f_cm(e, QW=QW, psT=psT):
                            return e.tensor_reduce(
                                coltmp.ap[:, 0:1], psT.ap[:, 0:QW],
                                AX.X, ALU.max)
                        p.emit("dve", f_cm, reads=(psT,), writes=(coltmp,))

                        def f_ct(e):
                            return e.transpose(
                                ps_b.ap[0:1, 0:128],
                                coltmp.ap[:, 0:1], p.idn32.ap[:])
                        p.emit("pe", f_ct, reads=(coltmp, p.idn32),
                               writes=(ps_b,))

                        def f_cr(e):
                            return e.tensor_reduce(
                                rowtmp.ap[:, 0:1, 0],
                                ps_b.ap[0:1, 0:128], AX.X, ALU.max)
                        p.emit("dve", f_cr, reads=(ps_b,), writes=(rowtmp,))

                        def f_cb(e):
                            return e.matmul(
                                ps_b.ap[:, 0:1], p.onesf.ap[0:1, :],
                                rowtmp.ap[:, 0:1, 0],
                                start=True, stop=True)
                        p.emit("pe", f_cb, reads=(rowtmp, p.onesf),
                               writes=(ps_b,))

                        def f_cn(e):
                            return e.tensor_scalar(
                                coltmp.ap[:, 1:2], ps_b.ap[:, 0:1],
                                -0.125, None, ALU.mult)
                        p.emit("dve", f_cn, reads=(ps_b,), writes=(coltmp,))

                    def f_ex(e, QW=QW, attT=attT, psT=psT):
                        return e.activation(
                            attT.ap[:, 0:QW], psT.ap[:, 0:QW],
                            AF.Exp, bias=coltmp.ap[:, 1:2], scale=0.125)
                    p.emit("act", f_ex, reads=(psT, coltmp), writes=(attT,))

                    for mi, (m0, mw) in enumerate(mvs(QW)):
                        def f_mo(e, m0=m0, mw=mw, kt=kt, hh=hh, attT=attT):
                            return e.matmul(
                                ps_oT.ap[:, m0:m0 + mw],
                                vth[hh].ap[:, kt, :],
                                attT.ap[:, m0:m0 + mw],
                                start=(kt == 0), stop=(kt == NTT - 1))
                        p.emit("pe", f_mo, reads=(attT, vth[hh]),
                               writes=(ps_oT,) if kt == NTT - 1 and
                               mi == len(mvs(QW)) - 1 else (),
                               war=(ps_oT,) if kt == 0 and mi == 0 else (),
                               inc=(mi == len(mvs(QW)) - 1))

                def f_oe(e, QW=QW, q0=q0, hr=hr):
                    return e.copy(att_o.ap[hr, q0:q0 + QW],
                                  ps_oT.ap[0:64, 0:QW])
                p.emit("act", f_oe, reads=(ps_oT,), writes=(att_o,))

                def f_dc(e, QW=QW):
                    return e.copy(ctx["drow"].ap[:, 0:QW],
                                  ps_oT.ap[64:65, 0:QW])
                p.emit("act", f_dc, reads=(ps_oT,), writes=(ctx["drow"],))

                def f_dv(e, QW=QW, q0=q0, h=h):
                    return e.dma_start(
                        out=ctx["d_hbm"].ap[h:h + 1, q0:q0 + QW],
                        in_=ctx["drow"].ap[:, 0:QW])
                p.emit("sp", f_dv, reads=(ctx["drow"],),
                       writes=(ctx["d_hbm"],), dma_queue="dsp")
                pump(1)

    if "qkv" in dbg:
        dump_tile(p, att_o, f"dbg_ao{mc}", (128, T), BF16)

    def f_sa(e, mc=mc):
        return e.dma_start(
            out=ctx["attn_hbm"].ap[mc * 128:(mc + 1) * 128, :],
            in_=att_o.ap[:])
    p.emit("sp", f_sa, reads=(att_o,), writes=(ctx["attn_hbm"],),
           dma_queue="dsp")
    pump(2)


def b1_attn_wo(p, T, ctx, W, s2t_hbm):
    attn_hbm = ctx["attn_hbm"]
    bo = ctx["bo"]
    d_hbm = ctx["d_hbm"]
    hm_col = ctx["hm_col"]
    with p.scope() as sc:
        dinv_all = p.sbuf("wo2_dinv", (16, T), F32, scope=sc)
        p.emit("sp", lambda e: e.dma_start(out=dinv_all.ap[:],
                                           in_=d_hbm.ap[:]),
               reads=(d_hbm,), writes=(dinv_all,), dma_queue="dsp")
        a_t = [p.sbuf(f"wo2_a{j}", (128, T), BF16, scope=sc)
               for j in range(NF)]
        for kc in range(NF):
            def f_al(e, kc=kc):
                return e.dma_start(
                    out=a_t[kc].ap[:],
                    in_=attn_hbm.ap[kc * 128:(kc + 1) * 128, :])
            p.emit("sp", f_al, reads=(attn_hbm,), writes=(a_t[kc],),
                   dma_queue="dsp")
        def f_rc(e):
            return e.reciprocal(dinv_all.ap[:], dinv_all.ap[:])
        p.emit("dve", f_rc, reads=(dinv_all,), writes=(dinv_all,))

        def f_hm(e):
            return e.tensor_scalar(dinv_all.ap[:], dinv_all.ap[:],
                                   hm_col.ap[:, 0:1], None, ALU.mult)
        p.emit("dve", f_hm, reads=(dinv_all, hm_col), writes=(dinv_all,))
        sel16 = p.sbuf("wo2_sel16", (16, NF, 128), F32, scope=sc)
        iotaP = p.sbuf("wo2_iotaP", (16, 1), F32, scope=sc)
        p.emit("pool", lambda e: e.iota(iotaP.ap[:], pattern=[[1, 1]],
                                        base=0, channel_multiplier=1,
                                        allow_small_or_imprecise_dtypes=True),
               writes=(iotaP,))
        for kc in range(NF):
            for half in range(2):
                def f_sel(e, kc=kc, half=half):
                    return e.tensor_scalar(
                        sel16.ap[:, kc, 64 * half:64 * half + 64],
                        iotaP.ap[:].broadcast_to((16, 64)),
                        float(2 * kc + half), None, ALU.is_equal)
                p.emit("dve", f_sel, reads=(iotaP,), writes=(sel16,))
        ps_n = [p.psum(f"wo2_psn{j}", (128, 512), F32, scope=sc)
                for j in range(2)]
        for kc in range(NF):
            for ci, (m0, mw) in enumerate(mvs(T)):
                pn = ps_n[ci % 2]

                def f_nb(e, kc=kc, m0=m0, mw=mw, pn=pn):
                    return e.matmul(pn.ap[:, 0:mw], sel16.ap[:, kc, :],
                                    dinv_all.ap[:, m0:m0 + mw],
                                    start=True, stop=True)
                p.emit("pe", f_nb, reads=(sel16, dinv_all), writes=(pn,),
                       war=(pn,))

                def f_nm(e, kc=kc, m0=m0, mw=mw, pn=pn):
                    return e.tensor_tensor(a_t[kc].ap[:, m0:m0 + mw],
                                           a_t[kc].ap[:, m0:m0 + mw],
                                           pn.ap[:, 0:mw], ALU.mult)
                p.emit("dve", f_nm, reads=(a_t[kc], pn), writes=(a_t[kc],))
        wraw2 = [p.sbuf("wo2_wr0", (128, NF, 128), BF16, scope=sc)]
        wlhs2 = [p.sbuf("wo2_wl0", (128, NF, 128), BF16, scope=sc)]
        s_t = p.sbuf("wo2_s", (128, T), BF16, scope=sc)
        psc = [p.psum(f"wo2_psc{j}", (128, 512), F32, scope=sc)
               for j in range(4)]
        ps_t = p.psum("wo2_pst", (128, 128), BF16, scope=sc)
        for m in range(NF):
            wr = wraw2[0]
            wl = wlhs2[0]
            trans_weight_blk(p, W["Wo"], m, NF, wr, wl, ps_t)
            for ci, (m0, mw) in enumerate(mvs(T)):
                pc = psc[ci]
                for kc in range(NF):
                    def f_mm(e, m0=m0, mw=mw, kc=kc, wl=wl, pc=pc):
                        return e.matmul(pc.ap[:, 0:mw], wl.ap[:, kc, :],
                                        a_t[kc].ap[:, m0:m0 + mw],
                                        start=(kc == 0), stop=(kc == NF - 1))
                    p.emit("pe", f_mm, reads=(wl, a_t[kc]),
                           writes=(pc,) if kc == NF - 1 else (),
                           war=(pc,) if kc == 0 else (),
                           inc=(kc == NF - 1))

                def f_sb(e, m=m, m0=m0, mw=mw, pc=pc):
                    return e.tensor_scalar(s_t.ap[:, m0:m0 + mw],
                                           pc.ap[:, 0:mw],
                                           bo.ap[:, m:m + 1], None, ALU.add)
                p.emit("dve", f_sb, reads=(pc, bo), writes=(s_t,))

            def f_st(e, m=m):
                return e.dma_start(out=s2t_hbm.ap[m * 128:(m + 1) * 128, :],
                                   in_=s_t.ap[:])
            p.emit("sp", f_st, reads=(s_t,), writes=(s2t_hbm,),
                   dma_queue="dsp")


def b1_pkm_pre(p, T, xn2_fm, xn2_tm, W, dbg, ncores=1, sc_outer=None):
    NTT = T // 128
    gam = load_featvec(p, "gam_fv", W["gamma"].ap, C, scope=sc_outer)
    bet = load_featvec(p, "bet_fv", W["beta"].ap, C, scope=sc_outer)

    sums = p.sbuf("bn_sums", (128, NF, 2), F32, scope=sc_outer)
    with p.scope() as sc:
        sqf = p.sbuf("bn_sqf", (128, T), BF16, scope=sc)
        for fi in range(NF):
            def f_s(e, fi=fi):
                return e.tensor_reduce(sums.ap[:, fi, 0:1], xn2_fm[fi].ap[:],
                                       AX.X, ALU.add)
            p.emit("dve", f_s, reads=(xn2_fm[fi],), writes=(sums,))

            def f_qq(e, fi=fi):
                return e.activation(sqf.ap[:], xn2_fm[fi].ap[:], AF.Square,
                                    accum_out=sums.ap[:, fi, 1:2])
            p.emit("act", f_qq, reads=(xn2_fm[fi],), writes=(sums, sqf))

    if ncores > 1:
        cin = p.dram_tmp("bn_cin", (128, 2 * NF), F32)
        cout = p.dram_tmp("bn_cout", (128, 2 * NF), F32)
        p.emit("sp", lambda e: e.dma_start(
            out=cin.ap[:], in_=sums.ap.rearrange("p a b -> p (a b)")),
            reads=(sums,), writes=(cin,), dma_queue="dsp")

        def f_cc(e):
            return e.collective_compute(
                "AllReduce", ALU.add,
                [[i for i in range(ncores)]], [cin.ap[:].opt()],
                [cout.ap[:].opt()])
        p.emit("pool", f_cc, reads=(cin,), writes=(cout,))
        p.emit("sp", lambda e: e.dma_start(
            out=sums.ap.rearrange("p a b -> p (a b)"), in_=cout.ap[:]),
            reads=(cout,), writes=(sums,), dma_queue="dsp")

    ab = p.sbuf("bn_ab", (128, NF, 2), F32, scope=sc_outer)
    with p.scope() as sc:
        st = p.sbuf("bn_st", (128, NF, 4), F32, scope=sc)
        inv_n = 1.0 / (ncores * T)
        p.emit("dve", lambda e: e.tensor_scalar(
            st.ap[:, :, 0], sums.ap[:, :, 0], inv_n, None, ALU.mult),
            reads=(sums,), writes=(st,))
        p.emit("dve", lambda e: e.tensor_tensor(
            st.ap[:, :, 1], st.ap[:, :, 0], st.ap[:, :, 0], ALU.mult),
            reads=(st,), writes=(st,))
        p.emit("dve", lambda e: e.scalar_tensor_tensor(
            st.ap[:, :, 2], sums.ap[:, :, 1], inv_n, st.ap[:, :, 1],
            ALU.mult, ALU.subtract), reads=(sums, st), writes=(st,))
        p.emit("dve", lambda e: e.tensor_scalar(
            st.ap[:, :, 2], st.ap[:, :, 2], 1e-5, None, ALU.add),
            reads=(st,), writes=(st,))
        p.emit("dve", lambda e: e.reciprocal(
            st.ap[:, :, 3], st.ap[:, :, 2]), reads=(st,), writes=(st,))
        p.emit("act", lambda e: e.activation(
            st.ap[:, :, 3], st.ap[:, :, 3], AF.Sqrt),
            reads=(st,), writes=(st,))
        p.emit("dve", lambda e: e.tensor_tensor(
            ab.ap[:, :, 0], gam.ap[:], st.ap[:, :, 3], ALU.mult),
            reads=(gam, st), writes=(ab,))
        p.emit("dve", lambda e: e.tensor_tensor(
            st.ap[:, :, 1], st.ap[:, :, 0], ab.ap[:, :, 0], ALU.mult),
            reads=(st, ab), writes=(st,))
        p.emit("dve", lambda e: e.tensor_tensor(
            ab.ap[:, :, 1], bet.ap[:], st.ap[:, :, 1], ALU.subtract),
            reads=(bet, st), writes=(ab,))

    xb_fm = [p.sbuf(f"xb_fm{i}", (128, T), BF16, scope=sc_outer)
             for i in range(NF)]
    for fi in range(NF):
        def f_xb(e, fi=fi):
            return e.tensor_scalar(xb_fm[fi].ap[:], xn2_fm[fi].ap[:],
                                   ab.ap[:, fi, 0:1], ab.ap[:, fi, 1:2],
                                   ALU.mult, ALU.add)
        p.emit("dve", f_xb, reads=(xn2_fm[fi], ab), writes=(xb_fm[fi],))
    if "xb" in dbg:
        dump_fm(p, xb_fm, T, "dbg_xb", BF16)

    svv = [p.sbuf(f"pk_sv{j}", (128, NTT, 8, 8), F32, scope=sc_outer)
           for j in range(2)]
    sii = [p.sbuf(f"pk_si{j}", (128, NTT, 8, 8), F32, scope=sc_outer)
           for j in range(2)]
    with p.scope() as sc:
        wraw = [p.sbuf(f"pk_wr{j}", (128, NF, 128), BF16, scope=sc)
                for j in range(2)]
        wlhs = [p.sbuf(f"pk_wl{j}", (128, NF, 128), BF16, scope=sc)
                for j in range(2)]
        wbi = [0]
        kraw = p.sbuf("pk_kr", (128, 2, 128), BF16, scope=sc)
        keysT = p.sbuf("pk_kT", (128, 256), BF16, scope=sc)
        q2b = p.sbuf("pk_q2", (128, T), BF16, scope=sc)
        sim_sb = p.sbuf("pk_sim", (128, 256), F32, scope=sc)
        m8 = p.sbuf("pk_m8", (128, 8), F32, scope=sc)
        mi8 = p.sbuf("pk_mi8", (128, 8), U32, scope=sc)
        for p2 in range(2):
            for h in range(8):
                with p.scope() as scq:
                    ps_q = p.psum("pk_psq", (128, T), F32, scope=scq)
                    ps_t = p.psum("pk_pst", (128, 128), BF16, scope=scq)
                    ps_k = p.psum("pk_psk", (128, 256), BF16, scope=scq)
                    ps_s = p.psum("pk_pss", (128, 256), F32, scope=scq)
                    mrow = p2 * 8 + h
                    wr = wraw[wbi[0] % 2]
                    wl = wlhs[wbi[0] % 2]
                    wbi[0] += 1
                    trans_weight_blk(p, W["w_q"], mrow, NF, wr, wl, ps_t)
                    for kc in range(NF):
                        sl = mvs(T)
                        for mi, (m0, mw) in enumerate(sl):
                            def f_mm(e, kc=kc, m0=m0, mw=mw, wl=wl):
                                return e.matmul(
                                    ps_q.ap[:, m0:m0 + mw], wl.ap[:, kc, :],
                                    xb_fm[kc].ap[:, m0:m0 + mw],
                                    start=(kc == 0), stop=(kc == NF - 1))
                            p.emit("pe", f_mm, reads=(wl, xb_fm[kc]),
                                   writes=(ps_q,) if kc == NF - 1 and
                                   mi == len(sl) - 1 else (),
                                   war=(ps_q,) if kc == 0 and mi == 0 else (),
                                   inc=(mi == len(sl) - 1))

                    def f_qe(e):
                        return e.tensor_copy(q2b.ap[:], ps_q.ap[:])
                    p.emit("dve", f_qe, reads=(ps_q,), writes=(q2b,))

                    def f_kl(e, p2=p2, h=h):
                        return e.dma_start(
                            out=kraw.ap[:],
                            in_=W["keys"].ap[h, :, p2, :].rearrange(
                                "(a b) c -> b a c", b=128))
                    p.emit("pool", f_kl, reads=(W["keys"],), writes=(kraw,),
                           dma_queue="dpool")
                    for half in range(2):
                        def f_kt(e, half=half):
                            return e.transpose(
                                ps_k.ap[:, half * 128:(half + 1) * 128],
                                kraw.ap[:, half, :], p.idn.ap[:])
                        p.emit("pe", f_kt, reads=(kraw, p.idn),
                               writes=(ps_k,) if half else (),
                               war=(ps_k,) if half == 0 else (),
                               inc=(half == 1))

                    def f_ke(e):
                        return e.copy(keysT.ap[:], ps_k.ap[:])
                    p.emit("act", f_ke, reads=(ps_k,), writes=(keysT,))

                    for tt in range(NTT):
                        def f_sm(e, tt=tt):
                            return e.matmul(
                                ps_s.ap[:], q2b.ap[:, tt * 128:(tt + 1) * 128],
                                keysT.ap[:], start=True, stop=True)
                        p.emit("pe", f_sm, reads=(q2b, keysT), writes=(ps_s,),
                               war=(ps_s,))

                        def f_se(e):
                            return e.tensor_copy(sim_sb.ap[:], ps_s.ap[:])
                        p.emit("dve", f_se, reads=(ps_s,), writes=(sim_sb,))

                        def f_m8(e):
                            return e.max(m8.ap[:], sim_sb.ap[:])
                        p.emit("dve", f_m8, reads=(sim_sb,), writes=(m8,))

                        def f_mi(e):
                            return e.max_index(mi8.ap[:], m8.ap[:],
                                               sim_sb.ap[:])
                        p.emit("dve", f_mi, reads=(m8, sim_sb), writes=(mi8,))

                        def f_sv(e, p2=p2, h=h, tt=tt):
                            return e.tensor_copy(svv[p2].ap[:, tt, h, :],
                                                 m8.ap[:])
                        p.emit("dve", f_sv, reads=(m8,), writes=(svv[p2],))

                        def f_si(e, p2=p2, h=h, tt=tt):
                            return e.tensor_copy(sii[p2].ap[:, tt, h, :],
                                                 mi8.ap[:])
                        p.emit("dve", f_si, reads=(mi8,), writes=(sii[p2],))


    idx_all = p.sbuf("pk_idxall", (128, NTT, 64), U32, scope=sc_outer)
    attw_all = p.sbuf("pk_attwall", (128, NTT, 64), F32, scope=sc_outer)
    iota64 = p.sbuf("pk_iota64", (128, 64), F32, scope=sc_outer)
    p.emit("pool", lambda e: e.iota(iota64.ap[:], pattern=[[1, 64]], base=0,
                                    channel_multiplier=0,
                                    allow_small_or_imprecise_dtypes=True),
           writes=(iota64,))
    with p.scope() as sc:
        alls = p.sbuf("pk_alls", (128, 8, 8, 8), F32, scope=sc)
        i256 = p.sbuf("pk_i256", (128, 8, 8), F32, scope=sc)
        alli = p.sbuf("pk_alli", (128, 8, 64), F32, scope=sc)
        sc8 = p.sbuf("pk_sc8", (128, 8, 8), F32, scope=sc)
        pk8 = p.sbuf("pk_pk8", (128, 8, 8), U32, scope=sc)
        pk8f = p.sbuf("pk_pk8f", (128, 8, 8), F32, scope=sc)
        eqb = p.sbuf("pk_eqb", (128, 8, 8, 64), F32, scope=sc)
        idxf = p.sbuf("pk_idxf", (128, 64), F32, scope=sc)
        attw = p.sbuf("pk_attw", (128, 8, 8), F32, scope=sc)
        srow = p.sbuf("pk_srow", (128, 8, 4), F32, scope=sc)
        for tt in range(NTT):
            def f_as(e, tt=tt):
                return e.tensor_tensor(
                    alls.ap[:],
                    svv[0].ap[:, tt, :, :].unsqueeze(3).broadcast_to(
                        (128, 8, 8, 8)),
                    svv[1].ap[:, tt, :, :].unsqueeze(2).broadcast_to(
                        (128, 8, 8, 8)), ALU.add)
            p.emit("dve", f_as, reads=(svv[0], svv[1]), writes=(alls,))

            def f_ai0(e, tt=tt):
                return e.tensor_scalar(i256.ap[:], sii[0].ap[:, tt, :, :],
                                       256.0, None, ALU.mult)
            p.emit("dve", f_ai0, reads=(sii[0],), writes=(i256,))

            def f_ai(e, tt=tt):
                return e.tensor_tensor(
                    alli.ap[:].rearrange("p h (a b) -> p h a b", a=8),
                    i256.ap[:].unsqueeze(3).broadcast_to((128, 8, 8, 8)),
                    sii[1].ap[:, tt, :, :].unsqueeze(2).broadcast_to(
                        (128, 8, 8, 8)), ALU.add)
            p.emit("dve", f_ai, reads=(i256, sii[1]), writes=(alli,))

            for h in range(8):
                def f_m8(e, h=h):
                    return e.max(sc8.ap[:, h, :],
                                 alls.ap[:, h, :, :].rearrange(
                                     "p a b -> p (a b)"))
                p.emit("dve", f_m8, reads=(alls,), writes=(sc8,))

                def f_mi(e, h=h):
                    return e.max_index(pk8.ap[:, h, :], sc8.ap[:, h, :],
                                       alls.ap[:, h, :, :].rearrange(
                                           "p a b -> p (a b)"))
                p.emit("dve", f_mi, reads=(sc8, alls), writes=(pk8,))

            def f_pf(e):
                return e.tensor_copy(pk8f.ap[:], pk8.ap[:])
            p.emit("dve", f_pf, reads=(pk8,), writes=(pk8f,))

            def f_eq(e):
                return e.tensor_tensor(
                    eqb.ap[:],
                    iota64.ap[:].unsqueeze(1).unsqueeze(2).broadcast_to(
                        (128, 8, 8, 64)),
                    pk8f.ap[:].unsqueeze(3).broadcast_to((128, 8, 8, 64)),
                    ALU.is_equal)
            p.emit("dve", f_eq, reads=(iota64, pk8f), writes=(eqb,))

            def f_eqm(e):
                return e.tensor_tensor(
                    eqb.ap[:], eqb.ap[:],
                    alli.ap[:].unsqueeze(2).broadcast_to((128, 8, 8, 64)),
                    ALU.mult)
            p.emit("dve", f_eqm, reads=(eqb, alli), writes=(eqb,))

            def f_red(e):
                return e.tensor_reduce(
                    idxf.ap[:].rearrange("p (h s) -> p h s", h=8).unsqueeze(3),
                    eqb.ap[:], AX.X, ALU.add)
            p.emit("dve", f_red, reads=(eqb,), writes=(idxf,))

            def f_iu(e, tt=tt):
                return e.tensor_copy(idx_all.ap[:, tt, :], idxf.ap[:])
            p.emit("dve", f_iu, reads=(idxf,), writes=(idx_all,))

            def f_smx(e):
                return e.tensor_reduce(srow.ap[:, :, 0:1], sc8.ap[:],
                                       AX.X, ALU.max)
            p.emit("dve", f_smx, reads=(sc8,), writes=(srow,))

            def f_ssub(e):
                return e.tensor_tensor(
                    attw.ap[:], sc8.ap[:],
                    srow.ap[:, :, 0:1].broadcast_to((128, 8, 8)),
                    ALU.subtract)
            p.emit("dve", f_ssub, reads=(sc8, srow), writes=(attw,))

            def f_sex(e):
                return e.activation(attw.ap[:], attw.ap[:], AF.Exp)
            p.emit("act", f_sex, reads=(attw,), writes=(attw,))

            def f_ssm(e):
                return e.tensor_reduce(srow.ap[:, :, 1:2], attw.ap[:],
                                       AX.X, ALU.add)
            p.emit("dve", f_ssm, reads=(attw,), writes=(srow,))

            def f_src(e):
                return e.reciprocal(srow.ap[:, :, 2:3], srow.ap[:, :, 1:2])
            p.emit("dve", f_src, reads=(srow,), writes=(srow,))

            def f_sw(e, tt=tt):
                return e.tensor_tensor(
                    attw_all.ap[:, tt, :].rearrange("p (h s) -> p h s", h=8),
                    attw.ap[:],
                    srow.ap[:, :, 2:3].broadcast_to((128, 8, 8)), ALU.mult)
            p.emit("dve", f_sw, reads=(attw, srow), writes=(attw_all,))
    return idx_all, attw_all


def b1_pkm_ctx(p, T, sc, xn2_tm, W, idx_all, attw_all, g2_hbm):
    NG = 3
    ctx = {"NG": NG, "xn2_tm": xn2_tm, "W": W, "idx_all": idx_all,
           "attw_all": attw_all, "g2_hbm": g2_hbm}
    ctx["G1"] = [p.sbuf(f"pk_G1{j}", (128, C), BF16, scope=sc)
                 for j in range(NG)]
    ctx["G2"] = [p.sbuf(f"pk_G2{j}", (128, C), BF16, scope=sc)
                 for j in range(NG)]
    ctx["junk"] = p.sbuf("pk_junk", (128, C), BF16, scope=sc)
    ctx["otk"] = p.sbuf("pk_otk", (128, 64), F32, scope=sc)
    ctx["ya"] = [p.sbuf(f"pk_ya{j}", (128, C), F32, scope=sc)
                 for j in range(2)]
    ctx["yfin"] = p.sbuf("pk_yfin", (128, C), F32, scope=sc)
    ctx["g2t"] = p.sbuf("pk_g2t", (128, C), BF16, scope=sc)
    return ctx


def b1_pkm_tile(p, T, ctx, tt):
    NG = ctx["NG"]
    PUMP = 16
    G1, G2 = ctx["G1"], ctx["G2"]
    junk, otk, ya, yfin = ctx["junk"], ctx["otk"], ctx["ya"], ctx["yfin"]
    g2t = ctx["g2t"]
    W, idx_all, attw_all = ctx["W"], ctx["idx_all"], ctx["attw_all"]
    xn2_tm = ctx["xn2_tm"]
    for s_ in range(64):
        g = G1[s_ % NG]

        def f_g(e, tt=tt, s_=s_, g=g):
            return e.indirect_dma_start(
                out=g.ap[:], out_offset=None, in_=W["key_in"].ap[:],
                in_offset=bass.IndirectOffsetOnAxis(
                    ap=idx_all.ap[:, tt, s_:s_ + 1], axis=0))
        p.emit("pool", f_g, reads=(W["key_in"], idx_all), writes=(g,),
               dma_queue="dpool")

        def f_o(e, s_=s_, tt=tt, g=g):
            return e.scalar_tensor_tensor(
                junk.ap[:], g.ap[:], 1.0, xn2_tm[tt].ap[:],
                ALU.mult, ALU.mult, accum_out=otk.ap[:, s_:s_ + 1])
        p.emit("dve", f_o, reads=(g, xn2_tm[tt]), writes=(otk,),
               war=(junk,))
        if s_ % PUMP == PUMP - 1:
            yield

    def f_gl(e):
        return e.activation(otk.ap[:], otk.ap[:], AF.Gelu)
    p.emit("act", f_gl, reads=(otk,), writes=(otk,))

    def f_aw(e, tt=tt):
        return e.tensor_tensor(otk.ap[:], otk.ap[:],
                               attw_all.ap[:, tt, :], ALU.mult)
    p.emit("dve", f_aw, reads=(otk, attw_all), writes=(otk,))

    for j in range(2):
        p.emit("dve", lambda e, j=j: e.memset(ya[j].ap[:], 0.0),
               writes=(ya[j],))
    for s_ in range(64):
        g = G2[s_ % NG]

        def f_g(e, tt=tt, s_=s_, g=g):
            return e.indirect_dma_start(
                out=g.ap[:], out_offset=None, in_=W["key_out"].ap[:],
                in_offset=bass.IndirectOffsetOnAxis(
                    ap=idx_all.ap[:, tt, s_:s_ + 1], axis=0))
        p.emit("pool", f_g, reads=(W["key_out"], idx_all), writes=(g,),
               dma_queue="dpool")

        acc = ya[s_ % 2]

        def f_y(e, s_=s_, g=g, acc=acc):
            return e.scalar_tensor_tensor(
                acc.ap[:], g.ap[:], otk.ap[:, s_:s_ + 1], acc.ap[:],
                ALU.mult, ALU.add)
        p.emit("dve", f_y, reads=(g, otk, acc), writes=(acc,))
        if s_ % PUMP == PUMP - 1:
            yield

    def f_t1(e):
        return e.tensor_tensor(yfin.ap[:], ya[0].ap[:], ya[1].ap[:],
                               ALU.add)
    p.emit("dve", f_t1, reads=(ya[0], ya[1]), writes=(yfin,))

    def f_g2(e):
        return e.activation(g2t.ap[:], yfin.ap[:], AF.Silu)
    p.emit("act", f_g2, reads=(yfin,), writes=(g2t,))

    def f_gs(e, tt=tt):
        return e.dma_start(out=ctx["g2_hbm"].ap[tt * 128:(tt + 1) * 128, :],
                           in_=g2t.ap[:])
    p.emit("sp", f_gs, reads=(g2t,), writes=(ctx["g2_hbm"],),
           dma_queue="dsp")
    yield


def b1_combine_setup(p, sc):
    cctx = {}
    cctx["x1t"] = p.sbuf("cbt_x1t", (128, C), F32, scope=sc)
    cctx["g2l"] = p.sbuf("cbt_g2", (128, C), BF16, scope=sc)
    cctx["s2stage"] = p.sbuf("cbt_s2st", (128, C), BF16, scope=sc)
    cctx["s2t"] = p.sbuf("cbt_s2t", (128, C), BF16, scope=sc)
    cctx["out_t"] = p.sbuf("cbt_out", (128, C), F32, scope=sc)
    cctx["ps_t2"] = [p.psum(f"cbt_pst{j}", (128, 128), BF16, scope=sc)
                     for j in range(2)]
    return cctx


def b1_combine_tile(p, T, cctx, tt, g2_hbm, s2f_hbm, x1t_hbm, out_hbm):
    x1t, g2l = cctx["x1t"], cctx["g2l"]
    s2stage, s2t = cctx["s2stage"], cctx["s2t"]
    out_t, ps_t2 = cctx["out_t"], cctx["ps_t2"]
    if True:
        if True:
            def f_x1(e, tt=tt):
                return e.dma_start(out=x1t.ap[:],
                                   in_=x1t_hbm.ap[tt * 128:(tt + 1) * 128, :])
            p.emit("sp", f_x1, reads=(x1t_hbm,), writes=(x1t,),
                   dma_queue="dsp")

            def f_gl(e, tt=tt):
                return e.dma_start(out=g2l.ap[:],
                                   in_=g2_hbm.ap[tt * 128:(tt + 1) * 128, :])
            p.emit("sp", f_gl, reads=(g2_hbm,), writes=(g2l,),
                   dma_queue="dsp")

            for fi in range(NF):
                def f_sl(e, tt=tt, fi=fi):
                    return e.dma_start(
                        out=s2stage.ap[:, fi * 128:(fi + 1) * 128],
                        in_=s2f_hbm.ap[fi * 128:(fi + 1) * 128,
                                       tt * 128:(tt + 1) * 128])
                p.emit("sp", f_sl, reads=(s2f_hbm,), writes=(s2stage,),
                       dma_queue="dsp")
            for fi in range(NF):
                pst = ps_t2[fi % 2]

                def f_tr(e, fi=fi, pst=pst):
                    return e.transpose(
                        pst.ap[:], s2stage.ap[:, fi * 128:(fi + 1) * 128],
                        p.idn.ap[:])
                p.emit("pe", f_tr, reads=(s2stage, p.idn), writes=(pst,))

                def f_ev(e, fi=fi, pst=pst):
                    return e.copy(s2t.ap[:, fi * 128:(fi + 1) * 128],
                                  pst.ap[:])
                p.emit("act", f_ev, reads=(pst,), writes=(s2t,))

            def f_sg(e):
                return e.tensor_tensor(out_t.ap[:], s2t.ap[:], g2l.ap[:],
                                       ALU.mult)
            p.emit("dve", f_sg, reads=(s2t, g2l), writes=(out_t,))

            def f_add(e):
                return e.tensor_tensor(out_t.ap[:], out_t.ap[:], x1t.ap[:],
                                       ALU.add)
            p.emit("dve", f_add, reads=(out_t, x1t), writes=(out_t,))

            def f_out(e, tt=tt):
                return e.dma_start(
                    out=out_hbm.ap[tt * 128:(tt + 1) * 128, :],
                    in_=out_t.ap[:])
            p.emit("sp", f_out, reads=(out_t,), writes=(out_hbm,),
                   dma_queue="dsp")


def b1_body(p, T, W, x1f_hbm, x1t_hbm, out_hbm, dbg, ncores):
    NTT = T // 128
    with p.scope() as scb1:
        xn2_fm = [p.sbuf(f"xn2_fm{i}", (128, T), BF16, scope=scb1)
                  for i in range(NF)]
        xn2_tm = [p.sbuf(f"xn2_tm{i}", (128, C), BF16, scope=scb1)
                  for i in range(T // 128)]
        b1_entry(p, T, x1f_hbm, xn2_fm, xn2_tm, dbg, ncores=ncores)
        if "xn2" in dbg:
            dump_fm(p, xn2_fm, T, "dbg_xn2", BF16)
        s2f_hbm = (p.dram_out("dbg_s2", (C, T), BF16) if "s2" in dbg
                   else p.dram_tmp("s2f_hbm", (C, T), BF16))
        idx_all, attw_all = b1_pkm_pre(p, T, xn2_fm, xn2_tm, W, dbg,
                                       ncores=ncores, sc_outer=scb1)
        g2_hbm = p.dram_tmp("g2_hbm", (T, C), BF16)
        biases = {
            "bq": load_featvec(p, "bq_fv", W["bq"].ap, C, scope=scb1),
            "bk": load_featvec(p, "bk_fv", W["bk"].ap, C, scope=scb1),
            "bv": load_featvec(p, "bv_fv", W["bv"].ap, C, scope=scb1),
            "bo": load_featvec(p, "bo_fv", W["bo"].ap, C, scope=scb1),
        }
        d_hbm = p.dram_tmp("d_hbm", (16, T), F32)
        hm_col = p.sbuf("at_hmcol", (16, 1), F32, scope=scb1)
        p.emit("sp", lambda e: e.dma_start(out=hm_col.ap[:],
                                           in_=W["hm"].ap.unsqueeze(1)),
               reads=(W["hm"],), writes=(hm_col,), dma_queue="dsp")
        with p.scope() as scpk:
            ctx2 = b1_pkm_ctx(p, T, scpk, xn2_tm, W, idx_all, attw_all,
                              g2_hbm)
            gens = [b1_pkm_tile(p, T, ctx2, tt) for tt in range(NTT)]

            def nopump(n):
                return None
            with p.scope() as scat:
                ctx = b1_attn_setup(p, T, W, scat, biases)
                ctx["d_hbm"] = d_hbm
                ctx["hm_col"] = hm_col
                for mc in range(NF):
                    b1_attn_chunk(p, T, ctx, xn2_fm, W, mc, dbg, nopump)
                attn_hbm = ctx["attn_hbm"]
            b1_attn_wo(p, T, ctx, W, s2f_hbm)
            cctx = b1_combine_setup(p, scpk)
            for tt in range(NTT):
                for _ in gens[tt]:
                    pass
                b1_combine_tile(p, T, cctx, tt, g2_hbm, s2f_hbm,
                                x1t_hbm, out_hbm)


def build_b1(T=2048, dbg=(), ncores=1):
    p = Prog("b1")
    W = declare_b1_weights(p)
    x1f_hbm = p.dram_in("x1f", (C, T), F32)
    x1t_hbm = p.dram_in("x1t", (T, C), F32)
    out_hbm = p.dram_out("out", (T, C), F32)
    setup_consts(p, T)
    b1_body(p, T, W, x1f_hbm, x1t_hbm, out_hbm, dbg, ncores)
    return p



def declare_b0_weights(p, T):
    W = {}
    W["maa_x"] = p.dram_in("maa_x", (1, 1, C), F32)
    W["maa_k"] = p.dram_in("maa_k", (1, 1, C), F32)
    W["maa_v"] = p.dram_in("maa_v", (1, 1, C), F32)
    W["maa_r"] = p.dram_in("maa_r", (1, 1, C), F32)
    W["w1"] = p.dram_in("w1", (C, 96), F32)
    W["w2"] = p.dram_in("w2", (3, 32, C), F32)
    W["u"] = p.dram_in("u", (16, 64), F32)
    W["Wr"] = p.dram_in("Wr", (C, C), F32)
    W["Wk"] = p.dram_in("Wk", (C, C), F32)
    W["Wvg"] = p.dram_in("Wvg", (C, 6 * C), F32)
    W["Wo"] = p.dram_in("Wo", (3 * C, C), F32)
    W["ln_g"] = p.dram_in("ln_g", (C,), F32)
    W["ln_b"] = p.dram_in("ln_b", (C,), F32)
    W["mWr"] = p.dram_in("mWr", (C, 8), F32)
    W["mbr"] = p.dram_in("mbr", (8,), F32)
    W["mW1"] = p.dram_in("mW1", (8, C, 4096), F32)
    W["mb1"] = p.dram_in("mb1", (8, 4096), F32)
    W["mW2"] = p.dram_in("mW2", (8, 4096, C), F32)
    W["mb2"] = p.dram_in("mb2", (8, C), F32)
    return W


def build_b0(T=2048, upto="all", dbg=()):
    p = Prog("b0")
    x_in = p.dram_in("x", (T, C), F32)
    W = declare_b0_weights(p, T)
    setup_consts(p, T)
    xn_hbm = p.dram_tmp("xn_hbm", (C, T), BF16)
    maa = {nm: load_featvec(p, nm,
                            W[nm].ap.rearrange("a b c -> (a b c)"), C)
           for nm in ("maa_x", "maa_k", "maa_v", "maa_r")}
    u_fm = load_featvec(p, "u_fv", W["u"].ap.rearrange("a b -> (a b)"), C)
    scmid = p.scope()
    scmid_es = scmid.__enter__()
    SEGW = min(256, T)
    QCHW = 1024
    nbAW = SEGW // 8
    p.wkv_bufs = {
        "P0": p.sbuf("wk_P0", (128, nbAW * 64), F32, scope=scmid_es),
        "E": p.sbuf("wk_E", (128, nbAW * 64), F32, scope=scmid_es),
        "RK": p.sbuf("wk_RK", (128, nbAW * 64), BF16, scope=scmid_es),
        "OIt": p.sbuf("wk_OIt", (128, QCHW), F32, scope=scmid_es),
        "OIr": p.sbuf("wk_OIr", (128, SEGW), F32, scope=scmid_es),
        "Msb": p.sbuf("wk_Msb", (2, QCHW), F32, scope=scmid_es),
    }
    xk_fm = [p.sbuf(f"xk_fm{i}", (128, T), BF16, scope=scmid_es)
             for i in range(NF)]
    xv_fm = [p.sbuf(f"xv_fm{i}", (128, T), BF16, scope=scmid_es)
             for i in range(NF)]
    xr_fm = [p.sbuf(f"xr_fm{i}", (128, T), BF16, scope=scmid_es)
             for i in range(NF)]
    with p.scope() as scxn:
        xn_fm = [p.sbuf(f"xn_fm{i}", (128, T), BF16, scope=scxn)
                 for i in range(NF)]
        b0_entry(p, T, x_in, xn_fm)
        if "xn" in dbg:
            dump_fm(p, xn_fm, T, "dbg_xn", BF16)
        if upto == "entry":
            return p
        b0_mixer(p, T, xn_fm, W, maa, dbg, xk_fm, xv_fm, xr_fm)
        for fi in range(NF):
            def f_px(e, fi=fi):
                return e.dma_start(out=xn_hbm.ap[fi * 128:(fi + 1) * 128, :],
                                   in_=xn_fm[fi].ap[:])
            p.emit("sp", f_px, reads=(xn_fm[fi],), writes=(xn_hbm,),
                   dma_queue="dsp")
    if "xk" in dbg:
        dump_fm(p, xk_fm, T, "dbg_xk", BF16)
        dump_fm(p, xv_fm, T, "dbg_xv", BF16)
        dump_fm(p, xr_fm, T, "dbg_xr", BF16)
    if upto == "mixer":
        return p
    o_hbm = (p.dram_out("dbg_o", (C, T), BF16) if "o" in dbg
             else p.dram_tmp("o_hbm", (C, T), BF16))
    ffn_hbm = (p.dram_out("dbg_ffn", (2 * C, T), BF16) if "ffn" in dbg
               else p.dram_tmp("ffn_hbm", (2 * C, T), BF16))
    b0_projections(p, T, xk_fm, xv_fm, xr_fm, W, u_fm, o_hbm, ffn_hbm, dbg,
                   do_wkv=(upto != "proj"))
    scmid.__exit__(None, None, None)
    if upto in ("proj", "wkv"):
        return p
    if upto == "fused":
        x1_hbm = p.dram_tmp("x1f", (C, T), F32)
        x1t_hbm = p.dram_tmp("x1t", (T, C), F32)
    elif upto == "launch":
        x1_hbm = p.dram_out("x1f", (C, T), F32)
        x1t_hbm = p.dram_out("x1t", (T, C), F32)
    else:
        x1_hbm = (p.dram_out("dbg_x1", (C, T), F32) if "x1" in dbg
                  else p.dram_tmp("x1_hbm", (C, T), F32))
        x1t_hbm = None
    b0_output(p, T, xn_hbm, o_hbm, ffn_hbm, W, x_in, x1_hbm, dbg,
              x1t_hbm=x1t_hbm)
    if upto == "fused":
        return p, x1_hbm, x1t_hbm
    return p


def build_fused(T=2048, ncores=8, dbg=()):
    p, x1f_hbm, x1t_hbm = build_b0(T=T, upto="fused", dbg=dbg)
    W1 = declare_b1_weights(p)
    out_hbm = p.dram_out("out", (T, C), F32)
    b1_body(p, T, W1, x1f_hbm, x1t_hbm, out_hbm, dbg, ncores)
    return p


B0_INMAP = {
    "maa_x": "L0_attn_maa_x", "maa_k": "L0_attn_maa_k",
    "maa_v": "L0_attn_maa_v", "maa_r": "L0_attn_maa_r",
    "w1": "L0_attn_w1", "w2": "L0_attn_w2", "u": "L0_attn_u",
    "Wr": "L0_attn_Wr", "Wk": "L0_attn_Wk", "Wvg": "L0_attn_Wvg",
    "Wo": "L0_attn_Wo", "ln_g": "L0_attn_ln_g", "ln_b": "L0_attn_ln_b",
    "mWr": "L0_ffn_Wr", "mbr": "L0_ffn_br", "mW1": "L0_ffn_W1",
    "mb1": "L0_ffn_b1", "mW2": "L0_ffn_W2", "mb2": "L0_ffn_b2",
}
B1_INMAP = {
    "aWq": "L1_attn_Wq", "abq": "L1_attn_bq", "aWk": "L1_attn_Wk",
    "abk": "L1_attn_bk", "aWv": "L1_attn_Wv", "abv": "L1_attn_bv",
    "aWo": "L1_attn_Wo", "abo": "L1_attn_bo", "ahm": "L1_attn_head_mask",
    "pgamma": "L1_ffn_gamma", "pbeta": "L1_ffn_beta", "pwq": "L1_ffn_w_q",
    "pkeys": "L1_ffn_keys", "pkin": "L1_ffn_key_in",
    "pkout": "L1_ffn_key_out",
}


def kernel(**inputs):
    T, NCORES = 2048, 8
    x = np.ascontiguousarray(np.asarray(inputs["L0_attn_x"], np.float32))

    p = build_fused(T=T, ncores=NCORES)
    nc = p.finalize()
    w = {k: np.ascontiguousarray(np.asarray(inputs[v], np.float32))
         for k, v in B0_INMAP.items()}
    w.update({k: np.ascontiguousarray(np.asarray(inputs[v], np.float32))
              for k, v in B1_INMAP.items()})
    maps = [dict(w, x=np.ascontiguousarray(x[i])) for i in range(NCORES)]
    res = run_bass_kernel_spmd(nc, maps, core_ids=list(range(NCORES)))

    return np.stack([np.asarray(res.results[i]["out"], np.float32)
                     for i in range(NCORES)], axis=0)


# version 10 (final)
# speedup vs baseline: 1.1249x; 1.1249x over previous
import numpy as np
from contextlib import ExitStack

import concourse.bass as bass
import concourse.mybir as mybir
from concourse.bass_utils import run_bass_kernel_spmd

F32 = mybir.dt.float32
BF16 = mybir.dt.bfloat16
U32 = mybir.dt.uint32
I32 = mybir.dt.int32
AF = mybir.ActivationFunctionType
ALU = mybir.AluOpType
AX = mybir.AxisListType

C = 1024
NF = C // 128

ENGINES = ("pe", "dve", "act", "pool", "sp")
DMAQ = ("dsp", "dact", "dpool")
NPOOL = 8


class Buf:
    __slots__ = ("name", "ap", "writer", "readers")

    def __init__(self, name, ap):
        self.name = name
        self.ap = ap
        self.writer = None
        self.readers = []


class Prog:
    def __init__(self, name="prog"):
        self.nc = bass.Bass("TRN2", target_bir_lowering=False)
        self.es = ExitStack()
        self.ops = {e: [] for e in ENGINES}
        self.sems = {}
        self.sem_count = {}
        for e in ENGINES:
            self.sems[e] = self.es.enter_context(self.nc.semaphore("sem_" + e))
            self.sem_count[e] = 0
        self.dma_pool = {}
        self.dma_idx = {}
        for q in DMAQ:
            self.dma_pool[q] = []
            self.dma_idx[q] = 0
            for j in range(NPOOL):
                nm = f"{q}{j}"
                self.sems[nm] = self.es.enter_context(
                    self.nc.semaphore("sem_" + nm))
                self.sem_count[nm] = 0
                self.dma_pool[q].append(nm)
        self.waited = {e: {} for e in ENGINES}
        self.fence = {}
        self.scope_bufs = None
        self.uid = 0
        self._ps_arena = self.es.enter_context(
            self.nc.psum_tensor([128, 4096], F32))
        self._ps_free = [True] * 8
        self.scope_banks = None

    def unq(self, tag):
        self.uid += 1
        return f"{tag}_{self.uid}"

    def shared(self, name, shape, dtype):
        key = (name, tuple(shape), dtype)
        if not hasattr(self, "_shared"):
            self._shared = {}
        if key not in self._shared:
            self._shared[key] = self.sbuf(name, shape, dtype)
        return self._shared[key]

    def sbuf(self, name, shape, dtype, scope=None):
        t = (scope or self.es).enter_context(
            self.nc.sbuf_tensor(self.unq(name), list(shape), dtype))
        b = Buf(name, t.ap() if hasattr(t, "ap") else t)
        if scope is not None:
            self.scope_bufs.append(b)
        return b

    def psum(self, name, shape, dtype=F32, scope=None):
        shape = list(shape)
        nelem = 1
        for s in shape[1:]:
            nelem *= s
        per = 512 if dtype == F32 else 1024
        nbank = (nelem + per - 1) // per
        run = -1
        for i in range(8 - nbank + 1):
            if all(self._ps_free[i:i + nbank]):
                run = i
                break
        if run < 0:
            raise RuntimeError(
                f"psum arena full for {name} ({nbank} banks; "
                f"free={self._ps_free})")
        for i in range(run, run + nbank):
            self._ps_free[i] = False
        base = self._ps_arena.ap() if hasattr(self._ps_arena, "ap") else \
            self._ps_arena
        ap = base[:, run * 512:(run + nbank) * 512]
        if dtype != F32:
            ap = ap.bitcast(dtype)
        ap = ap[:, 0:nelem]
        if len(shape) > 2:
            spec = " ".join(f"d{i}" for i in range(1, len(shape)))
            ap = ap.rearrange(f"p ({spec}) -> p {spec}",
                              **{f"d{i}": shape[i]
                                 for i in range(1, len(shape))})
        elif shape[0] != 128:
            ap = ap[0:shape[0], :]
        b = Buf(name, ap)
        banks = (run, nbank)
        if scope is not None:
            self.scope_bufs.append(b)
            self.scope_banks.append(banks)
        else:
            pass
        return b

    def dram_in(self, name, shape, dtype):
        return Buf(name, self.nc.dram_tensor(name, list(shape), dtype,
                                             kind="ExternalInput").ap())

    def dram_out(self, name, shape, dtype):
        return Buf(name, self.nc.dram_tensor(name, list(shape), dtype,
                                             kind="ExternalOutput").ap())

    def dram_tmp(self, name, shape, dtype):
        return Buf(name, self.nc.dram_tensor(name, list(shape), dtype,
                                             kind="Internal").ap())

    class _Scope:
        def __init__(self, p):
            self.p = p
            self.es = ExitStack()
            self.prev = None

        def __enter__(self):
            self.prev = self.p.scope_bufs
            self.prev_banks = self.p.scope_banks
            self.p.scope_bufs = []
            self.p.scope_banks = []
            return self.es

        def __exit__(self, *a):
            for b in self.p.scope_bufs:
                toks = list(b.readers)
                if b.writer is not None:
                    toks.append(b.writer)
                for t in toks:
                    if t is not None:
                        s, v = t
                        if self.p.fence.get(s, 0) < v:
                            self.p.fence[s] = v
            for run, nbank in self.p.scope_banks:
                for i in range(run, run + nbank):
                    self.p._ps_free[i] = True
            self.p.scope_bufs = self.prev
            self.p.scope_banks = self.prev_banks
            self.es.close()
            return False

    def scope(self):
        return Prog._Scope(self)

    def emit(self, eng, fn, reads=(), writes=(), war=(), dma_queue=None,
             inc=True):
        waits = {}

        def add_wait(tok):
            if tok is None:
                return
            s, v = tok
            if waits.get(s, 0) < v:
                waits[s] = v

        for b in reads:
            add_wait(b.writer)
        for b in writes:
            add_wait(b.writer)
            for t in b.readers:
                add_wait(t)
            for s, v in self.fence.items():
                if waits.get(s, 0) < v:
                    waits[s] = v
        for b in war:
            add_wait(b.writer)
            for t in b.readers:
                add_wait(t)
            for s, v in self.fence.items():
                if waits.get(s, 0) < v:
                    waits[s] = v

        if dma_queue is not None:
            nm = self.dma_pool[dma_queue][self.dma_idx[dma_queue] % NPOOL]
            self.dma_idx[dma_queue] += 1
            prev = self.sem_count[nm]
            if prev and waits.get(nm, 0) < prev:
                waits[nm] = prev
            self.sem_count[nm] = prev + 16
            tok = (nm, prev + 16)
            sem, amt = self.sems[nm], 16
        elif inc:
            self.sem_count[eng] += 1
            tok = (eng, self.sem_count[eng])
            sem, amt = self.sems[eng], 1
        else:
            tok, sem, amt = None, None, 0

        self.ops[eng].append((waits, fn, sem, amt))

        for b in reads:
            if tok is not None:
                b.readers.append(tok)
        for b in writes:
            b.writer = tok
            b.readers = []
        return tok

    def finalize(self):
        final_waits = {}
        for q in DMAQ:
            for nm in self.dma_pool[q]:
                if self.sem_count[nm]:
                    final_waits[nm] = self.sem_count[nm]
        self.ops["sp"].append((final_waits, lambda e: None, None, 0))
        nc = self.nc
        with nc.Block() as block:
            for eng in ENGINES:
                opslist = self.ops[eng]
                waited = self.waited[eng]
                sems = self.sems

                def replay(engobj, opslist=opslist, waited=waited):
                    for waits, fn, sem, amt in opslist:
                        for s, v in waits.items():
                            if waited.get(s, 0) >= v:
                                continue
                            engobj.wait_ge(sems[s], v)
                            waited[s] = v
                        inst = fn(engobj)
                        if sem is not None:
                            if inst is None:
                                raise RuntimeError("op returned no inst")
                            inst.then_inc(sem, amt)

                {"pe": block.tensor, "dve": block.vector, "act": block.scalar,
                 "pool": block.gpsimd, "sp": block.sync}[eng](replay)
        nc.finalize()
        return nc



def mvs(T, w=512):
    w = min(w, T)
    return [(i * w, w) for i in range(T // w)]


def setup_consts(p, T=None, big_pool=False, wkv_pool=False):
    if T is not None:
        p.shared("sh_wraw", (128, 128), BF16)
        p.shared("sh_wlhs", (128, 128), BF16)
        for j in range(3):
            p.shared(f"sh_wt{j}", (128, 128), BF16)
            if big_pool:
                p.shared(f"sh_ld{j}", (128, T), BF16)
        pass
    onesf = p.sbuf("c_onesf", (128, 128), F32)
    p.emit("dve", lambda e: e.memset(onesf.ap[:], 1.0), writes=(onesf,))
    p.onesf = onesf
    p.idn32 = p.sbuf("c_idn32", (128, 128), F32)
    p.emit("pool", lambda e: e.affine_select(
        p.idn32.ap[:], onesf.ap[:], pattern=[[-1, 128]],
        compare_op=ALU.is_equal, fill=0.0, base=0, channel_multiplier=1),
        reads=(onesf,), writes=(p.idn32,))
    p.idn = p.sbuf("c_idn", (128, 128), BF16)
    p.emit("dve", lambda e: e.tensor_copy(p.idn.ap[:], p.idn32.ap[:]),
           reads=(p.idn32,), writes=(p.idn,))
    p.ones2 = p.sbuf("c_ones2", (128, 2), BF16)
    p.emit("dve", lambda e: e.memset(p.ones2.ap[:], 0.0), writes=(p.ones2,))
    p.emit("dve", lambda e: e.memset(p.ones2.ap[0:64, 0:1], 1.0),
           writes=(p.ones2,))
    p.emit("dve", lambda e: e.memset(p.ones2.ap[64:128, 1:2], 1.0),
           writes=(p.ones2,))
    p.rep2 = p.sbuf("c_rep2", (2, 128), BF16)
    with p.scope() as sc:
        ps = p.psum("c_psr2", (2, 128), BF16, scope=sc)
        p.emit("pe", lambda e: e.transpose(ps.ap[:], p.ones2.ap[:],
                                           p.idn.ap[:]),
               reads=(p.ones2, p.idn), writes=(ps,))
        p.emit("act", lambda e: e.copy(p.rep2.ap[:], ps.ap[:]),
               reads=(ps,), writes=(p.rep2,))
    p.ones_bf = p.sbuf("c_ones_bf", (128, 128), BF16)
    p.emit("dve", lambda e: e.memset(p.ones_bf.ap[:], 1.0), writes=(p.ones_bf,))
    p.mask8 = p.sbuf("c_mask8", (128, 64), F32)
    p.emit("dve", lambda e: e.memset(p.mask8.ap[:], 0.0), writes=(p.mask8,))
    p.emit("pool", lambda e: e.affine_select(
        p.mask8.ap[:].rearrange("p (t j) -> p t j", t=8),
        p.mask8.ap[:].rearrange("p (t j) -> p t j", t=8),
        pattern=[[1, 8], [-1, 8]], compare_op=ALU.is_ge, fill=-1e9,
        base=-1, channel_multiplier=0),
        reads=(p.mask8,), writes=(p.mask8,))
    p.iota8 = p.sbuf("c_iota8", (128, 8), F32)
    p.emit("pool", lambda e: e.iota(p.iota8.ap[:], pattern=[[1, 8]], base=0,
                                    channel_multiplier=0,
                                    allow_small_or_imprecise_dtypes=True),
           writes=(p.iota8,))
    p.ones2f = p.sbuf("c_ones2f", (128, 2), F32)
    p.emit("dve", lambda e: e.tensor_copy(p.ones2f.ap[:], p.ones2.ap[:]),
           reads=(p.ones2,), writes=(p.ones2f,))
    p.rep2f = p.sbuf("c_rep2f", (2, 128), F32)
    p.emit("dve", lambda e: e.tensor_copy(p.rep2f.ap[:], p.rep2.ap[:]),
           reads=(p.rep2,), writes=(p.rep2f,))


def load_featvec(p, name, src_ap_1d, n, scope=None, dtype=F32, out=None,
                 outsl=None):
    nt = n // 128
    if out is None:
        out = p.sbuf(name + "_fv", (128, nt), dtype, scope=scope)
        outsl = (slice(None), slice(None))
    with p.scope() as sc:
        stage = p.sbuf(name + "_st", (nt, 128), F32, scope=sc)
        srcbuf = Buf(name + "_src", src_ap_1d)

        def f_ld(e):
            return e.dma_start(out=stage.ap[:],
                               in_=src_ap_1d.rearrange("(a b) -> a b", b=128))
        p.emit("sp", f_ld, reads=(srcbuf,), writes=(stage,), dma_queue="dsp")
        ps = p.psum(name + "_ps", (128, nt), F32, scope=sc)
        p.emit("pe", lambda e: e.transpose(ps.ap[:], stage.ap[:],
                                           p.idn32.ap[0:nt, 0:nt]),
               reads=(stage, p.idn32), writes=(ps,))
        p.emit("act", lambda e: e.copy(out.ap[outsl], ps.ap[:]),
               reads=(ps,), writes=(out,))
    return out


def dump_fm(p, tiles, T, name, dtype=F32):
    nt = len(tiles)
    out = p.dram_out(name, (nt * 128, T), dtype)
    for fi in range(nt):
        def f(e, fi=fi):
            return e.dma_start(out=out.ap[fi * 128:(fi + 1) * 128, :],
                               in_=tiles[fi].ap[:])
        p.emit("sp", f, reads=(tiles[fi],), writes=(out,), dma_queue="dsp")
    return out


def dump_tile(p, buf, name, shape, dtype=F32):
    out = p.dram_out(name, list(shape), dtype)
    p.emit("sp", lambda e: e.dma_start(out=out.ap[:], in_=buf.ap[:]),
           reads=(buf,), writes=(out,), dma_queue="dsp")
    return out



def b0_entry(p, T, x_in, xn_fm):
    with p.scope() as sc:
        stage = [p.sbuf(f"e_stage{j}", (128, C), F32, scope=sc)
                 for j in range(2)]
        xnt = [p.sbuf(f"e_xnt{j}", (128, C), BF16, scope=sc) for j in range(2)]
        ssum = p.sbuf("e_ssum", (128, 8), F32, scope=sc)
        ps_tr = [p.psum(f"e_pstr{j}", (128, 128), BF16, scope=sc)
                 for j in range(2)]
        for tt in range(T // 128):
            st = stage[tt % 2]
            xn_t = xnt[tt % 2]

            def f_ld(e, tt=tt, st=st):
                return e.dma_start(out=st.ap[:],
                                   in_=x_in.ap[tt * 128:(tt + 1) * 128, :])
            p.emit("sp", f_ld, reads=(x_in,), writes=(st,), dma_queue="dsp")

            def f_sq2(e, st=st, xn_t=xn_t):
                return e.activation(xn_t.ap[:], st.ap[:], AF.Square,
                                    accum_out=ssum.ap[:, 0:1])
            p.emit("act", f_sq2, reads=(st,), writes=(ssum,), war=(xn_t,))

            def f_sum(e, st=st):
                return e.tensor_reduce(ssum.ap[:, 1:2], st.ap[:], AX.X, ALU.add)
            p.emit("dve", f_sum, reads=(st,), writes=(ssum,))

            def f_mean(e):
                return e.tensor_scalar(ssum.ap[:, 2:3], ssum.ap[:, 1:2],
                                       1.0 / C, None, ALU.mult)
            p.emit("dve", f_mean, reads=(ssum,), writes=(ssum,))

            def f_msq(e):
                return e.tensor_tensor(ssum.ap[:, 3:4], ssum.ap[:, 2:3],
                                       ssum.ap[:, 2:3], ALU.mult)
            p.emit("dve", f_msq, reads=(ssum,), writes=(ssum,))

            def f_var(e):
                return e.scalar_tensor_tensor(ssum.ap[:, 4:5], ssum.ap[:, 0:1],
                                              1.0 / C, ssum.ap[:, 3:4],
                                              ALU.mult, ALU.subtract)
            p.emit("dve", f_var, reads=(ssum,), writes=(ssum,))

            def f_vep(e):
                return e.tensor_scalar(ssum.ap[:, 5:6], ssum.ap[:, 4:5],
                                       1e-5, None, ALU.add)
            p.emit("dve", f_vep, reads=(ssum,), writes=(ssum,))

            def f_rec(e):
                return e.reciprocal(ssum.ap[:, 6:7], ssum.ap[:, 5:6])
            p.emit("dve", f_rec, reads=(ssum,), writes=(ssum,))

            def f_rstd(e):
                return e.activation(ssum.ap[:, 7:8], ssum.ap[:, 6:7], AF.Sqrt)
            p.emit("act", f_rstd, reads=(ssum,), writes=(ssum,))

            def f_xn(e, st=st, xn_t=xn_t):
                return e.tensor_scalar(xn_t.ap[:], st.ap[:], ssum.ap[:, 2:3],
                                       ssum.ap[:, 7:8], ALU.subtract, ALU.mult)
            p.emit("dve", f_xn, reads=(st, ssum), writes=(xn_t,))

            for fi in range(NF):
                pst = ps_tr[fi % 2]

                def f_t1(e, fi=fi, xn_t=xn_t, pst=pst):
                    return e.transpose(pst.ap[:],
                                       xn_t.ap[:, fi * 128:(fi + 1) * 128],
                                       p.idn.ap[:])
                p.emit("pe", f_t1, reads=(xn_t, p.idn), writes=(pst,))

                def f_e1(e, fi=fi, tt=tt, pst=pst):
                    return e.copy(xn_fm[fi].ap[:, tt * 128:(tt + 1) * 128],
                                  pst.ap[:])
                p.emit("act", f_e1, reads=(pst,), writes=(xn_fm[fi],))



def b0_mixer(p, T, xn_fm, W, maa, dbg, xk_fm, xv_fm, xr_fm):
    with p.scope() as sc:
        xx_t = p.sbuf("xx_t", (128, T), BF16, scope=sc)

        def build_xx(fi):
            xnf = xn_fm[fi]

            def f_xx(e, xnf=xnf):
                return e.tensor_tensor(xx_t.ap[:, 1:T], xnf.ap[:, 0:T - 1],
                                       xnf.ap[:, 1:T], ALU.subtract)
            p.emit("dve", f_xx, reads=(xnf,), writes=(xx_t,))

            def f_xx0(e, xnf=xnf):
                return e.tensor_scalar(xx_t.ap[:, 0:1], xnf.ap[:, 0:1],
                                       -1.0, None, ALU.mult)
            p.emit("dve", f_xx0, reads=(xnf,), writes=(xx_t,))
        w1sb = p.sbuf("w1sb", (128, NF, 96), BF16, scope=sc)
        p.emit("pool", lambda e: e.dma_start(
            out=w1sb.ap[:],
            in_=W["w1"].ap.rearrange("(a p) m -> p a m", p=128)),
            reads=(W["w1"],), writes=(w1sb,), dma_queue="dpool")
        w2sb = p.sbuf("w2sb", (96, C), BF16, scope=sc)
        p.emit("pool", lambda e: e.dma_start(
            out=w2sb.ap[:], in_=W["w2"].ap.rearrange("a b c -> (a b) c")),
            reads=(W["w2"],), writes=(w2sb,), dma_queue="dpool")

        ps_m = p.psum("ps_m", (96, T), F32, scope=sc)
        xxx = p.sbuf("xxx_t", (128, T), BF16, scope=sc)
        for fi in range(NF):
            xnf = xn_fm[fi]
            build_xx(fi)

            def f_xxx(e, fi=fi, xnf=xnf):
                return e.scalar_tensor_tensor(
                    xxx.ap[:], xx_t.ap[:], maa["maa_x"].ap[:, fi:fi + 1],
                    xnf.ap[:], ALU.mult, ALU.add)
            p.emit("dve", f_xxx, reads=(xx_t, xnf, maa["maa_x"]),
                   writes=(xxx,))

            sl = mvs(T)
            for mi, (m0, mw) in enumerate(sl):
                def f_mm(e, fi=fi, m0=m0, mw=mw):
                    return e.matmul(ps_m.ap[:, m0:m0 + mw], w1sb.ap[:, fi, :],
                                    xxx.ap[:, m0:m0 + mw],
                                    start=(fi == 0), stop=(fi == NF - 1))
                last = (mi == len(sl) - 1)
                p.emit("pe", f_mm, reads=(w1sb, xxx),
                       writes=(ps_m,) if (fi == NF - 1 and last) else (),
                       war=(ps_m,) if (fi == 0 and mi == 0) else (), inc=last)

        m_tm = p.sbuf("m_tm", (96, T), BF16, scope=sc)
        p.emit("act", lambda e: e.activation(m_tm.ap[:], ps_m.ap[:], AF.Tanh),
               reads=(ps_m,), writes=(m_tm,))

        ps_mk = p.psum("ps_mk", (128, T), F32, scope=sc)
        tmp1 = p.sbuf("mix_tmp1", (128, T), F32, scope=sc)
        branch = [("maa_k", xk_fm), ("maa_v", xv_fm), ("maa_r", xr_fm)]
        for l in range(3):
            maa_nm, dst = branch[l]
            for fi in range(NF):
                build_xx(fi)
                sl = mvs(T)
                for mi, (m0, mw) in enumerate(sl):
                    def f_mm(e, l=l, fi=fi, m0=m0, mw=mw):
                        return e.matmul(
                            ps_mk.ap[:, m0:m0 + mw],
                            w2sb.ap[l * 32:(l + 1) * 32,
                                    fi * 128:(fi + 1) * 128],
                            m_tm.ap[l * 32:(l + 1) * 32, m0:m0 + mw],
                            start=True, stop=True)
                    p.emit("pe", f_mm, reads=(w2sb, m_tm),
                           writes=(ps_mk,) if mi == len(sl) - 1 else (),
                           war=(ps_mk,) if mi == 0 else (),
                           inc=(mi == len(sl) - 1))

                def f_t1(e, maa_nm=maa_nm, fi=fi):
                    return e.tensor_scalar(tmp1.ap[:], ps_mk.ap[:],
                                           maa[maa_nm].ap[:, fi:fi + 1],
                                           None, ALU.add)
                p.emit("dve", f_t1, reads=(ps_mk, maa[maa_nm]), writes=(tmp1,))

                def f_t2(e, fi=fi):
                    return e.tensor_tensor(tmp1.ap[:], tmp1.ap[:],
                                           xx_t.ap[:], ALU.mult)
                p.emit("dve", f_t2, reads=(tmp1, xx_t), writes=(tmp1,))

                def f_t3(e, fi=fi, dst=dst):
                    return e.tensor_tensor(dst[fi].ap[:], tmp1.ap[:],
                                           xn_fm[fi].ap[:], ALU.add)
                p.emit("dve", f_t3, reads=(tmp1, xn_fm[fi]), writes=(dst[fi],))
    return xk_fm, xv_fm, xr_fm



def wkv_hp(p, T, hp, r, k, v, ew, u_fm, o_hp):
    LCH = 32
    nch = T // LCH

    with p.scope() as sc0:
        Lpad = p.sbuf("wk_Lpad", (128, T + 1), F32, scope=sc0)
        negone = p.sbuf("wk_neg1", (128, 1), F32, scope=sc0)
        p.emit("dve", lambda e: e.memset(negone.ap[:], -1.0), writes=(negone,))
        p.emit("dve", lambda e: e.memset(Lpad.ap[:, 0:1], 0.0), writes=(Lpad,))

        def f_scan(e):
            return e.tensor_tensor_scan(
                Lpad.ap[:, 1:T + 1], ew.ap[:],
                negone.ap[:].broadcast_to((128, T)), 0.0,
                ALU.subtract, ALU.mult)
        p.emit("dve", f_scan, reads=(ew, negone), writes=(Lpad,))

        R32 = p.sbuf("wk_R32", (128, T), BF16, scope=sc0)
        K32 = p.sbuf("wk_K32", (128, T), BF16, scope=sc0)
        R16 = p.sbuf("wk_R16", (128, T), BF16, scope=sc0)
        K16 = p.sbuf("wk_K16", (128, T), BF16, scope=sc0)
        scdk = p.scope()
        scdk_es = scdk.__enter__()
        dtmp = p.sbuf("wk_dtmp", (128, T), F32, scope=scdk_es)

        def decay_build(out, base_buf, kind, blk):
            nb = T // blk
            if kind == "r":
                i0 = Lpad.ap[:, 0:T].rearrange("p (b t) -> p b t", t=blk)
                i1 = Lpad.ap[:, 0:T].rearrange(
                    "p (b t) -> p b t", t=blk)[:, :, 0:1].broadcast_to(
                    (128, nb, blk))
            else:
                i0 = Lpad.ap[:, 1:T + 1].rearrange(
                    "p (b t) -> p b t", t=blk)[:, :, blk - 1:blk].broadcast_to(
                    (128, nb, blk))
                i1 = Lpad.ap[:, 1:T + 1].rearrange("p (b t) -> p b t", t=blk)

            def f_sub(e):
                return e.tensor_tensor(
                    dtmp.ap[:].rearrange("p (b t) -> p b t", t=blk), i0, i1,
                    ALU.subtract)
            p.emit("dve", f_sub, reads=(Lpad,), writes=(dtmp,))
            p.emit("act", lambda e: e.activation(dtmp.ap[:], dtmp.ap[:],
                                                 AF.Exp),
                   reads=(dtmp,), writes=(dtmp,))

            def f_mul(e):
                return e.tensor_tensor(out.ap[:], dtmp.ap[:], base_buf.ap[:],
                                       ALU.mult)
            p.emit("dve", f_mul, reads=(dtmp, base_buf), writes=(out,))
            return out

        decay_build(R32, r, "r", 32)
        decay_build(K32, k, "k", 32)
        decay_build(R16, r, "r", 16)
        decay_build(K16, k, "k", 16)
        scdk.__exit__(None, None, None)

        sdec = p.sbuf("wk_sdec", (128, nch), F32, scope=sc0)

        def f_sd(e):
            i0 = Lpad.ap[:, 1:T + 1].rearrange(
                "p (b t) -> p b t", t=LCH)[:, :, LCH - 1:LCH]
            i1 = Lpad.ap[:, 0:T].rearrange(
                "p (b t) -> p b t", t=LCH)[:, :, 0:1]
            return e.tensor_tensor(sdec.ap[:].unsqueeze(2), i0, i1,
                                   ALU.subtract)
        p.emit("dve", f_sd, reads=(Lpad,), writes=(sdec,))
        p.emit("act", lambda e: e.activation(sdec.ap[:], sdec.ap[:], AF.Exp),
               reads=(sdec,), writes=(sdec,))

        if getattr(p, "wkv_stop", 99) <= 0:
            def f_dbg(e):
                return e.tensor_copy(o_hp.ap[:], R32.ap[:])
            p.emit("dve", f_dbg, reads=(R32,), writes=(o_hp,))
            return

        vT = p.sbuf("wk_vT", (128, T), BF16, scope=sc0)
        K32T = p.sbuf("wk_K32T", (128, T), BF16, scope=sc0)
        with p.scope() as sctr:
            ps_tr = [p.psum(f"wk_pstr{j}", (128, 128), BF16, scope=sctr)
                     for j in range(2)]
            i = 0
            for bi in range(T // 128):
                for src, dst in ((v, vT), (K32, K32T)):
                    pst = ps_tr[i % 2]
                    i += 1

                    def f_tr(e, bi=bi, src=src, pst=pst):
                        return e.transpose(pst.ap[:],
                                           src.ap[:, bi * 128:(bi + 1) * 128],
                                           p.idn.ap[:])
                    p.emit("pe", f_tr, reads=(src, p.idn), writes=(pst,))

                    def f_ev(e, bi=bi, dst=dst, pst=pst):
                        return e.copy(dst.ap[:, bi * 128:(bi + 1) * 128],
                                      pst.ap[:])
                    p.emit("act", f_ev, reads=(pst,), writes=(dst,))

        if getattr(p, "wkv_stop", 99) <= 1:
            def f_dbg(e):
                return e.tensor_copy(o_hp.ap[:], vT.ap[:])
            p.emit("dve", f_dbg, reads=(vT,), writes=(o_hp,))
            return

        pc = p.sbuf("wk_pc", (128, T), F32, scope=sc0)
        with p.scope() as scdu:
            ruk = p.sbuf("wk_ruk", (128, T), BF16, scope=scdu)

            def f_ru(e):
                return e.tensor_scalar(ruk.ap[:], r.ap[:],
                                       u_fm.ap[:, hp:hp + 1], None, ALU.mult)
            p.emit("dve", f_ru, reads=(r, u_fm), writes=(ruk,))
            p.emit("dve", lambda e: e.tensor_tensor(ruk.ap[:], ruk.ap[:],
                                                    k.ap[:], ALU.mult),
                   reads=(ruk, k), writes=(ruk,))

            ps_du = p.psum("wk_psdu", (2, T), F32, scope=scdu)
            sl = mvs(T)
            for mi, (m0, mw) in enumerate(sl):
                def f_mm(e, m0=m0, mw=mw):
                    return e.matmul(ps_du.ap[:, m0:m0 + mw], p.ones2.ap[:],
                                    ruk.ap[:, m0:m0 + mw],
                                    start=True, stop=True)
                p.emit("pe", f_mm, reads=(p.ones2, ruk),
                       writes=(ps_du,) if mi == len(sl) - 1 else (),
                       war=(ps_du,) if mi == 0 else (), inc=(mi == len(sl) - 1))
            du_sb = p.sbuf("wk_du", (2, T), BF16, scope=scdu)
            p.emit("act", lambda e: e.copy(du_sb.ap[:], ps_du.ap[:]),
                   reads=(ps_du,), writes=(du_sb,))

            ps_duB = p.psum("wk_psduB", (128, T), F32, scope=scdu)
            for mi, (m0, mw) in enumerate(sl):
                def f_mm(e, m0=m0, mw=mw):
                    return e.matmul(ps_duB.ap[:, m0:m0 + mw], p.rep2.ap[:],
                                    du_sb.ap[:, m0:m0 + mw],
                                    start=True, stop=True)
                p.emit("pe", f_mm, reads=(p.rep2, du_sb),
                       writes=(ps_duB,) if mi == len(sl) - 1 else (),
                       war=(ps_duB,) if mi == 0 else (),
                       inc=(mi == len(sl) - 1))
            p.emit("dve", lambda e: e.tensor_tensor(pc.ap[:], ps_duB.ap[:],
                                                    v.ap[:], ALU.mult),
                   reads=(ps_duB, v), writes=(pc,))

        if getattr(p, "wkv_stop", 99) <= 2:
            def f_dbg(e):
                return e.tensor_copy(o_hp.ap[:], pc.ap[:])
            p.emit("dve", f_dbg, reads=(pc,), writes=(o_hp,))
            return

        SEG = min(256, T)
        QCH = 1024
        with p.scope() as scpw:
            nbA = SEG // 8
            P0 = p.wkv_bufs["P0"]
            E = p.wkv_bufs["E"]
            RK = p.wkv_bufs["RK"]
            OIt = p.wkv_bufs["OIt"]
            OIr = p.wkv_bufs["OIr"]
            ps_M = p.psum("wk_psM", (2, QCH), F32, scope=scpw)
            ps_R = p.psum("wk_psR", (128, QCH), F32, scope=scpw)
            M_sb = p.wkv_bufs["Msb"]

            def pairwise(s0, nb, tb, span, off_t, mask):
                npair = nb * tb * tb

                def tview(ap_src, off, bcast_ax):
                    if off + tb <= span:
                        a = ap_src[:, s0: s0 + span * nb].rearrange(
                            "p (b x) -> p b x", x=span)[:, :, off:off + tb]
                    else:
                        a = ap_src[:, s0 + off: s0 + off + span * nb].rearrange(
                            "p (b x) -> p b x", x=span)[:, :, 0:tb]
                    return a.unsqueeze(bcast_ax).broadcast_to(
                        (128, nb, tb, tb))

                def f_sub(e):
                    return e.tensor_tensor(
                        P0.ap[:, 0:npair].rearrange(
                            "p (b t j) -> p b t j", t=tb, j=tb),
                        tview(Lpad.ap, off_t, 3), tview(Lpad.ap, 1, 2),
                        ALU.subtract)
                p.emit("dve", f_sub, reads=(Lpad,), writes=(P0,))

                if mask:
                    def f_mask(e):
                        mb = p.mask8.ap[:].rearrange(
                            "p (t j) -> p t j", t=tb).unsqueeze(
                            1).broadcast_to((128, nb, tb, tb))
                        return e.tensor_tensor(
                            P0.ap[:, 0:npair].rearrange(
                                "p (b t j) -> p b t j", t=tb, j=tb),
                            P0.ap[:, 0:npair].rearrange(
                                "p (b t j) -> p b t j", t=tb, j=tb),
                            mb, ALU.add)
                    p.emit("dve", f_mask, reads=(P0, p.mask8), writes=(P0,))

                p.emit("act", lambda e: e.activation(
                    E.ap[:, 0:npair], P0.ap[:, 0:npair], AF.Exp),
                    reads=(P0,), writes=(E,))

                def f_rk(e):
                    return e.tensor_tensor(
                        RK.ap[:, 0:npair].rearrange(
                            "p (b t j) -> p b t j", t=tb, j=tb),
                        tview(r.ap, off_t, 3), tview(k.ap, 0, 2), ALU.mult)
                p.emit("dve", f_rk, reads=(r, k), writes=(RK,))

                def f_p(e):
                    return e.tensor_tensor(E.ap[:, 0:npair], E.ap[:, 0:npair],
                                           RK.ap[:, 0:npair], ALU.mult)
                p.emit("dve", f_p, reads=(E, RK), writes=(E,))

                for q0 in range(0, npair, QCH):
                    qw = min(QCH, npair - q0)
                    sl = mvs(qw)
                    for mi, (m0, mw) in enumerate(sl):
                        def f_mm(e, q0=q0, m0=m0, mw=mw):
                            return e.matmul(ps_M.ap[:, m0:m0 + mw],
                                            p.ones2f.ap[:],
                                            E.ap[:, q0 + m0:q0 + m0 + mw],
                                            start=True, stop=True)
                        p.emit("pe", f_mm, reads=(p.ones2f, E),
                               writes=(ps_M,) if mi == len(sl) - 1 else (),
                               war=(ps_M,) if mi == 0 else (),
                               inc=(mi == len(sl) - 1))

                    def f_mev(e, qw=qw):
                        return e.copy(M_sb.ap[:, 0:qw], ps_M.ap[:, 0:qw])
                    p.emit("act", f_mev, reads=(ps_M,), writes=(M_sb,))

                    for mi, (m0, mw) in enumerate(sl):
                        def f_mm(e, m0=m0, mw=mw):
                            return e.matmul(ps_R.ap[:, m0:m0 + mw],
                                            p.rep2f.ap[:],
                                            M_sb.ap[:, m0:m0 + mw],
                                            start=True, stop=True)
                        p.emit("pe", f_mm, reads=(p.rep2f, M_sb),
                               writes=(ps_R,) if mi == len(sl) - 1 else (),
                               war=(ps_R,) if mi == 0 else (),
                               inc=(mi == len(sl) - 1))

                    nbq = qw // (tb * tb)
                    b0 = q0 // (tb * tb)

                    def f_oim(e, qw=qw, nbq=nbq, b0=b0):
                        vb = v.ap[:, s0 + b0 * span:
                                  s0 + b0 * span + span * nbq].rearrange(
                            "p (b x) -> p b x", x=span)[:, :, 0:tb].unsqueeze(
                            2).broadcast_to((128, nbq, tb, tb))
                        return e.tensor_tensor(
                            OIt.ap[:, 0:qw].rearrange(
                                "p (b t j) -> p b t j", t=tb, j=tb),
                            ps_R.ap[:, 0:qw].rearrange(
                                "p (b t j) -> p b t j", t=tb, j=tb),
                            vb, ALU.mult)
                    p.emit("dve", f_oim, reads=(ps_R, v), writes=(OIt,))

                    def f_oir(e, qw=qw, nbq=nbq, b0=b0):
                        return e.tensor_reduce(
                            OIr.ap[:, b0 * tb: (b0 + nbq) * tb].rearrange(
                                "p (b t) -> p b t", t=tb).unsqueeze(3),
                            OIt.ap[:, 0:qw].rearrange(
                                "p (b t j) -> p b t j", t=tb, j=tb),
                            AX.X, ALU.add)
                    p.emit("dve", f_oir, reads=(OIt,), writes=(OIr,))

                def f_acc(e):
                    if off_t + tb <= span:
                        dst = pc.ap[:, s0: s0 + span * nb].rearrange(
                            "p (b x) -> p b x", x=span)[:, :, off_t:off_t + tb]
                    else:
                        dst = pc.ap[:, s0 + off_t: s0 + off_t +
                                    span * nb].rearrange(
                            "p (b x) -> p b x", x=span)[:, :, 0:tb]
                    return e.tensor_tensor(
                        dst, dst, OIr.ap[:, 0:nb * tb].rearrange(
                            "p (b t) -> p b t", t=tb), ALU.add)
                p.emit("dve", f_acc, reads=(pc, OIr), writes=(pc,))

            for sg in range(T // SEG):
                s0 = sg * SEG
                pairwise(s0, SEG // 8, 8, 8, 0, True)
                pairwise(s0, SEG // 16, 8, 16, 8, False)

        if getattr(p, "wkv_dbg", False) and hp == 0:
            dump_tile(p, Lpad, "dbgw_Lpad", (128, T + 1), F32)
            dump_tile(p, R32, "dbgw_R32", (128, T), BF16)
            dump_tile(p, K32, "dbgw_K32", (128, T), BF16)
            dump_tile(p, R16, "dbgw_R16", (128, T), BF16)
            dump_tile(p, K16, "dbgw_K16", (128, T), BF16)
            dump_tile(p, sdec, "dbgw_sdec", (128, T // LCH), F32)
            dump_tile(p, pc, "dbgw_pc", (128, T), F32)

        if getattr(p, "wkv_stop", 99) <= 3:
            def f_dbg(e):
                return e.tensor_copy(o_hp.ap[:], pc.ap[:])
            p.emit("dve", f_dbg, reads=(pc,), writes=(o_hp,))
            return

        with p.scope() as scch:
            S_sb = p.sbuf("wk_S", (128, 128), F32, scope=scch)
            p.emit("dve", lambda e: e.memset(S_sb.ap[:], 0.0), writes=(S_sb,))
            Sbf = p.sbuf("wk_Sbf", (128, 128), BF16, scope=scch)
            ps_o = [[p.psum(f"wk_pso{j}h{h}", (128, LCH), F32, scope=scch)
                     for h in range(2)] for j in range(2)]
            ps_S = p.psum("wk_psS", (128, 128), F32, scope=scch)
            M16 = p.sbuf("wk_M16", (128, 32), BF16, scope=scch)
            ps_M16h = [p.psum(f"wk_psM16{j}", (16, 16), F32, scope=scch)
                       for j in range(2)]

            parts = getattr(p, "wkv_parts", {"inter", "cross", "supd"})
            for c in range(nch):
                t0 = c * LCH
                po = ps_o[c % 2]
                rbase = 32 * (c % 4)
                cbase = 128 * (c // 4)

                def f_sbf(e):
                    return e.tensor_copy(Sbf.ap[:], S_sb.ap[:])
                p.emit("dve", f_sbf, reads=(S_sb,), writes=(Sbf,))

                for h in range(2):
                    def f_mm(e, h=h, t0=t0, po=po):
                        return e.matmul(
                            po[h].ap[64 * h:64 * h + 64, :],
                            Sbf.ap[64 * h:64 * h + 64, 64 * h:64 * h + 64],
                            R32.ap[64 * h:64 * h + 64, t0:t0 + LCH],
                            start=True, stop=("cross" not in parts),
                            skip_group_check=True)
                    p.emit("pe", f_mm, reads=(Sbf, R32),
                           war=(po[h],), writes=(po[h],) if
                           "cross" not in parts else (),
                           inc=("cross" not in parts))

                if "cross" in parts or "crossm" in parts:
                  for h in range(2):
                    def f_mm(e, h=h, t0=t0):
                        return e.matmul(
                            ps_M16h[h].ap[:],
                            K16.ap[64 * h:64 * h + 64, t0:t0 + 16],
                            R16.ap[64 * h:64 * h + 64, t0 + 16:t0 + 32],
                            start=True, stop=True,
                            skip_group_check=True)
                    p.emit("pe", f_mm, reads=(K16, R16),
                           writes=(ps_M16h[h],), war=(ps_M16h[h],))

                if "cross" in parts or "crossm" in parts:
                  for h in range(2):
                    def f_m16ev(e, rbase=rbase, h=h):
                        return e.copy(
                            M16.ap[rbase:rbase + 16, 16 * h:16 * h + 16],
                            ps_M16h[h].ap[:])
                    p.emit("act", f_m16ev, reads=(ps_M16h[h],), writes=(M16,))

                  for h in (range(2) if "cross" in parts else ()):
                    def f_mm(e, h=h, rbase=rbase, cbase=cbase, po=po):
                        return e.matmul(
                            po[h].ap[64 * h:64 * h + 64, 16:32],
                            vT.ap[rbase:rbase + 16,
                                  cbase + 64 * h:cbase + 64 * h + 64],
                            M16.ap[rbase:rbase + 16, 16 * h:16 * h + 16],
                            start=False, stop=True,
                            tile_position=((96, 64 * h) if rbase == 96
                                           else None),
                            skip_group_check=True)
                    p.emit("pe", f_mm, reads=(vT, M16), writes=(po[h],))

                for h in range(2):
                    def f_oev(e, h=h, t0=t0, po=po):
                        return e.scalar_tensor_tensor(
                            o_hp.ap[64 * h:64 * h + 64, t0:t0 + LCH],
                            po[h].ap[64 * h:64 * h + 64, :], 1.0,
                            pc.ap[64 * h:64 * h + 64, t0:t0 + LCH],
                            ALU.mult, ALU.add)
                    p.emit("dve", f_oev, reads=(po[h], pc), writes=(o_hp,))

                if "supd" in parts:
                  def f_smm(e, rbase=rbase, cbase=cbase):
                    return e.matmul(ps_S.ap[:],
                                    K32T.ap[rbase:rbase + 32,
                                            cbase:cbase + 128],
                                    vT.ap[rbase:rbase + 32, cbase:cbase + 128],
                                    start=True, stop=True,
                                    tile_position=((96, 0) if rbase == 96
                                                   else None),
                                    skip_group_check=True)
                  p.emit("pe", f_smm, reads=(K32T, vT), writes=(ps_S,),
                         war=(ps_S,))

                  def f_supd(e, c=c):
                    return e.scalar_tensor_tensor(
                        S_sb.ap[:], S_sb.ap[:], sdec.ap[:, c:c + 1],
                        ps_S.ap[:], ALU.mult, ALU.add)
                  p.emit("dve", f_supd, reads=(S_sb, sdec, ps_S),
                         writes=(S_sb,))
                if getattr(p, "wkv_dbg", False) and hp == 0 and c < 2:
                    dump_tile(p, S_sb, f"dbgw_S{c}", (128, 128), F32)



def b0_projections(p, T, xk_fm, xv_fm, xr_fm, W, u_fm, o_hbm, ffn_hbm, dbg,
                   do_wkv=True):
    with p.scope() as sc:
        r_hp = p.sbuf("pj_r", (128, T), BF16, scope=sc)
        k_hp = p.sbuf("pj_k", (128, T), BF16, scope=sc)
        v_hp = p.sbuf("pj_v", (128, T), BF16, scope=sc)
        ew_hp = p.sbuf("pj_ew", (128, T), F32, scope=sc)
        sg_hp = p.sbuf("pj_sg", (128, T), BF16, scope=sc)
        o_hp = p.sbuf("pj_o", (128, T), BF16, scope=sc)
        wti = [0]

        def proj(dst_ps, w_hbm, col0, x_tiles, wblk):
            tl = wblk[wti[0] % 2]
            wti[0] += 1

            def f_ld(e, tl=tl, col0=col0, w_hbm=w_hbm):
                return e.dma_start(
                    out=tl.ap[:],
                    in_=w_hbm.ap[:, col0:col0 + 128].rearrange(
                        "(c p) m -> p c m", p=128))
            p.emit("pool", f_ld, reads=(w_hbm,), writes=(tl,),
                   dma_queue="dpool")
            for kc in range(NF):
                sl = mvs(T)
                for mi, (m0, mw) in enumerate(sl):
                    def f_mm(e, kc=kc, tl=tl, m0=m0, mw=mw):
                        return e.matmul(dst_ps.ap[:, m0:m0 + mw],
                                        tl.ap[:, kc, :],
                                        x_tiles[kc].ap[:, m0:m0 + mw],
                                        start=(kc == 0), stop=(kc == NF - 1))
                    last = (kc == NF - 1 and mi == len(sl) - 1)
                    p.emit("pe", f_mm, reads=(tl, x_tiles[kc]),
                           writes=(dst_ps,) if last else (),
                           war=(dst_ps,) if (kc == 0 and mi == 0) else (),
                           inc=(mi == len(sl) - 1))

        for oc in range(NF):
            with p.scope() as socp:
                ps4 = p.psum("pj_ps4", (128, T), F32, scope=socp)
                ps4b = p.psum("pj_ps4b", (128, T), F32, scope=socp)
                wblk = [p.sbuf(f"pj_wb{j}", (128, NF, 128), BF16, scope=socp)
                        for j in range(2)]
                proj(ps4, W["Wr"], oc * 128, xr_fm, wblk)
                p.emit("act", lambda e: e.copy(r_hp.ap[:], ps4.ap[:]),
                       reads=(ps4,), writes=(r_hp,))

                proj(ps4b, W["Wk"], oc * 128, xk_fm, wblk)
                p.emit("act", lambda e: e.activation(ew_hp.ap[:], ps4b.ap[:],
                                                     AF.Exp),
                       reads=(ps4b,), writes=(ew_hp,))
                p.emit("act", lambda e: e.activation(k_hp.ap[:], ew_hp.ap[:],
                                                     AF.Exp, scale=-1.0),
                       reads=(ew_hp,), writes=(k_hp,))
                p.emit("dve", lambda e: e.tensor_scalar(
                    k_hp.ap[:], k_hp.ap[:], -1.0, 1.0, ALU.mult, ALU.add),
                    reads=(k_hp,), writes=(k_hp,))

                proj(ps4, W["Wvg"], (24 + oc) * 128, xv_fm, wblk)
                p.emit("act", lambda e: e.activation(sg_hp.ap[:], ps4.ap[:],
                                                     AF.Silu),
                       reads=(ps4,), writes=(sg_hp,))
                proj(ps4b, W["Wvg"], oc * 128, xv_fm, wblk)
                p.emit("dve", lambda e: e.tensor_tensor(
                    v_hp.ap[:], ps4b.ap[:], sg_hp.ap[:], ALU.mult),
                    reads=(ps4b, sg_hp), writes=(v_hp,))

            if "rkev" in dbg:
                dump_tile(p, r_hp, f"dbg_r{oc}", (128, T), BF16)
                dump_tile(p, k_hp, f"dbg_k{oc}", (128, T), BF16)
                dump_tile(p, v_hp, f"dbg_v{oc}", (128, T), BF16)
                dump_tile(p, ew_hp, f"dbg_ew{oc}", (128, T), F32)

            if do_wkv:
                wkv_hp(p, T, oc, r_hp, k_hp, v_hp, ew_hp, u_fm, o_hp)

                def f_so(e, oc=oc):
                    return e.dma_start(
                        out=o_hbm.ap[oc * 128:(oc + 1) * 128, :],
                        in_=o_hp.ap[:])
                p.emit("sp", f_so, reads=(o_hp,), writes=(o_hbm,),
                       dma_queue="dsp")

        for oc2 in range(8, 24):
            with p.scope() as socp:
                ps4 = p.psum("pj_ps4f", (128, T), F32, scope=socp)
                ps4b = p.psum("pj_ps4fb", (128, T), F32, scope=socp)
                wblk = [p.sbuf(f"pj_wbf{j}", (128, NF, 128), BF16,
                               scope=socp) for j in range(2)]
                proj(ps4, W["Wvg"], (24 + oc2) * 128, xv_fm, wblk)
                p.emit("act", lambda e: e.activation(sg_hp.ap[:], ps4.ap[:],
                                                     AF.Silu),
                       reads=(ps4,), writes=(sg_hp,))
                proj(ps4b, W["Wvg"], oc2 * 128, xv_fm, wblk)
                p.emit("dve", lambda e: e.tensor_tensor(
                    v_hp.ap[:], ps4b.ap[:], sg_hp.ap[:], ALU.mult),
                    reads=(ps4b, sg_hp), writes=(v_hp,))

            def f_st(e, oc2=oc2):
                return e.dma_start(
                    out=ffn_hbm.ap[(oc2 - 8) * 128:(oc2 - 7) * 128, :],
                    in_=v_hp.ap[:])
            p.emit("sp", f_st, reads=(v_hp,), writes=(ffn_hbm,),
                   dma_queue="dsp")



def b0_output(p, T, xn_hbm, o_hbm, ffn_hbm, W, x_in, x1_hbm, dbg, x1t_hbm=None):
    lng = load_featvec(p, "lng", W["ln_g"].ap, C)
    lnb = load_featvec(p, "lnb", W["ln_b"].ap, C)

    s_hbm = (p.dram_out("dbg_s0", (C, T), BF16) if "s0" in dbg
             else p.dram_tmp("s_hbm", (C, T), BF16))

    scow = p.scope()
    scow_es = scow.__enter__()
    oln_fm = [p.sbuf(f"oln{i}", (128, T), BF16, scope=scow_es)
              for i in range(NF)]
    with p.scope() as sc:
        o_t = p.sbuf("ol_o", (128, T), BF16, scope=sc)
        osq = p.sbuf("ol_osq", (128, T), BF16, scope=sc)
        strow = p.sbuf("ol_st", (1, T, 6), F32, scope=sc)
        scps = p.scope()
        scps_es = scps.__enter__()
        ps_s1 = p.psum("ol_ps1", (1, T), F32, scope=scps_es)
        ps_s2 = p.psum("ol_ps2", (1, T), F32, scope=scps_es)
        o_tiles = []
        for hp in range(NF):
            ot = p.sbuf(f"ol_ot{hp}", (128, T), BF16, scope=sc)
            o_tiles.append(ot)

            def f_ld(e, hp=hp, ot=ot):
                return e.dma_start(out=ot.ap[:],
                                   in_=o_hbm.ap[hp * 128:(hp + 1) * 128, :])
            p.emit("sp", f_ld, reads=(o_hbm,), writes=(ot,), dma_queue="dsp")

            def f_sq(e, ot=ot):
                return e.activation(osq.ap[:], ot.ap[:], AF.Square)
            p.emit("act", f_sq, reads=(ot,), writes=(osq,))

            sl = mvs(T)
            for mi, (m0, mw) in enumerate(sl):
                def f_mm(e, ot=ot, m0=m0, mw=mw, hp=hp):
                    return e.matmul(ps_s1.ap[:, m0:m0 + mw], p.ones_bf.ap[:, 0:1],
                                    ot.ap[:, m0:m0 + mw],
                                    start=(hp == 0), stop=(hp == NF - 1))
                p.emit("pe", f_mm, reads=(ot, p.ones_bf),
                       writes=(ps_s1,) if hp == NF - 1 and mi == len(sl) - 1
                       else (),
                       war=(ps_s1,) if hp == 0 and mi == 0 else (),
                       inc=(mi == len(sl) - 1))

                def f_mm2(e, m0=m0, mw=mw, hp=hp):
                    return e.matmul(ps_s2.ap[:, m0:m0 + mw], p.ones_bf.ap[:, 0:1],
                                    osq.ap[:, m0:m0 + mw],
                                    start=(hp == 0), stop=(hp == NF - 1))
                p.emit("pe", f_mm2, reads=(osq, p.ones_bf),
                       writes=(ps_s2,) if hp == NF - 1 and mi == len(sl) - 1
                       else (),
                       war=(ps_s2,) if hp == 0 and mi == 0 else (),
                       inc=(mi == len(sl) - 1))

        def f_mu(e):
            return e.tensor_scalar(strow.ap[:, :, 0], ps_s1.ap[:], 1.0 / C,
                                   None, ALU.mult)
        p.emit("dve", f_mu, reads=(ps_s1,), writes=(strow,))

        def f_m2(e):
            return e.tensor_tensor(strow.ap[:, :, 1], strow.ap[:, :, 0],
                                   strow.ap[:, :, 0], ALU.mult)
        p.emit("dve", f_m2, reads=(strow,), writes=(strow,))

        def f_var(e):
            return e.scalar_tensor_tensor(strow.ap[:, :, 2], ps_s2.ap[:],
                                          1.0 / C, strow.ap[:, :, 1],
                                          ALU.mult, ALU.subtract)
        p.emit("dve", f_var, reads=(ps_s2, strow), writes=(strow,))

        def f_vep(e):
            return e.tensor_scalar(strow.ap[:, :, 2], strow.ap[:, :, 2],
                                   1e-5, None, ALU.add)
        p.emit("dve", f_vep, reads=(strow,), writes=(strow,))

        def f_rc(e):
            return e.reciprocal(strow.ap[:, :, 3], strow.ap[:, :, 2])
        p.emit("dve", f_rc, reads=(strow,), writes=(strow,))

        def f_sq2(e):
            return e.activation(strow.ap[:, :, 4], strow.ap[:, :, 3], AF.Sqrt)
        p.emit("act", f_sq2, reads=(strow,), writes=(strow,))

        def f_nb(e):
            return e.tensor_tensor(strow.ap[:, :, 5], strow.ap[:, :, 0],
                                   strow.ap[:, :, 4], ALU.mult)
        p.emit("dve", f_nb, reads=(strow,), writes=(strow,))

        def f_nb2(e):
            return e.tensor_scalar(strow.ap[:, :, 5], strow.ap[:, :, 5],
                                   -1.0, None, ALU.mult)
        p.emit("dve", f_nb2, reads=(strow,), writes=(strow,))
        scps.__exit__(None, None, None)

        ab_sb = p.sbuf("ol_ab", (128, T, 2), F32, scope=sc)
        ps_ab = p.psum("ol_psab", (128, 512), F32, scope=sc)
        for mi, (m0, mw) in enumerate(mvs(T)):
            for j, col in ((0, 4), (1, 5)):
                def f_mm(e, m0=m0, mw=mw, j=j, col=col):
                    return e.matmul(ps_ab.ap[:, 0:mw], p.onesf.ap[0:1, :],
                                    strow.ap[:, m0:m0 + mw, col],
                                    start=True, stop=True)
                p.emit("pe", f_mm, reads=(strow, p.onesf), writes=(ps_ab,),
                       war=(ps_ab,))

                def f_ev(e, m0=m0, mw=mw, j=j):
                    return e.copy(ab_sb.ap[:, m0:m0 + mw, j], ps_ab.ap[:, 0:mw])
                p.emit("act", f_ev, reads=(ps_ab,), writes=(ab_sb,))

        tmpn = p.sbuf("ol_tmpn", (128, T), F32, scope=sc)
        for hp in range(NF):
            def f_n1(e, hp=hp):
                return e.tensor_tensor(tmpn.ap[:], o_tiles[hp].ap[:],
                                       ab_sb.ap[:, :, 0], ALU.mult)
            p.emit("dve", f_n1, reads=(o_tiles[hp], ab_sb), writes=(tmpn,))

            def f_n2(e):
                return e.tensor_tensor(tmpn.ap[:], tmpn.ap[:],
                                       ab_sb.ap[:, :, 1], ALU.add)
            p.emit("dve", f_n2, reads=(tmpn, ab_sb), writes=(tmpn,))

            def f_n3(e, hp=hp):
                return e.tensor_scalar(oln_fm[hp].ap[:], tmpn.ap[:],
                                       lng.ap[:, hp:hp + 1],
                                       lnb.ap[:, hp:hp + 1],
                                       ALU.mult, ALU.add)
            p.emit("dve", f_n3, reads=(tmpn, lng, lnb), writes=(oln_fm[hp],))

    with p.scope() as sc:
        ffn_t = [p.sbuf(f"wo_ffn{j}", (128, T), BF16, scope=sc)
                 for j in range(16)]
        for j in range(16):
            def f_f(e, j=j):
                return e.dma_start(
                    out=ffn_t[j].ap[:],
                    in_=ffn_hbm.ap[j * 128:(j + 1) * 128, :])
            p.emit("sp", f_f, reads=(ffn_hbm,), writes=(ffn_t[j],),
                   dma_queue="dsp")
        wo_b = [p.sbuf(f"wo_wb{j}", (128, 24, 128), BF16, scope=sc)
                for j in range(2)]
        s_t = p.sbuf("wo_s", (128, T), BF16, scope=sc)
        psc = [p.psum(f"wo_psc{j}", (128, 512), F32, scope=sc)
               for j in range(4)]
        for m in range(NF):
            wbl = wo_b[m % 2]

            def f_w(e, m=m, wbl=wbl):
                return e.dma_start(
                    out=wbl.ap[:],
                    in_=W["Wo"].ap[:, m * 128:(m + 1) * 128].rearrange(
                        "(c p) m -> p c m", p=128))
            p.emit("pool", f_w, reads=(W["Wo"],), writes=(wbl,),
                   dma_queue="dpool")
            for ci, (m0, mw) in enumerate(mvs(T)):
                pc = psc[ci]
                for kc in range(24):
                    rhs = oln_fm[kc] if kc < 8 else ffn_t[kc - 8]

                    def f_mm(e, wbl=wbl, rhs=rhs, m0=m0, mw=mw, kc=kc, pc=pc):
                        return e.matmul(pc.ap[:, 0:mw],
                                        wbl.ap[:, kc, :],
                                        rhs.ap[:, m0:m0 + mw],
                                        start=(kc == 0), stop=(kc == 23))
                    p.emit("pe", f_mm, reads=(wbl, rhs),
                           writes=(pc,) if kc == 23 else (),
                           war=(pc,) if kc == 0 else (),
                           inc=(kc == 23))

                def f_sv(e, s_t=s_t, m0=m0, mw=mw, pc=pc):
                    return e.copy(s_t.ap[:, m0:m0 + mw], pc.ap[:, 0:mw])
                p.emit("act", f_sv, reads=(pc,), writes=(s_t,))

            def f_st(e, m=m, s_t=s_t):
                return e.dma_start(out=s_hbm.ap[m * 128:(m + 1) * 128, :],
                                   in_=s_t.ap[:])
            p.emit("sp", f_st, reads=(s_t,), writes=(s_hbm,), dma_queue="dsp")
    scow.__exit__(None, None, None)

    y_hbm = p.dram_tmp("moe_y_hbm", (C, T), BF16)
    NTT = T // 128
    with p.scope() as sc:
        xn_fm = [p.sbuf(f"moe_xn{i}", (128, T), BF16, scope=sc)
                 for i in range(NF)]
        for fi in range(NF):
            def f_lx(e, fi=fi):
                return e.dma_start(out=xn_fm[fi].ap[:],
                                   in_=xn_hbm.ap[fi * 128:(fi + 1) * 128, :])
            p.emit("sp", f_lx, reads=(xn_hbm,), writes=(xn_fm[fi],),
                   dma_queue="dsp")
        wr_sb = p.sbuf("moe_wr", (128, NF, 8), BF16, scope=sc)
        p.emit("pool", lambda e: e.dma_start(
            out=wr_sb.ap[:],
            in_=W["mWr"].ap.rearrange("(a p) m -> p a m", p=128)),
            reads=(W["mWr"],), writes=(wr_sb,), dma_queue="dpool")
        br_sb = p.sbuf("moe_br", (1, 8), BF16, scope=sc)
        p.emit("pool", lambda e: e.dma_start(
            out=br_sb.ap[:], in_=W["mbr"].ap.unsqueeze(0)),
            reads=(W["mbr"],), writes=(br_sb,), dma_queue="dpool")

        mask_hbm = p.dram_tmp("moe_mask_hbm", (8, T), F32)
        mrow = p.sbuf("moe_mrow", (1, T), F32, scope=sc)
        msk_ev = p.sbuf("moe_mskev", (8, 128), F32, scope=sc)
        ps_l = p.psum("moe_psl", (128, 8), F32, scope=sc)
        ps_mt = p.psum("moe_psmt", (8, 128), F32, scope=sc)
        lrow = p.sbuf("moe_lrow", (128, 12), F32, scope=sc)
        prob = p.sbuf("moe_prob", (128, 8), F32, scope=sc)
        midx = p.sbuf("moe_midx", (128, 8), U32, scope=sc)
        msrt = p.sbuf("moe_msrt", (128, 8), F32, scope=sc)
        l_sb = p.sbuf("moe_lsb", (128, 8), F32, scope=sc)
        msk8 = p.sbuf("moe_msk8", (128, 8), F32, scope=sc)
        for tt in range(NTT):
            for kc in range(NF):
                def f_mm(e, tt=tt, kc=kc):
                    return e.matmul(ps_l.ap[:],
                                    xn_fm[kc].ap[:, tt * 128:(tt + 1) * 128],
                                    wr_sb.ap[:, kc, :],
                                    start=(kc == 0), stop=False)
                p.emit("pe", f_mm, reads=(xn_fm[kc], wr_sb),
                       war=(ps_l,) if kc == 0 else (), inc=False)

            def f_mmb(e):
                return e.matmul(ps_l.ap[:], p.ones_bf.ap[0:1, :], br_sb.ap[:],
                                start=False, stop=True)
            p.emit("pe", f_mmb, reads=(br_sb, p.ones_bf), writes=(ps_l,))

            def f_mx(e):
                return e.tensor_reduce(lrow.ap[:, 0:1], ps_l.ap[:], AX.X,
                                       ALU.max)
            p.emit("dve", f_mx, reads=(ps_l,), writes=(lrow,))

            def f_nm(e):
                return e.tensor_scalar(lrow.ap[:, 1:2], lrow.ap[:, 0:1], -1.0,
                                       None, ALU.mult)
            p.emit("dve", f_nm, reads=(lrow,), writes=(lrow,))

            def f_ex(e):
                return e.activation(prob.ap[:], ps_l.ap[:], AF.Exp,
                                    bias=lrow.ap[:, 1:2],
                                    accum_out=lrow.ap[:, 2:3])
            p.emit("act", f_ex, reads=(ps_l, lrow), writes=(prob, lrow))

            def f_gate(e):
                return e.reciprocal(lrow.ap[:, 3:4], lrow.ap[:, 2:3])
            p.emit("dve", f_gate, reads=(lrow,), writes=(lrow,))

            def f_lsb(e):
                return e.tensor_copy(l_sb.ap[:], ps_l.ap[:])
            p.emit("dve", f_lsb, reads=(ps_l,), writes=(l_sb,))

            def f_srt(e):
                return e.max(msrt.ap[:], l_sb.ap[:])
            p.emit("dve", f_srt, reads=(l_sb,), writes=(msrt,))

            def f_idx(e):
                return e.max_index(midx.ap[:], msrt.ap[:], l_sb.ap[:])
            p.emit("dve", f_idx, reads=(msrt, l_sb), writes=(midx,))

            def f_if(e):
                return e.tensor_copy(lrow.ap[:, 4:5], midx.ap[:, 0:1])
            p.emit("dve", f_if, reads=(midx,), writes=(lrow,))

            def f_m8(e):
                return e.tensor_single_scalar(
                    msk8.ap[:], p.iota8.ap[:].broadcast_to((128, 8)),
                    lrow.ap[:, 4:5], ALU.is_equal)
            p.emit("dve", f_m8, reads=(p.iota8, lrow), writes=(msk8,))

            def f_m8g(e):
                return e.tensor_scalar(msk8.ap[:], msk8.ap[:],
                                       lrow.ap[:, 3:4], None, ALU.mult)
            p.emit("dve", f_m8g, reads=(msk8, lrow), writes=(msk8,))

            def f_tr(e):
                return e.transpose(ps_mt.ap[:], msk8.ap[:],
                                   p.idn32.ap[:])
            p.emit("pe", f_tr, reads=(msk8, p.idn32), writes=(ps_mt,))

            def f_te(e, tt=tt):
                return e.copy(msk_ev.ap[:], ps_mt.ap[:])
            p.emit("act", f_te, reads=(ps_mt,), writes=(msk_ev,))

            def f_td(e, tt=tt):
                return e.dma_start(
                    out=mask_hbm.ap[:, tt * 128:(tt + 1) * 128],
                    in_=msk_ev.ap[:])
            p.emit("sp", f_td, reads=(msk_ev,), writes=(mask_hbm,),
                   dma_queue="dsp")


        h1 = [p.sbuf(f"moe_h1_{m}", (128, T), BF16, scope=sc)
              for m in range(32)]
        wb1 = [p.sbuf(f"moe_wb1{j}", (128, 8, 128), BF16, scope=sc)
               for j in range(2)]
        wb2 = [p.sbuf(f"moe_wb2{j}", (128, 32, 128), BF16, scope=sc)
               for j in range(2)]
        maskB = p.sbuf("moe_maskB", (128, T), BF16, scope=sc)
        yacc = p.sbuf("moe_yacc", (128, T), BF16, scope=sc)
        psc = [p.psum(f"moe_psc{j}", (128, 512), F32, scope=sc)
               for j in range(4)]
        ps_mb = p.psum("moe_psmb", (128, 512), F32, scope=sc)
        tmp_y = p.sbuf("moe_tmpy", (128, T), BF16, scope=sc)
        p.emit("dve", lambda e: e.memset(tmp_y.ap[:], 0.0), writes=(tmp_y,))
        for fi in range(NF):
            def f_z(e, fi=fi):
                return e.dma_start(out=y_hbm.ap[fi * 128:(fi + 1) * 128, :],
                                   in_=tmp_y.ap[:])
            p.emit("sp", f_z, reads=(tmp_y,), writes=(y_hbm,),
                   dma_queue="dsp")
        wi = [0]
        b1_all = p.sbuf("moe_b1a", (128, 8, 32), F32, scope=sc)
        b2_all = p.sbuf("moe_b2a", (128, 8, 8), F32, scope=sc)
        for ee in range(8):
            load_featvec(p, f"b1_{ee}", W["mb1"].ap[ee, :], 4096,
                         out=b1_all, outsl=(slice(None), ee, slice(None)))
            load_featvec(p, f"b2_{ee}", W["mb2"].ap[ee, :], C,
                         out=b2_all, outsl=(slice(None), ee, slice(None)))
        for ee in range(8):
            b1e = Buf("b1e", b1_all.ap[:, ee, :])
            b1e.writer = b1_all.writer
            b2e = Buf("b2e", b2_all.ap[:, ee, :])
            b2e.writer = b2_all.writer
            def f_mr(e, ee=ee):
                return e.dma_start(out=mrow.ap[:],
                                   in_=mask_hbm.ap[ee:ee + 1, :])
            p.emit("sp", f_mr, reads=(mask_hbm,), writes=(mrow,),
                   dma_queue="dsp")
            for mi, (m0, mw) in enumerate(mvs(T)):
                def f_mm(e, ee=ee, m0=m0, mw=mw):
                    return e.matmul(ps_mb.ap[:, 0:mw], p.onesf.ap[0:1, :],
                                    mrow.ap[:, m0:m0 + mw],
                                    start=True, stop=True)
                p.emit("pe", f_mm, reads=(mrow, p.onesf), writes=(ps_mb,),
                       war=(ps_mb,))

                def f_ev(e, m0=m0, mw=mw):
                    return e.copy(maskB.ap[:, m0:m0 + mw], ps_mb.ap[:, 0:mw])
                p.emit("act", f_ev, reads=(ps_mb,), writes=(maskB,))

            for m in range(32):
                wbl = wb1[m % 2]

                def f_w(e, ee=ee, m=m, wbl=wbl):
                    return e.dma_start(
                        out=wbl.ap[:],
                        in_=W["mW1"].ap[ee, :,
                                        m * 128:(m + 1) * 128].rearrange(
                            "(c p) m -> p c m", p=128))
                p.emit("pool", f_w, reads=(W["mW1"],), writes=(wbl,),
                       dma_queue="dpool")
                for ci, (m0, mw) in enumerate(mvs(T)):
                    pc = psc[ci]
                    for kc in range(NF):
                        def f_mm(e, wbl=wbl, kc=kc, m0=m0, mw=mw, pc=pc):
                            return e.matmul(pc.ap[:, 0:mw],
                                            wbl.ap[:, kc, :],
                                            xn_fm[kc].ap[:, m0:m0 + mw],
                                            start=(kc == 0),
                                            stop=(kc == NF - 1))
                        p.emit("pe", f_mm, reads=(wbl, xn_fm[kc]),
                               writes=(pc,) if kc == NF - 1 else (),
                               war=(pc,) if kc == 0 else (),
                               inc=(kc == NF - 1))

                    def f_relu(e, m=m, b1e=b1e, m0=m0, mw=mw, pc=pc):
                        return e.activation(h1[m].ap[:, m0:m0 + mw],
                                            pc.ap[:, 0:mw], AF.Relu,
                                            bias=b1e.ap[:, m:m + 1])
                    p.emit("act", f_relu, reads=(pc, b1e), writes=(h1[m],))

            for m2 in range(NF):
                wbl = wb2[m2 % 2]

                def f_w(e, ee=ee, m2=m2, wbl=wbl):
                    return e.dma_start(
                        out=wbl.ap[:],
                        in_=W["mW2"].ap[ee, :,
                                        m2 * 128:(m2 + 1) * 128].rearrange(
                            "(c p) m -> p c m", p=128))
                p.emit("pool", f_w, reads=(W["mW2"],), writes=(wbl,),
                       dma_queue="dpool")
                for ci, (m0, mw) in enumerate(mvs(T)):
                    pc = psc[ci]
                    for kc2 in range(32):
                        def f_mm(e, wbl=wbl, kc2=kc2, m0=m0, mw=mw, pc=pc):
                            return e.matmul(pc.ap[:, 0:mw],
                                            wbl.ap[:, kc2, :],
                                            h1[kc2].ap[:, m0:m0 + mw],
                                            start=(kc2 == 0),
                                            stop=(kc2 == 31))
                        p.emit("pe", f_mm, reads=(wbl, h1[kc2]),
                               writes=(pc,) if kc2 == 31 else (),
                               war=(pc,) if kc2 == 0 else (),
                               inc=(kc2 == 31))

                    def f_b2(e, m2=m2, b2e=b2e, m0=m0, mw=mw, pc=pc):
                        return e.tensor_scalar(tmp_y.ap[:, m0:m0 + mw],
                                               pc.ap[:, 0:mw],
                                               b2e.ap[:, m2:m2 + 1], None,
                                               ALU.add)
                    p.emit("dve", f_b2, reads=(pc, b2e), writes=(tmp_y,))

                def f_msk(e):
                    return e.tensor_tensor(tmp_y.ap[:], tmp_y.ap[:],
                                           maskB.ap[:], ALU.mult)
                p.emit("dve", f_msk, reads=(tmp_y, maskB), writes=(tmp_y,))

                def f_yl(e, m2=m2):
                    return e.dma_start(
                        out=yacc.ap[:],
                        in_=y_hbm.ap[m2 * 128:(m2 + 1) * 128, :])
                p.emit("sp", f_yl, reads=(y_hbm,), writes=(yacc,),
                       dma_queue="dsp")

                def f_acc(e, m2=m2):
                    return e.tensor_tensor(yacc.ap[:], yacc.ap[:],
                                           tmp_y.ap[:], ALU.add)
                p.emit("dve", f_acc, reads=(yacc, tmp_y), writes=(yacc,))

                def f_ys(e, m2=m2):
                    return e.dma_start(
                        out=y_hbm.ap[m2 * 128:(m2 + 1) * 128, :],
                        in_=yacc.ap[:])
                p.emit("sp", f_ys, reads=(yacc,), writes=(y_hbm,),
                       dma_queue="dsp")

    if "y" in dbg:
        ydmp = p.dram_out("dbg_y", (C, T), BF16)
        yb = Buf("ydump", y_hbm.ap)
        yb.writer = y_hbm.writer
        p.emit("sp", lambda e: e.dma_start(out=ydmp.ap[:], in_=y_hbm.ap[:]),
               reads=(y_hbm,), writes=(ydmp,), dma_queue="dsp")

    with p.scope() as sc:
        stage = p.sbuf("cb_stage", (128, C), F32, scope=sc)
        ps_tr = [p.psum(f"cb_pstr{j}", (128, 128), F32, scope=sc)
                 for j in range(2)]
        s_t = p.sbuf("cb_st", (128, T), BF16, scope=sc)
        g_t = p.sbuf("cb_gt", (128, T), BF16, scope=sc)
        x_cols = [p.sbuf(f"cb_xc{i}", (128, T), F32, scope=sc)
                  for i in range(NF)]
        for tt in range(T // 128):
            def f_ld(e, tt=tt):
                return e.dma_start(out=stage.ap[:],
                                   in_=x_in.ap[tt * 128:(tt + 1) * 128, :])
            p.emit("sp", f_ld, reads=(x_in,), writes=(stage,), dma_queue="dsp")
            for fi in range(NF):
                pst = ps_tr[fi % 2]

                def f_tr(e, fi=fi, pst=pst):
                    return e.transpose(pst.ap[:],
                                       stage.ap[:, fi * 128:(fi + 1) * 128],
                                       p.idn32.ap[:])
                p.emit("pe", f_tr, reads=(stage, p.idn32), writes=(pst,))

                def f_ev(e, fi=fi, tt=tt, pst=pst):
                    return e.copy(x_cols[fi].ap[:, tt * 128:(tt + 1) * 128],
                                  pst.ap[:])
                p.emit("act", f_ev, reads=(pst,), writes=(x_cols[fi],))

        y_t = p.sbuf("cb_yt", (128, T), BF16, scope=sc)
        for fi in range(NF):
            def f_yl(e, fi=fi):
                return e.dma_start(out=y_t.ap[:],
                                   in_=y_hbm.ap[fi * 128:(fi + 1) * 128, :])
            p.emit("sp", f_yl, reads=(y_hbm,), writes=(y_t,), dma_queue="dsp")

            def f_g(e, fi=fi):
                return e.activation(g_t.ap[:], y_t.ap[:], AF.Silu)
            p.emit("act", f_g, reads=(y_t,), writes=(g_t,))

            def f_sl(e, fi=fi):
                return e.dma_start(out=s_t.ap[:],
                                   in_=s_hbm.ap[fi * 128:(fi + 1) * 128, :])
            p.emit("sp", f_sl, reads=(s_hbm,), writes=(s_t,), dma_queue="dsp")

            def f_sg(e):
                return e.tensor_tensor(g_t.ap[:], g_t.ap[:], s_t.ap[:],
                                       ALU.mult)
            p.emit("dve", f_sg, reads=(g_t, s_t), writes=(g_t,))

            def f_x1(e, fi=fi):
                return e.tensor_tensor(x_cols[fi].ap[:], x_cols[fi].ap[:],
                                       g_t.ap[:], ALU.add)
            p.emit("dve", f_x1, reads=(x_cols[fi], g_t),
                   writes=(x_cols[fi],))

            def f_st(e, fi=fi):
                return e.dma_start(out=x1_hbm.ap[fi * 128:(fi + 1) * 128, :],
                                   in_=x_cols[fi].ap[:])
            p.emit("sp", f_st, reads=(x_cols[fi],), writes=(x1_hbm,),
                   dma_queue="dsp")

        if x1t_hbm is not None:
            for tt in range(T // 128):
                for fi in range(NF):
                    pst = ps_tr[fi % 2]

                    def f_tr(e, fi=fi, tt=tt, pst=pst):
                        return e.transpose(
                            pst.ap[:],
                            x_cols[fi].ap[:, tt * 128:(tt + 1) * 128],
                            p.idn32.ap[:])
                    p.emit("pe", f_tr, reads=(x_cols[fi], p.idn32),
                           writes=(pst,))

                    def f_ev(e, fi=fi, tt=tt, pst=pst):
                        return e.copy(stage.ap[:, fi * 128:(fi + 1) * 128],
                                      pst.ap[:])
                    p.emit("act", f_ev, reads=(pst,), writes=(stage,))

                def f_so(e, tt=tt):
                    return e.dma_start(
                        out=x1t_hbm.ap[tt * 128:(tt + 1) * 128, :],
                        in_=stage.ap[:])
                p.emit("sp", f_so, reads=(stage,), writes=(x1t_hbm,),
                       dma_queue="dsp")



def declare_b1_weights(p):
    W = {}
    W["Wq"] = p.dram_in("aWq", (C, C), F32)
    W["bq"] = p.dram_in("abq", (C,), F32)
    W["Wk"] = p.dram_in("aWk", (C, C), F32)
    W["bk"] = p.dram_in("abk", (C,), F32)
    W["Wv"] = p.dram_in("aWv", (C, C), F32)
    W["bv"] = p.dram_in("abv", (C,), F32)
    W["Wo"] = p.dram_in("aWo", (C, C), F32)
    W["bo"] = p.dram_in("abo", (C,), F32)
    W["hm"] = p.dram_in("ahm", (16,), F32)
    W["gamma"] = p.dram_in("pgamma", (C,), F32)
    W["beta"] = p.dram_in("pbeta", (C,), F32)
    W["w_q"] = p.dram_in("pwq", (2 * C, C), F32)
    W["keys"] = p.dram_in("pkeys", (8, 256, 2, 128), F32)
    W["key_in"] = p.dram_in("pkin", (65536, C), F32)
    W["key_out"] = p.dram_in("pkout", (65536, C), F32)
    return W


def b1_entry(p, T, x1f_hbm, xn2_fm, xn2_tm, dbg, ncores=1):
    with p.scope() as sc:
        xt = [p.sbuf(f"n2_x{j}", (128, T), F32, scope=sc) for j in range(2)]
        x2sq = p.sbuf("n2_sq", (128, T), BF16, scope=sc)
        xbf = [p.sbuf(f"n2_xb{j}", (128, T), BF16, scope=sc) for j in range(2)]
        strow = p.sbuf("n2_st", (1, T, 6), F32, scope=sc)
        scps = p.scope()
        scps_es = scps.__enter__()
        ps_s1 = p.psum("n2_ps1", (1, T), F32, scope=scps_es)
        ps_s2 = p.psum("n2_ps2", (1, T), F32, scope=scps_es)
        for fi in range(NF):
            x_t = xt[fi % 2]

            def f_ld(e, fi=fi, x_t=x_t):
                return e.dma_start(out=x_t.ap[:],
                                   in_=x1f_hbm.ap[fi * 128:(fi + 1) * 128, :])
            p.emit("sp", f_ld, reads=(x1f_hbm,), writes=(x_t,),
                   dma_queue="dsp")

            def f_sq(e, x_t=x_t):
                return e.activation(x2sq.ap[:], x_t.ap[:], AF.Square)
            p.emit("act", f_sq, reads=(x_t,), writes=(x2sq,))

            sl = mvs(T)
            for mi, (m0, mw) in enumerate(sl):
                def f_mm(e, x_t=x_t, m0=m0, mw=mw, fi=fi):
                    return e.matmul(ps_s1.ap[:, m0:m0 + mw],
                                    p.onesf.ap[:, 0:1],
                                    x_t.ap[:, m0:m0 + mw],
                                    start=(fi == 0), stop=(fi == NF - 1))
                p.emit("pe", f_mm, reads=(x_t, p.onesf),
                       writes=(ps_s1,) if fi == NF - 1 and mi == len(sl) - 1
                       else (),
                       war=(ps_s1,) if fi == 0 and mi == 0 else (),
                       inc=(mi == len(sl) - 1))

                def f_mm2(e, m0=m0, mw=mw, fi=fi):
                    return e.matmul(ps_s2.ap[:, m0:m0 + mw],
                                    p.ones_bf.ap[:, 0:1],
                                    x2sq.ap[:, m0:m0 + mw],
                                    start=(fi == 0), stop=(fi == NF - 1))
                p.emit("pe", f_mm2, reads=(x2sq, p.ones_bf),
                       writes=(ps_s2,) if fi == NF - 1 and mi == len(sl) - 1
                       else (),
                       war=(ps_s2,) if fi == 0 and mi == 0 else (),
                       inc=(mi == len(sl) - 1))

        p.emit("dve", lambda e: e.tensor_scalar(
            strow.ap[:, :, 0], ps_s1.ap[:], 1.0 / C, None, ALU.mult),
            reads=(ps_s1,), writes=(strow,))
        p.emit("dve", lambda e: e.tensor_tensor(
            strow.ap[:, :, 1], strow.ap[:, :, 0], strow.ap[:, :, 0],
            ALU.mult), reads=(strow,), writes=(strow,))
        p.emit("dve", lambda e: e.scalar_tensor_tensor(
            strow.ap[:, :, 2], ps_s2.ap[:], 1.0 / C, strow.ap[:, :, 1],
            ALU.mult, ALU.subtract), reads=(ps_s2, strow), writes=(strow,))
        p.emit("dve", lambda e: e.tensor_scalar(
            strow.ap[:, :, 2], strow.ap[:, :, 2], 1e-5, None, ALU.add),
            reads=(strow,), writes=(strow,))
        p.emit("dve", lambda e: e.reciprocal(
            strow.ap[:, :, 3], strow.ap[:, :, 2]),
            reads=(strow,), writes=(strow,))
        p.emit("act", lambda e: e.activation(
            strow.ap[:, :, 4], strow.ap[:, :, 3], AF.Sqrt),
            reads=(strow,), writes=(strow,))
        p.emit("dve", lambda e: e.tensor_tensor(
            strow.ap[:, :, 5], strow.ap[:, :, 0], strow.ap[:, :, 4],
            ALU.mult), reads=(strow,), writes=(strow,))
        p.emit("dve", lambda e: e.tensor_scalar(
            strow.ap[:, :, 5], strow.ap[:, :, 5], -1.0, None, ALU.mult),
            reads=(strow,), writes=(strow,))
        scps.__exit__(None, None, None)

        ab_sb = p.sbuf("n2_ab", (128, T, 2), F32, scope=sc)
        ps_ab = p.psum("n2_psab", (128, 512), F32, scope=sc)
        for mi, (m0, mw) in enumerate(mvs(T)):
            for j, col in ((0, 4), (1, 5)):
                def f_mm(e, m0=m0, mw=mw, j=j, col=col):
                    return e.matmul(ps_ab.ap[:, 0:mw], p.onesf.ap[0:1, :],
                                    strow.ap[:, m0:m0 + mw, col],
                                    start=True, stop=True)
                p.emit("pe", f_mm, reads=(strow, p.onesf), writes=(ps_ab,),
                       war=(ps_ab,))

                def f_ev(e, m0=m0, mw=mw, j=j):
                    return e.copy(ab_sb.ap[:, m0:m0 + mw, j],
                                  ps_ab.ap[:, 0:mw])
                p.emit("act", f_ev, reads=(ps_ab,), writes=(ab_sb,))

        tmpn = p.sbuf("n2_tmp", (128, T), F32, scope=sc)
        ps_tr = [p.psum(f"n2_pstr{j}", (128, 128), BF16, scope=sc)
                 for j in range(2)]
        for fi in range(NF):
            x_t = xbf[fi % 2]

            def f_rl(e, fi=fi, x_t=x_t):
                return e.dma_start(out=x_t.ap[:],
                                   in_=x1f_hbm.ap[fi * 128:(fi + 1) * 128, :])
            p.emit("pool", f_rl, reads=(x1f_hbm,), writes=(x_t,),
                   dma_queue="dpool")

            def f_n1(e, fi=fi, x_t=x_t):
                return e.tensor_tensor(tmpn.ap[:], x_t.ap[:],
                                       ab_sb.ap[:, :, 0], ALU.mult)
            p.emit("dve", f_n1, reads=(x_t, ab_sb), writes=(tmpn,))

            def f_n2(e, fi=fi):
                return e.tensor_tensor(xn2_fm[fi].ap[:], tmpn.ap[:],
                                       ab_sb.ap[:, :, 1], ALU.add)
            p.emit("dve", f_n2, reads=(tmpn, ab_sb), writes=(xn2_fm[fi],))

            for tt in range(T // 128):
                pst = ps_tr[tt % 2]

                def f_tr(e, fi=fi, tt=tt, pst=pst):
                    return e.transpose(
                        pst.ap[:], xn2_fm[fi].ap[:, tt * 128:(tt + 1) * 128],
                        p.idn.ap[:])
                p.emit("pe", f_tr, reads=(xn2_fm[fi], p.idn), writes=(pst,))

                def f_ev(e, fi=fi, tt=tt, pst=pst):
                    return e.copy(
                        xn2_tm[tt].ap[:, fi * 128:(fi + 1) * 128], pst.ap[:])
                p.emit("act", f_ev, reads=(pst,), writes=(xn2_tm[tt],))


def trans_weight_lhsT(p, w_hbm, m, kc, wt_raw, wt_out, ps_tr, transT=True):
    def f_ld(e):
        return e.dma_start(out=wt_raw.ap[:],
                           in_=w_hbm.ap[m * 128:(m + 1) * 128,
                                        kc * 128:(kc + 1) * 128])
    p.emit("pool", f_ld, reads=(w_hbm,), writes=(wt_raw,), dma_queue="dpool")

    def f_tr(e):
        return e.transpose(ps_tr.ap[:], wt_raw.ap[:], p.idn.ap[:])
    p.emit("pe", f_tr, reads=(wt_raw, p.idn), writes=(ps_tr,))

    def f_ev(e):
        return e.copy(wt_out.ap[:], ps_tr.ap[:])
    p.emit("act", f_ev, reads=(ps_tr,), writes=(wt_out,))


def trans_weight_blk(p, w_hbm, m, nkc, wraw_blk, wlhs_blk, ps_tr):
    def f_ld(e):
        return e.dma_start(out=wraw_blk.ap[:, 0:nkc, :],
                           in_=w_hbm.ap[m * 128:(m + 1) * 128, :].rearrange(
                               "p (c q) -> p c q", q=128))
    p.emit("pool", f_ld, reads=(w_hbm,), writes=(wraw_blk,),
           dma_queue="dpool")
    for kc in range(nkc):
        def f_tr(e, kc=kc):
            return e.transpose(ps_tr.ap[:], wraw_blk.ap[:, kc, :],
                               p.idn.ap[:])
        p.emit("pe", f_tr, reads=(wraw_blk, p.idn), writes=(ps_tr,))

        def f_ev(e, kc=kc):
            return e.copy(wlhs_blk.ap[:, kc, :], ps_tr.ap[:])
        p.emit("act", f_ev, reads=(ps_tr,), writes=(wlhs_blk,))


def b1_attn_setup(p, T, W, sc, biases):
    NTT = T // 128
    ctx = {}
    ctx.update(biases)
    ctx["attn_hbm"] = p.dram_tmp("attn_hbm", (C, T), BF16)
    hm_sb = p.sbuf("at_hm", (1, 16), F32, scope=sc)
    p.emit("sp", lambda e: e.dma_start(out=hm_sb.ap[:],
                                       in_=W["hm"].ap.unsqueeze(0)),
           reads=(W["hm"],), writes=(hm_sb,), dma_queue="dsp")
    ctx["hm"] = hm_sb
    ctx["q_c"] = p.sbuf("at_q", (128, T), BF16, scope=sc)
    ctx["k_c"] = p.sbuf("at_k", (128, T), BF16, scope=sc)
    ctx["v_c"] = p.sbuf("at_v", (128, T), BF16, scope=sc)
    ctx["vt_c"] = p.sbuf("at_vt", (128, T), BF16, scope=sc)
    ctx["wraw"] = [p.sbuf(f"at_wr{j}", (128, NF, 128), BF16, scope=sc)
                   for j in range(2)]
    ctx["wlhs"] = [p.sbuf(f"at_wl{j}", (128, NF, 128), BF16, scope=sc)
                   for j in range(2)]
    ctx["wbi"] = [0]
    ctx["att_o"] = p.sbuf("at_ao", (128, T), BF16, scope=sc)
    ctx["rowtmp"] = p.sbuf("at_row", (1, 1, 2), F32, scope=sc)
    ctx["coltmp"] = p.sbuf("at_col", (128, 4), F32, scope=sc)
    ctx["attT"] = [p.sbuf(f"at_attT{j}", (128, 1024), BF16, scope=sc)
                   for j in range(2)]
    ctx["vth"] = [p.sbuf(f"at_vth{j}", (128, NTT, 65), BF16, scope=sc)
                  for j in range(2)]
    ctx["drow"] = p.sbuf("at_drow", (1, min(1024, T)), F32, scope=sc)
    for j in range(2):
        def f_one(e, j=j):
            return e.memset(ctx["vth"][j].ap[:, :, 64:65], 1.0)
        p.emit("dve", f_one, writes=(ctx["vth"][j],))
    return ctx


def b1_attn_chunk(p, T, ctx, xn2_fm, W, mc, dbg, pump):
    NTT = T // 128
    q_c, k_c, v_c = ctx["q_c"], ctx["k_c"], ctx["v_c"]
    att_o, rowtmp, coltmp = ctx["att_o"], ctx["rowtmp"], ctx["coltmp"]
    attTs, hm_sb = ctx["attT"], ctx["hm"]
    vth = ctx["vth"]
    with p.scope() as scp:
        psc = [p.psum(f"at_psq{j}", (128, 512), F32, scope=scp)
               for j in range(4)]
        ps_t = p.psum("at_pst", (128, 128), BF16, scope=scp)
        for dst, wname, bias in ((q_c, "Wq", ctx["bq"]),
                                 (k_c, "Wk", ctx["bk"]),
                                 (v_c, "Wv", ctx["bv"])):
            wr = ctx["wraw"][ctx["wbi"][0] % 2]
            wl = ctx["wlhs"][ctx["wbi"][0] % 2]
            ctx["wbi"][0] += 1
            trans_weight_blk(p, W[wname], mc, NF, wr, wl, ps_t)
            for ci, (m0, mw) in enumerate(mvs(T)):
                pc = psc[ci]
                for kc in range(NF):
                    def f_mm(e, kc=kc, m0=m0, mw=mw, wl=wl, pc=pc):
                        return e.matmul(
                            pc.ap[:, 0:mw], wl.ap[:, kc, :],
                            xn2_fm[kc].ap[:, m0:m0 + mw],
                            start=(kc == 0), stop=(kc == NF - 1))
                    p.emit("pe", f_mm, reads=(wl, xn2_fm[kc]),
                           writes=(pc,) if kc == NF - 1 else (),
                           war=(pc,) if kc == 0 else (),
                           inc=(kc == NF - 1))

                def f_b(e, dst=dst, bias=bias, mc=mc, m0=m0, mw=mw, pc=pc):
                    return e.activation(dst.ap[:, m0:m0 + mw],
                                        pc.ap[:, 0:mw], AF.Identity,
                                        bias=bias.ap[:, mc:mc + 1])
                p.emit("act", f_b, reads=(pc, bias), writes=(dst,))

        for tt in range(NTT):
            def f_tr(e, tt=tt, ps_t=ps_t):
                return e.transpose(
                    ps_t.ap[:], v_c.ap[:, tt * 128:(tt + 1) * 128],
                    p.idn.ap[:])
            p.emit("pe", f_tr, reads=(v_c, p.idn), writes=(ps_t,))

            for hh in range(2):
                def f_ev(e, tt=tt, hh=hh, ps_t=ps_t):
                    return e.copy(vth[hh].ap[:, tt, 0:64],
                                  ps_t.ap[:, 64 * hh:64 * hh + 64])
                p.emit("act", f_ev, reads=(ps_t,), writes=(vth[hh],))

    if "qkv" in dbg:
        dump_tile(p, q_c, f"dbg_q{mc}", (128, T), BF16)
        dump_tile(p, k_c, f"dbg_k1{mc}", (128, T), BF16)
        dump_tile(p, v_c, f"dbg_v1{mc}", (128, T), BF16)
    pump(6)
    for hh in range(2):
        h = mc * 2 + hh
        hr = slice(64 * hh, 64 * hh + 64)
        with p.scope() as sch:
            ps_sT = [p.psum(f"at_psT{j}", (128, 1024), F32, scope=sch)
                     for j in range(2)]
            ps_oT = p.psum("at_psoT", (65, 1024), F32, scope=sch)
            ps_b = p.psum("at_psb", (128, 512), F32, scope=sch)
            for qh in range(T // 1024 if T >= 1024 else 1):
                QW = min(1024, T)
                q0 = qh * QW
                for kt in range(NTT):
                    attT = attTs[kt % 2]
                    psT = ps_sT[kt % 2]
                    sl2 = mvs(QW)
                    for mi, (m0, mw) in enumerate(sl2):
                        def f_mm(e, kt=kt, m0=m0, mw=mw, hr=hr, q0=q0,
                                 psT=psT):
                            return e.matmul(
                                psT.ap[:, m0:m0 + mw],
                                k_c.ap[hr, kt * 128:(kt + 1) * 128],
                                q_c.ap[hr, q0 + m0:q0 + m0 + mw],
                                start=True, stop=True)
                        p.emit("pe", f_mm, reads=(k_c, q_c),
                               writes=(psT,) if mi == len(sl2) - 1 else (),
                               war=(psT,) if mi == 0 else (),
                               inc=(mi == len(sl2) - 1))

                    if kt == 0 and qh == 0:
                        def f_cm(e, QW=QW, psT=psT):
                            return e.tensor_reduce(
                                coltmp.ap[:, 0:1], psT.ap[:, 0:QW],
                                AX.X, ALU.max)
                        p.emit("dve", f_cm, reads=(psT,), writes=(coltmp,))

                        def f_ct(e):
                            return e.transpose(
                                ps_b.ap[0:1, 0:128],
                                coltmp.ap[:, 0:1], p.idn32.ap[:])
                        p.emit("pe", f_ct, reads=(coltmp, p.idn32),
                               writes=(ps_b,))

                        def f_cr(e):
                            return e.tensor_reduce(
                                rowtmp.ap[:, 0:1, 0],
                                ps_b.ap[0:1, 0:128], AX.X, ALU.max)
                        p.emit("dve", f_cr, reads=(ps_b,), writes=(rowtmp,))

                        def f_cb(e):
                            return e.matmul(
                                ps_b.ap[:, 0:1], p.onesf.ap[0:1, :],
                                rowtmp.ap[:, 0:1, 0],
                                start=True, stop=True)
                        p.emit("pe", f_cb, reads=(rowtmp, p.onesf),
                               writes=(ps_b,))

                        def f_cn(e):
                            return e.tensor_scalar(
                                coltmp.ap[:, 1:2], ps_b.ap[:, 0:1],
                                -0.125, None, ALU.mult)
                        p.emit("dve", f_cn, reads=(ps_b,), writes=(coltmp,))

                    def f_ex(e, QW=QW, attT=attT, psT=psT):
                        return e.activation(
                            attT.ap[:, 0:QW], psT.ap[:, 0:QW],
                            AF.Exp, bias=coltmp.ap[:, 1:2], scale=0.125)
                    p.emit("act", f_ex, reads=(psT, coltmp), writes=(attT,))

                    for mi, (m0, mw) in enumerate(mvs(QW)):
                        def f_mo(e, m0=m0, mw=mw, kt=kt, hh=hh, attT=attT):
                            return e.matmul(
                                ps_oT.ap[:, m0:m0 + mw],
                                vth[hh].ap[:, kt, :],
                                attT.ap[:, m0:m0 + mw],
                                start=(kt == 0), stop=(kt == NTT - 1))
                        p.emit("pe", f_mo, reads=(attT, vth[hh]),
                               writes=(ps_oT,) if kt == NTT - 1 and
                               mi == len(mvs(QW)) - 1 else (),
                               war=(ps_oT,) if kt == 0 and mi == 0 else (),
                               inc=(mi == len(mvs(QW)) - 1))

                def f_oe(e, QW=QW, q0=q0, hr=hr):
                    return e.copy(att_o.ap[hr, q0:q0 + QW],
                                  ps_oT.ap[0:64, 0:QW])
                p.emit("act", f_oe, reads=(ps_oT,), writes=(att_o,))

                def f_dc(e, QW=QW):
                    return e.copy(ctx["drow"].ap[:, 0:QW],
                                  ps_oT.ap[64:65, 0:QW])
                p.emit("act", f_dc, reads=(ps_oT,), writes=(ctx["drow"],))

                def f_dv(e, QW=QW, q0=q0, h=h):
                    return e.dma_start(
                        out=ctx["d_hbm"].ap[h:h + 1, q0:q0 + QW],
                        in_=ctx["drow"].ap[:, 0:QW])
                p.emit("sp", f_dv, reads=(ctx["drow"],),
                       writes=(ctx["d_hbm"],), dma_queue="dsp")
                pump(1)

    if "qkv" in dbg:
        dump_tile(p, att_o, f"dbg_ao{mc}", (128, T), BF16)

    def f_sa(e, mc=mc):
        return e.dma_start(
            out=ctx["attn_hbm"].ap[mc * 128:(mc + 1) * 128, :],
            in_=att_o.ap[:])
    p.emit("sp", f_sa, reads=(att_o,), writes=(ctx["attn_hbm"],),
           dma_queue="dsp")
    pump(2)


def b1_attn_wo(p, T, ctx, W, s2t_hbm):
    attn_hbm = ctx["attn_hbm"]
    bo = ctx["bo"]
    d_hbm = ctx["d_hbm"]
    hm_col = ctx["hm_col"]
    with p.scope() as sc:
        dinv_all = p.sbuf("wo2_dinv", (16, T), F32, scope=sc)
        p.emit("sp", lambda e: e.dma_start(out=dinv_all.ap[:],
                                           in_=d_hbm.ap[:]),
               reads=(d_hbm,), writes=(dinv_all,), dma_queue="dsp")
        a_t = [p.sbuf(f"wo2_a{j}", (128, T), BF16, scope=sc)
               for j in range(NF)]
        for kc in range(NF):
            def f_al(e, kc=kc):
                return e.dma_start(
                    out=a_t[kc].ap[:],
                    in_=attn_hbm.ap[kc * 128:(kc + 1) * 128, :])
            p.emit("sp", f_al, reads=(attn_hbm,), writes=(a_t[kc],),
                   dma_queue="dsp")
        def f_rc(e):
            return e.reciprocal(dinv_all.ap[:], dinv_all.ap[:])
        p.emit("dve", f_rc, reads=(dinv_all,), writes=(dinv_all,))

        def f_hm(e):
            return e.tensor_scalar(dinv_all.ap[:], dinv_all.ap[:],
                                   hm_col.ap[:, 0:1], None, ALU.mult)
        p.emit("dve", f_hm, reads=(dinv_all, hm_col), writes=(dinv_all,))
        sel16 = p.sbuf("wo2_sel16", (16, NF, 128), F32, scope=sc)
        iotaP = p.sbuf("wo2_iotaP", (16, 1), F32, scope=sc)
        p.emit("pool", lambda e: e.iota(iotaP.ap[:], pattern=[[1, 1]],
                                        base=0, channel_multiplier=1,
                                        allow_small_or_imprecise_dtypes=True),
               writes=(iotaP,))
        for kc in range(NF):
            for half in range(2):
                def f_sel(e, kc=kc, half=half):
                    return e.tensor_scalar(
                        sel16.ap[:, kc, 64 * half:64 * half + 64],
                        iotaP.ap[:].broadcast_to((16, 64)),
                        float(2 * kc + half), None, ALU.is_equal)
                p.emit("dve", f_sel, reads=(iotaP,), writes=(sel16,))
        ps_n = [p.psum(f"wo2_psn{j}", (128, 512), F32, scope=sc)
                for j in range(2)]
        for kc in range(NF):
            for ci, (m0, mw) in enumerate(mvs(T)):
                pn = ps_n[ci % 2]

                def f_nb(e, kc=kc, m0=m0, mw=mw, pn=pn):
                    return e.matmul(pn.ap[:, 0:mw], sel16.ap[:, kc, :],
                                    dinv_all.ap[:, m0:m0 + mw],
                                    start=True, stop=True)
                p.emit("pe", f_nb, reads=(sel16, dinv_all), writes=(pn,),
                       war=(pn,))

                def f_nm(e, kc=kc, m0=m0, mw=mw, pn=pn):
                    return e.tensor_tensor(a_t[kc].ap[:, m0:m0 + mw],
                                           a_t[kc].ap[:, m0:m0 + mw],
                                           pn.ap[:, 0:mw], ALU.mult)
                p.emit("dve", f_nm, reads=(a_t[kc], pn), writes=(a_t[kc],))
        wraw2 = [p.sbuf("wo2_wr0", (128, NF, 128), BF16, scope=sc)]
        wlhs2 = [p.sbuf("wo2_wl0", (128, NF, 128), BF16, scope=sc)]
        s_t = p.sbuf("wo2_s", (128, T), BF16, scope=sc)
        psc = [p.psum(f"wo2_psc{j}", (128, 512), F32, scope=sc)
               for j in range(4)]
        ps_t = p.psum("wo2_pst", (128, 128), BF16, scope=sc)
        for m in range(NF):
            wr = wraw2[0]
            wl = wlhs2[0]
            trans_weight_blk(p, W["Wo"], m, NF, wr, wl, ps_t)
            for ci, (m0, mw) in enumerate(mvs(T)):
                pc = psc[ci]
                for kc in range(NF):
                    def f_mm(e, m0=m0, mw=mw, kc=kc, wl=wl, pc=pc):
                        return e.matmul(pc.ap[:, 0:mw], wl.ap[:, kc, :],
                                        a_t[kc].ap[:, m0:m0 + mw],
                                        start=(kc == 0), stop=(kc == NF - 1))
                    p.emit("pe", f_mm, reads=(wl, a_t[kc]),
                           writes=(pc,) if kc == NF - 1 else (),
                           war=(pc,) if kc == 0 else (),
                           inc=(kc == NF - 1))

                def f_sb(e, m=m, m0=m0, mw=mw, pc=pc):
                    return e.tensor_scalar(s_t.ap[:, m0:m0 + mw],
                                           pc.ap[:, 0:mw],
                                           bo.ap[:, m:m + 1], None, ALU.add)
                p.emit("dve", f_sb, reads=(pc, bo), writes=(s_t,))

            def f_st(e, m=m):
                return e.dma_start(out=s2t_hbm.ap[m * 128:(m + 1) * 128, :],
                                   in_=s_t.ap[:])
            p.emit("sp", f_st, reads=(s_t,), writes=(s2t_hbm,),
                   dma_queue="dsp")


def b1_pkm_pre(p, T, xn2_fm, xn2_tm, W, dbg, ncores=1, sc_outer=None):
    NTT = T // 128
    gam = load_featvec(p, "gam_fv", W["gamma"].ap, C, scope=sc_outer)
    bet = load_featvec(p, "bet_fv", W["beta"].ap, C, scope=sc_outer)

    sums = p.sbuf("bn_sums", (128, NF, 2), F32, scope=sc_outer)
    with p.scope() as sc:
        sqf = p.sbuf("bn_sqf", (128, T), BF16, scope=sc)
        for fi in range(NF):
            def f_s(e, fi=fi):
                return e.tensor_reduce(sums.ap[:, fi, 0:1], xn2_fm[fi].ap[:],
                                       AX.X, ALU.add)
            p.emit("dve", f_s, reads=(xn2_fm[fi],), writes=(sums,))

            def f_qq(e, fi=fi):
                return e.activation(sqf.ap[:], xn2_fm[fi].ap[:], AF.Square,
                                    accum_out=sums.ap[:, fi, 1:2])
            p.emit("act", f_qq, reads=(xn2_fm[fi],), writes=(sums, sqf))

    if ncores > 1:
        cin = p.dram_tmp("bn_cin", (128, 2 * NF), F32)
        cout = p.dram_tmp("bn_cout", (128, 2 * NF), F32)
        p.emit("sp", lambda e: e.dma_start(
            out=cin.ap[:], in_=sums.ap.rearrange("p a b -> p (a b)")),
            reads=(sums,), writes=(cin,), dma_queue="dsp")

        def f_cc(e):
            return e.collective_compute(
                "AllReduce", ALU.add,
                [[i for i in range(ncores)]], [cin.ap[:].opt()],
                [cout.ap[:].opt()])
        p.emit("pool", f_cc, reads=(cin,), writes=(cout,))
        p.emit("sp", lambda e: e.dma_start(
            out=sums.ap.rearrange("p a b -> p (a b)"), in_=cout.ap[:]),
            reads=(cout,), writes=(sums,), dma_queue="dsp")

    ab = p.sbuf("bn_ab", (128, NF, 2), F32, scope=sc_outer)
    with p.scope() as sc:
        st = p.sbuf("bn_st", (128, NF, 4), F32, scope=sc)
        inv_n = 1.0 / (ncores * T)
        p.emit("dve", lambda e: e.tensor_scalar(
            st.ap[:, :, 0], sums.ap[:, :, 0], inv_n, None, ALU.mult),
            reads=(sums,), writes=(st,))
        p.emit("dve", lambda e: e.tensor_tensor(
            st.ap[:, :, 1], st.ap[:, :, 0], st.ap[:, :, 0], ALU.mult),
            reads=(st,), writes=(st,))
        p.emit("dve", lambda e: e.scalar_tensor_tensor(
            st.ap[:, :, 2], sums.ap[:, :, 1], inv_n, st.ap[:, :, 1],
            ALU.mult, ALU.subtract), reads=(sums, st), writes=(st,))
        p.emit("dve", lambda e: e.tensor_scalar(
            st.ap[:, :, 2], st.ap[:, :, 2], 1e-5, None, ALU.add),
            reads=(st,), writes=(st,))
        p.emit("dve", lambda e: e.reciprocal(
            st.ap[:, :, 3], st.ap[:, :, 2]), reads=(st,), writes=(st,))
        p.emit("act", lambda e: e.activation(
            st.ap[:, :, 3], st.ap[:, :, 3], AF.Sqrt),
            reads=(st,), writes=(st,))
        p.emit("dve", lambda e: e.tensor_tensor(
            ab.ap[:, :, 0], gam.ap[:], st.ap[:, :, 3], ALU.mult),
            reads=(gam, st), writes=(ab,))
        p.emit("dve", lambda e: e.tensor_tensor(
            st.ap[:, :, 1], st.ap[:, :, 0], ab.ap[:, :, 0], ALU.mult),
            reads=(st, ab), writes=(st,))
        p.emit("dve", lambda e: e.tensor_tensor(
            ab.ap[:, :, 1], bet.ap[:], st.ap[:, :, 1], ALU.subtract),
            reads=(bet, st), writes=(ab,))

    xb_fm = [p.sbuf(f"xb_fm{i}", (128, T), BF16, scope=sc_outer)
             for i in range(NF)]
    for fi in range(NF):
        def f_xb(e, fi=fi):
            return e.tensor_scalar(xb_fm[fi].ap[:], xn2_fm[fi].ap[:],
                                   ab.ap[:, fi, 0:1], ab.ap[:, fi, 1:2],
                                   ALU.mult, ALU.add)
        p.emit("dve", f_xb, reads=(xn2_fm[fi], ab), writes=(xb_fm[fi],))
    if "xb" in dbg:
        dump_fm(p, xb_fm, T, "dbg_xb", BF16)

    svv = [p.sbuf(f"pk_sv{j}", (128, NTT, 8, 8), F32, scope=sc_outer)
           for j in range(2)]
    sii = [p.sbuf(f"pk_si{j}", (128, NTT, 8, 8), F32, scope=sc_outer)
           for j in range(2)]
    with p.scope() as sc:
        wraw = [p.sbuf(f"pk_wr{j}", (128, NF, 128), BF16, scope=sc)
                for j in range(2)]
        wlhs = [p.sbuf(f"pk_wl{j}", (128, NF, 128), BF16, scope=sc)
                for j in range(2)]
        wbi = [0]
        kraw = p.sbuf("pk_kr", (128, 2, 128), BF16, scope=sc)
        keysT = p.sbuf("pk_kT", (128, 256), BF16, scope=sc)
        q2b = p.sbuf("pk_q2", (128, T), BF16, scope=sc)
        sim_sb = p.sbuf("pk_sim", (128, 256), F32, scope=sc)
        m8 = p.sbuf("pk_m8", (128, 8), F32, scope=sc)
        mi8 = p.sbuf("pk_mi8", (128, 8), U32, scope=sc)
        for p2 in range(2):
            for h in range(8):
                with p.scope() as scq:
                    ps_q = p.psum("pk_psq", (128, T), F32, scope=scq)
                    ps_t = p.psum("pk_pst", (128, 128), BF16, scope=scq)
                    ps_k = p.psum("pk_psk", (128, 256), BF16, scope=scq)
                    ps_s = p.psum("pk_pss", (128, 256), F32, scope=scq)
                    mrow = p2 * 8 + h
                    wr = wraw[wbi[0] % 2]
                    wl = wlhs[wbi[0] % 2]
                    wbi[0] += 1
                    trans_weight_blk(p, W["w_q"], mrow, NF, wr, wl, ps_t)
                    for kc in range(NF):
                        sl = mvs(T)
                        for mi, (m0, mw) in enumerate(sl):
                            def f_mm(e, kc=kc, m0=m0, mw=mw, wl=wl):
                                return e.matmul(
                                    ps_q.ap[:, m0:m0 + mw], wl.ap[:, kc, :],
                                    xb_fm[kc].ap[:, m0:m0 + mw],
                                    start=(kc == 0), stop=(kc == NF - 1))
                            p.emit("pe", f_mm, reads=(wl, xb_fm[kc]),
                                   writes=(ps_q,) if kc == NF - 1 and
                                   mi == len(sl) - 1 else (),
                                   war=(ps_q,) if kc == 0 and mi == 0 else (),
                                   inc=(mi == len(sl) - 1))

                    def f_qe(e):
                        return e.tensor_copy(q2b.ap[:], ps_q.ap[:])
                    p.emit("dve", f_qe, reads=(ps_q,), writes=(q2b,))

                    def f_kl(e, p2=p2, h=h):
                        return e.dma_start(
                            out=kraw.ap[:],
                            in_=W["keys"].ap[h, :, p2, :].rearrange(
                                "(a b) c -> b a c", b=128))
                    p.emit("pool", f_kl, reads=(W["keys"],), writes=(kraw,),
                           dma_queue="dpool")
                    for half in range(2):
                        def f_kt(e, half=half):
                            return e.transpose(
                                ps_k.ap[:, half * 128:(half + 1) * 128],
                                kraw.ap[:, half, :], p.idn.ap[:])
                        p.emit("pe", f_kt, reads=(kraw, p.idn),
                               writes=(ps_k,) if half else (),
                               war=(ps_k,) if half == 0 else (),
                               inc=(half == 1))

                    def f_ke(e):
                        return e.copy(keysT.ap[:], ps_k.ap[:])
                    p.emit("act", f_ke, reads=(ps_k,), writes=(keysT,))

                    for tt in range(NTT):
                        def f_sm(e, tt=tt):
                            return e.matmul(
                                ps_s.ap[:], q2b.ap[:, tt * 128:(tt + 1) * 128],
                                keysT.ap[:], start=True, stop=True)
                        p.emit("pe", f_sm, reads=(q2b, keysT), writes=(ps_s,),
                               war=(ps_s,))

                        def f_se(e):
                            return e.tensor_copy(sim_sb.ap[:], ps_s.ap[:])
                        p.emit("dve", f_se, reads=(ps_s,), writes=(sim_sb,))

                        def f_m8(e):
                            return e.max(m8.ap[:], sim_sb.ap[:])
                        p.emit("dve", f_m8, reads=(sim_sb,), writes=(m8,))

                        def f_mi(e):
                            return e.max_index(mi8.ap[:], m8.ap[:],
                                               sim_sb.ap[:])
                        p.emit("dve", f_mi, reads=(m8, sim_sb), writes=(mi8,))

                        def f_sv(e, p2=p2, h=h, tt=tt):
                            return e.tensor_copy(svv[p2].ap[:, tt, h, :],
                                                 m8.ap[:])
                        p.emit("dve", f_sv, reads=(m8,), writes=(svv[p2],))

                        def f_si(e, p2=p2, h=h, tt=tt):
                            return e.tensor_copy(sii[p2].ap[:, tt, h, :],
                                                 mi8.ap[:])
                        p.emit("dve", f_si, reads=(mi8,), writes=(sii[p2],))


    idx_all = p.sbuf("pk_idxall", (128, NTT, 64), U32, scope=sc_outer)
    attw_all = p.sbuf("pk_attwall", (128, NTT, 64), F32, scope=sc_outer)
    iota64 = p.sbuf("pk_iota64", (128, 64), F32, scope=sc_outer)
    p.emit("pool", lambda e: e.iota(iota64.ap[:], pattern=[[1, 64]], base=0,
                                    channel_multiplier=0,
                                    allow_small_or_imprecise_dtypes=True),
           writes=(iota64,))
    with p.scope() as sc:
        alls = p.sbuf("pk_alls", (128, 8, 8, 8), F32, scope=sc)
        i256 = p.sbuf("pk_i256", (128, 8, 8), F32, scope=sc)
        alli = p.sbuf("pk_alli", (128, 8, 64), F32, scope=sc)
        sc8 = p.sbuf("pk_sc8", (128, 8, 8), F32, scope=sc)
        pk8 = p.sbuf("pk_pk8", (128, 8, 8), U32, scope=sc)
        pk8f = p.sbuf("pk_pk8f", (128, 8, 8), F32, scope=sc)
        eqb = p.sbuf("pk_eqb", (128, 8, 8, 64), F32, scope=sc)
        idxf = p.sbuf("pk_idxf", (128, 64), F32, scope=sc)
        attw = p.sbuf("pk_attw", (128, 8, 8), F32, scope=sc)
        srow = p.sbuf("pk_srow", (128, 8, 4), F32, scope=sc)
        for tt in range(NTT):
            def f_as(e, tt=tt):
                return e.tensor_tensor(
                    alls.ap[:],
                    svv[0].ap[:, tt, :, :].unsqueeze(3).broadcast_to(
                        (128, 8, 8, 8)),
                    svv[1].ap[:, tt, :, :].unsqueeze(2).broadcast_to(
                        (128, 8, 8, 8)), ALU.add)
            p.emit("dve", f_as, reads=(svv[0], svv[1]), writes=(alls,))

            def f_ai0(e, tt=tt):
                return e.tensor_scalar(i256.ap[:], sii[0].ap[:, tt, :, :],
                                       256.0, None, ALU.mult)
            p.emit("dve", f_ai0, reads=(sii[0],), writes=(i256,))

            def f_ai(e, tt=tt):
                return e.tensor_tensor(
                    alli.ap[:].rearrange("p h (a b) -> p h a b", a=8),
                    i256.ap[:].unsqueeze(3).broadcast_to((128, 8, 8, 8)),
                    sii[1].ap[:, tt, :, :].unsqueeze(2).broadcast_to(
                        (128, 8, 8, 8)), ALU.add)
            p.emit("dve", f_ai, reads=(i256, sii[1]), writes=(alli,))

            for h in range(8):
                def f_m8(e, h=h):
                    return e.max(sc8.ap[:, h, :],
                                 alls.ap[:, h, :, :].rearrange(
                                     "p a b -> p (a b)"))
                p.emit("dve", f_m8, reads=(alls,), writes=(sc8,))

                def f_mi(e, h=h):
                    return e.max_index(pk8.ap[:, h, :], sc8.ap[:, h, :],
                                       alls.ap[:, h, :, :].rearrange(
                                           "p a b -> p (a b)"))
                p.emit("dve", f_mi, reads=(sc8, alls), writes=(pk8,))

            def f_pf(e):
                return e.tensor_copy(pk8f.ap[:], pk8.ap[:])
            p.emit("dve", f_pf, reads=(pk8,), writes=(pk8f,))

            def f_eq(e):
                return e.tensor_tensor(
                    eqb.ap[:],
                    iota64.ap[:].unsqueeze(1).unsqueeze(2).broadcast_to(
                        (128, 8, 8, 64)),
                    pk8f.ap[:].unsqueeze(3).broadcast_to((128, 8, 8, 64)),
                    ALU.is_equal)
            p.emit("dve", f_eq, reads=(iota64, pk8f), writes=(eqb,))

            def f_eqm(e):
                return e.tensor_tensor(
                    eqb.ap[:], eqb.ap[:],
                    alli.ap[:].unsqueeze(2).broadcast_to((128, 8, 8, 64)),
                    ALU.mult)
            p.emit("dve", f_eqm, reads=(eqb, alli), writes=(eqb,))

            def f_red(e):
                return e.tensor_reduce(
                    idxf.ap[:].rearrange("p (h s) -> p h s", h=8).unsqueeze(3),
                    eqb.ap[:], AX.X, ALU.add)
            p.emit("dve", f_red, reads=(eqb,), writes=(idxf,))

            def f_iu(e, tt=tt):
                return e.tensor_copy(idx_all.ap[:, tt, :], idxf.ap[:])
            p.emit("dve", f_iu, reads=(idxf,), writes=(idx_all,))

            def f_smx(e):
                return e.tensor_reduce(srow.ap[:, :, 0:1], sc8.ap[:],
                                       AX.X, ALU.max)
            p.emit("dve", f_smx, reads=(sc8,), writes=(srow,))

            def f_ssub(e):
                return e.tensor_tensor(
                    attw.ap[:], sc8.ap[:],
                    srow.ap[:, :, 0:1].broadcast_to((128, 8, 8)),
                    ALU.subtract)
            p.emit("dve", f_ssub, reads=(sc8, srow), writes=(attw,))

            def f_sex(e):
                return e.activation(attw.ap[:], attw.ap[:], AF.Exp)
            p.emit("act", f_sex, reads=(attw,), writes=(attw,))

            def f_ssm(e):
                return e.tensor_reduce(srow.ap[:, :, 1:2], attw.ap[:],
                                       AX.X, ALU.add)
            p.emit("dve", f_ssm, reads=(attw,), writes=(srow,))

            def f_src(e):
                return e.reciprocal(srow.ap[:, :, 2:3], srow.ap[:, :, 1:2])
            p.emit("dve", f_src, reads=(srow,), writes=(srow,))

            def f_sw(e, tt=tt):
                return e.tensor_tensor(
                    attw_all.ap[:, tt, :].rearrange("p (h s) -> p h s", h=8),
                    attw.ap[:],
                    srow.ap[:, :, 2:3].broadcast_to((128, 8, 8)), ALU.mult)
            p.emit("dve", f_sw, reads=(attw, srow), writes=(attw_all,))
    return idx_all, attw_all


def b1_pkm_ctx(p, T, sc, xn2_tm, W, idx_all, attw_all, g2_hbm):
    NG = 5
    ctx = {"NG": NG, "xn2_tm": xn2_tm, "W": W, "idx_all": idx_all,
           "attw_all": attw_all, "g2_hbm": g2_hbm}
    ctx["G1"] = [p.sbuf(f"pk_G1{j}", (128, C), BF16, scope=sc)
                 for j in range(NG)]
    ctx["G2"] = [p.sbuf(f"pk_G2{j}", (128, C), BF16, scope=sc)
                 for j in range(NG)]
    ctx["junk"] = p.sbuf("pk_junk", (128, C), BF16, scope=sc)
    ctx["otk"] = p.sbuf("pk_otk", (128, 64), F32, scope=sc)
    ctx["ya"] = [p.sbuf(f"pk_ya{j}", (128, C), F32, scope=sc)
                 for j in range(2)]
    ctx["yfin"] = p.sbuf("pk_yfin", (128, C), F32, scope=sc)
    ctx["g2t"] = p.sbuf("pk_g2t", (128, C), BF16, scope=sc)
    return ctx


def b1_pkm_tile(p, T, ctx, tt):
    NG = ctx["NG"]
    PUMP = 16
    G1, G2 = ctx["G1"], ctx["G2"]
    junk, otk, ya, yfin = ctx["junk"], ctx["otk"], ctx["ya"], ctx["yfin"]
    g2t = ctx["g2t"]
    W, idx_all, attw_all = ctx["W"], ctx["idx_all"], ctx["attw_all"]
    xn2_tm = ctx["xn2_tm"]
    for s_ in range(64):
        g = G1[s_ % NG]

        def f_g(e, tt=tt, s_=s_, g=g):
            return e.indirect_dma_start(
                out=g.ap[:], out_offset=None, in_=W["key_in"].ap[:],
                in_offset=bass.IndirectOffsetOnAxis(
                    ap=idx_all.ap[:, tt, s_:s_ + 1], axis=0))
        p.emit("pool", f_g, reads=(W["key_in"], idx_all), writes=(g,),
               dma_queue="dpool")

        def f_o(e, s_=s_, tt=tt, g=g):
            return e.scalar_tensor_tensor(
                junk.ap[:], g.ap[:], 1.0, xn2_tm[tt].ap[:],
                ALU.mult, ALU.mult, accum_out=otk.ap[:, s_:s_ + 1])
        p.emit("dve", f_o, reads=(g, xn2_tm[tt]), writes=(otk,),
               war=(junk,))
        if s_ % PUMP == PUMP - 1:
            yield

    def f_gl(e):
        return e.activation(otk.ap[:], otk.ap[:], AF.Gelu)
    p.emit("act", f_gl, reads=(otk,), writes=(otk,))

    def f_aw(e, tt=tt):
        return e.tensor_tensor(otk.ap[:], otk.ap[:],
                               attw_all.ap[:, tt, :], ALU.mult)
    p.emit("dve", f_aw, reads=(otk, attw_all), writes=(otk,))

    for j in range(2):
        p.emit("dve", lambda e, j=j: e.memset(ya[j].ap[:], 0.0),
               writes=(ya[j],))
    for s_ in range(64):
        g = G2[s_ % NG]

        def f_g(e, tt=tt, s_=s_, g=g):
            return e.indirect_dma_start(
                out=g.ap[:], out_offset=None, in_=W["key_out"].ap[:],
                in_offset=bass.IndirectOffsetOnAxis(
                    ap=idx_all.ap[:, tt, s_:s_ + 1], axis=0))
        p.emit("pool", f_g, reads=(W["key_out"], idx_all), writes=(g,),
               dma_queue="dpool")

        acc = ya[s_ % 2]

        def f_y(e, s_=s_, g=g, acc=acc):
            return e.scalar_tensor_tensor(
                acc.ap[:], g.ap[:], otk.ap[:, s_:s_ + 1], acc.ap[:],
                ALU.mult, ALU.add)
        p.emit("dve", f_y, reads=(g, otk, acc), writes=(acc,))
        if s_ % PUMP == PUMP - 1:
            yield

    def f_t1(e):
        return e.tensor_tensor(yfin.ap[:], ya[0].ap[:], ya[1].ap[:],
                               ALU.add)
    p.emit("dve", f_t1, reads=(ya[0], ya[1]), writes=(yfin,))

    def f_g2(e):
        return e.activation(g2t.ap[:], yfin.ap[:], AF.Silu)
    p.emit("act", f_g2, reads=(yfin,), writes=(g2t,))

    def f_gs(e, tt=tt):
        return e.dma_start(out=ctx["g2_hbm"].ap[tt * 128:(tt + 1) * 128, :],
                           in_=g2t.ap[:])
    p.emit("sp", f_gs, reads=(g2t,), writes=(ctx["g2_hbm"],),
           dma_queue="dsp")
    yield


def b1_combine_tail(p, T, g2_hbm, s2f_hbm, x1t_hbm, out_hbm):
    NTT = T // 128
    with p.scope() as sc:
        x1t = p.sbuf("cbt_x1t", (128, C), F32, scope=sc)
        g2l = p.sbuf("cbt_g2", (128, C), BF16, scope=sc)
        s2stage = p.sbuf("cbt_s2st", (128, C), BF16, scope=sc)
        s2t = p.sbuf("cbt_s2t", (128, C), BF16, scope=sc)
        out_t = p.sbuf("cbt_out", (128, C), F32, scope=sc)
        ps_t2 = [p.psum(f"cbt_pst{j}", (128, 128), BF16, scope=sc)
                 for j in range(2)]
        for tt in range(NTT):
            def f_x1(e, tt=tt):
                return e.dma_start(out=x1t.ap[:],
                                   in_=x1t_hbm.ap[tt * 128:(tt + 1) * 128, :])
            p.emit("sp", f_x1, reads=(x1t_hbm,), writes=(x1t,),
                   dma_queue="dsp")

            def f_gl(e, tt=tt):
                return e.dma_start(out=g2l.ap[:],
                                   in_=g2_hbm.ap[tt * 128:(tt + 1) * 128, :])
            p.emit("sp", f_gl, reads=(g2_hbm,), writes=(g2l,),
                   dma_queue="dsp")

            for fi in range(NF):
                def f_sl(e, tt=tt, fi=fi):
                    return e.dma_start(
                        out=s2stage.ap[:, fi * 128:(fi + 1) * 128],
                        in_=s2f_hbm.ap[fi * 128:(fi + 1) * 128,
                                       tt * 128:(tt + 1) * 128])
                p.emit("sp", f_sl, reads=(s2f_hbm,), writes=(s2stage,),
                       dma_queue="dsp")
            for fi in range(NF):
                pst = ps_t2[fi % 2]

                def f_tr(e, fi=fi, pst=pst):
                    return e.transpose(
                        pst.ap[:], s2stage.ap[:, fi * 128:(fi + 1) * 128],
                        p.idn.ap[:])
                p.emit("pe", f_tr, reads=(s2stage, p.idn), writes=(pst,))

                def f_ev(e, fi=fi, pst=pst):
                    return e.copy(s2t.ap[:, fi * 128:(fi + 1) * 128],
                                  pst.ap[:])
                p.emit("act", f_ev, reads=(pst,), writes=(s2t,))

            def f_sg(e):
                return e.tensor_tensor(out_t.ap[:], s2t.ap[:], g2l.ap[:],
                                       ALU.mult)
            p.emit("dve", f_sg, reads=(s2t, g2l), writes=(out_t,))

            def f_add(e):
                return e.tensor_tensor(out_t.ap[:], out_t.ap[:], x1t.ap[:],
                                       ALU.add)
            p.emit("dve", f_add, reads=(out_t, x1t), writes=(out_t,))

            def f_out(e, tt=tt):
                return e.dma_start(
                    out=out_hbm.ap[tt * 128:(tt + 1) * 128, :],
                    in_=out_t.ap[:])
            p.emit("sp", f_out, reads=(out_t,), writes=(out_hbm,),
                   dma_queue="dsp")


def b1_body(p, T, W, x1f_hbm, x1t_hbm, out_hbm, dbg, ncores):
    NTT = T // 128
    with p.scope() as scb1:
        xn2_fm = [p.sbuf(f"xn2_fm{i}", (128, T), BF16, scope=scb1)
                  for i in range(NF)]
        xn2_tm = [p.sbuf(f"xn2_tm{i}", (128, C), BF16, scope=scb1)
                  for i in range(T // 128)]
        b1_entry(p, T, x1f_hbm, xn2_fm, xn2_tm, dbg, ncores=ncores)
        if "xn2" in dbg:
            dump_fm(p, xn2_fm, T, "dbg_xn2", BF16)
        s2f_hbm = (p.dram_out("dbg_s2", (C, T), BF16) if "s2" in dbg
                   else p.dram_tmp("s2f_hbm", (C, T), BF16))
        idx_all, attw_all = b1_pkm_pre(p, T, xn2_fm, xn2_tm, W, dbg,
                                       ncores=ncores, sc_outer=scb1)
        g2_hbm = p.dram_tmp("g2_hbm", (T, C), BF16)
        biases = {
            "bq": load_featvec(p, "bq_fv", W["bq"].ap, C, scope=scb1),
            "bk": load_featvec(p, "bk_fv", W["bk"].ap, C, scope=scb1),
            "bv": load_featvec(p, "bv_fv", W["bv"].ap, C, scope=scb1),
            "bo": load_featvec(p, "bo_fv", W["bo"].ap, C, scope=scb1),
        }
        d_hbm = p.dram_tmp("d_hbm", (16, T), F32)
        hm_col = p.sbuf("at_hmcol", (16, 1), F32, scope=scb1)
        p.emit("sp", lambda e: e.dma_start(out=hm_col.ap[:],
                                           in_=W["hm"].ap.unsqueeze(1)),
               reads=(W["hm"],), writes=(hm_col,), dma_queue="dsp")
        with p.scope() as scpk:
            ctx2 = b1_pkm_ctx(p, T, scpk, xn2_tm, W, idx_all, attw_all,
                              g2_hbm)
            gens = [b1_pkm_tile(p, T, ctx2, tt) for tt in range(NTT)]
            gens.reverse()

            def pump(n, gens=gens):
                while n > 0 and gens:
                    try:
                        next(gens[-1])
                        n -= 1
                    except StopIteration:
                        gens.pop()
            with p.scope() as scat:
                ctx = b1_attn_setup(p, T, W, scat, biases)
                ctx["d_hbm"] = d_hbm
                ctx["hm_col"] = hm_col
                for mc in range(NF):
                    b1_attn_chunk(p, T, ctx, xn2_fm, W, mc, dbg, pump)
                pump(10 ** 9)
                attn_hbm = ctx["attn_hbm"]
        b1_attn_wo(p, T, ctx, W, s2f_hbm)
        b1_combine_tail(p, T, g2_hbm, s2f_hbm, x1t_hbm, out_hbm)


def build_b1(T=2048, dbg=(), ncores=1):
    p = Prog("b1")
    W = declare_b1_weights(p)
    x1f_hbm = p.dram_in("x1f", (C, T), F32)
    x1t_hbm = p.dram_in("x1t", (T, C), F32)
    out_hbm = p.dram_out("out", (T, C), F32)
    setup_consts(p, T)
    b1_body(p, T, W, x1f_hbm, x1t_hbm, out_hbm, dbg, ncores)
    return p



def declare_b0_weights(p, T):
    W = {}
    W["maa_x"] = p.dram_in("maa_x", (1, 1, C), F32)
    W["maa_k"] = p.dram_in("maa_k", (1, 1, C), F32)
    W["maa_v"] = p.dram_in("maa_v", (1, 1, C), F32)
    W["maa_r"] = p.dram_in("maa_r", (1, 1, C), F32)
    W["w1"] = p.dram_in("w1", (C, 96), F32)
    W["w2"] = p.dram_in("w2", (3, 32, C), F32)
    W["u"] = p.dram_in("u", (16, 64), F32)
    W["Wr"] = p.dram_in("Wr", (C, C), F32)
    W["Wk"] = p.dram_in("Wk", (C, C), F32)
    W["Wvg"] = p.dram_in("Wvg", (C, 6 * C), F32)
    W["Wo"] = p.dram_in("Wo", (3 * C, C), F32)
    W["ln_g"] = p.dram_in("ln_g", (C,), F32)
    W["ln_b"] = p.dram_in("ln_b", (C,), F32)
    W["mWr"] = p.dram_in("mWr", (C, 8), F32)
    W["mbr"] = p.dram_in("mbr", (8,), F32)
    W["mW1"] = p.dram_in("mW1", (8, C, 4096), F32)
    W["mb1"] = p.dram_in("mb1", (8, 4096), F32)
    W["mW2"] = p.dram_in("mW2", (8, 4096, C), F32)
    W["mb2"] = p.dram_in("mb2", (8, C), F32)
    return W


def build_b0(T=2048, upto="all", dbg=()):
    p = Prog("b0")
    x_in = p.dram_in("x", (T, C), F32)
    W = declare_b0_weights(p, T)
    setup_consts(p, T)
    xn_hbm = p.dram_tmp("xn_hbm", (C, T), BF16)
    maa = {nm: load_featvec(p, nm,
                            W[nm].ap.rearrange("a b c -> (a b c)"), C)
           for nm in ("maa_x", "maa_k", "maa_v", "maa_r")}
    u_fm = load_featvec(p, "u_fv", W["u"].ap.rearrange("a b -> (a b)"), C)
    scmid = p.scope()
    scmid_es = scmid.__enter__()
    SEGW = min(256, T)
    QCHW = 1024
    nbAW = SEGW // 8
    p.wkv_bufs = {
        "P0": p.sbuf("wk_P0", (128, nbAW * 64), F32, scope=scmid_es),
        "E": p.sbuf("wk_E", (128, nbAW * 64), F32, scope=scmid_es),
        "RK": p.sbuf("wk_RK", (128, nbAW * 64), BF16, scope=scmid_es),
        "OIt": p.sbuf("wk_OIt", (128, QCHW), F32, scope=scmid_es),
        "OIr": p.sbuf("wk_OIr", (128, SEGW), F32, scope=scmid_es),
        "Msb": p.sbuf("wk_Msb", (2, QCHW), F32, scope=scmid_es),
    }
    xk_fm = [p.sbuf(f"xk_fm{i}", (128, T), BF16, scope=scmid_es)
             for i in range(NF)]
    xv_fm = [p.sbuf(f"xv_fm{i}", (128, T), BF16, scope=scmid_es)
             for i in range(NF)]
    xr_fm = [p.sbuf(f"xr_fm{i}", (128, T), BF16, scope=scmid_es)
             for i in range(NF)]
    with p.scope() as scxn:
        xn_fm = [p.sbuf(f"xn_fm{i}", (128, T), BF16, scope=scxn)
                 for i in range(NF)]
        b0_entry(p, T, x_in, xn_fm)
        if "xn" in dbg:
            dump_fm(p, xn_fm, T, "dbg_xn", BF16)
        if upto == "entry":
            return p
        b0_mixer(p, T, xn_fm, W, maa, dbg, xk_fm, xv_fm, xr_fm)
        for fi in range(NF):
            def f_px(e, fi=fi):
                return e.dma_start(out=xn_hbm.ap[fi * 128:(fi + 1) * 128, :],
                                   in_=xn_fm[fi].ap[:])
            p.emit("sp", f_px, reads=(xn_fm[fi],), writes=(xn_hbm,),
                   dma_queue="dsp")
    if "xk" in dbg:
        dump_fm(p, xk_fm, T, "dbg_xk", BF16)
        dump_fm(p, xv_fm, T, "dbg_xv", BF16)
        dump_fm(p, xr_fm, T, "dbg_xr", BF16)
    if upto == "mixer":
        return p
    o_hbm = (p.dram_out("dbg_o", (C, T), BF16) if "o" in dbg
             else p.dram_tmp("o_hbm", (C, T), BF16))
    ffn_hbm = (p.dram_out("dbg_ffn", (2 * C, T), BF16) if "ffn" in dbg
               else p.dram_tmp("ffn_hbm", (2 * C, T), BF16))
    b0_projections(p, T, xk_fm, xv_fm, xr_fm, W, u_fm, o_hbm, ffn_hbm, dbg,
                   do_wkv=(upto != "proj"))
    scmid.__exit__(None, None, None)
    if upto in ("proj", "wkv"):
        return p
    if upto == "fused":
        x1_hbm = p.dram_tmp("x1f", (C, T), F32)
        x1t_hbm = p.dram_tmp("x1t", (T, C), F32)
    elif upto == "launch":
        x1_hbm = p.dram_out("x1f", (C, T), F32)
        x1t_hbm = p.dram_out("x1t", (T, C), F32)
    else:
        x1_hbm = (p.dram_out("dbg_x1", (C, T), F32) if "x1" in dbg
                  else p.dram_tmp("x1_hbm", (C, T), F32))
        x1t_hbm = None
    b0_output(p, T, xn_hbm, o_hbm, ffn_hbm, W, x_in, x1_hbm, dbg,
              x1t_hbm=x1t_hbm)
    if upto == "fused":
        return p, x1_hbm, x1t_hbm
    return p


def build_fused(T=2048, ncores=8, dbg=()):
    p, x1f_hbm, x1t_hbm = build_b0(T=T, upto="fused", dbg=dbg)
    W1 = declare_b1_weights(p)
    out_hbm = p.dram_out("out", (T, C), F32)
    b1_body(p, T, W1, x1f_hbm, x1t_hbm, out_hbm, dbg, ncores)
    return p


B0_INMAP = {
    "maa_x": "L0_attn_maa_x", "maa_k": "L0_attn_maa_k",
    "maa_v": "L0_attn_maa_v", "maa_r": "L0_attn_maa_r",
    "w1": "L0_attn_w1", "w2": "L0_attn_w2", "u": "L0_attn_u",
    "Wr": "L0_attn_Wr", "Wk": "L0_attn_Wk", "Wvg": "L0_attn_Wvg",
    "Wo": "L0_attn_Wo", "ln_g": "L0_attn_ln_g", "ln_b": "L0_attn_ln_b",
    "mWr": "L0_ffn_Wr", "mbr": "L0_ffn_br", "mW1": "L0_ffn_W1",
    "mb1": "L0_ffn_b1", "mW2": "L0_ffn_W2", "mb2": "L0_ffn_b2",
}
B1_INMAP = {
    "aWq": "L1_attn_Wq", "abq": "L1_attn_bq", "aWk": "L1_attn_Wk",
    "abk": "L1_attn_bk", "aWv": "L1_attn_Wv", "abv": "L1_attn_bv",
    "aWo": "L1_attn_Wo", "abo": "L1_attn_bo", "ahm": "L1_attn_head_mask",
    "pgamma": "L1_ffn_gamma", "pbeta": "L1_ffn_beta", "pwq": "L1_ffn_w_q",
    "pkeys": "L1_ffn_keys", "pkin": "L1_ffn_key_in",
    "pkout": "L1_ffn_key_out",
}


def kernel(**inputs):
    T, NCORES = 2048, 8
    x = np.ascontiguousarray(np.asarray(inputs["L0_attn_x"], np.float32))

    p = build_fused(T=T, ncores=NCORES)
    nc = p.finalize()
    w = {k: np.ascontiguousarray(np.asarray(inputs[v], np.float32))
         for k, v in B0_INMAP.items()}
    w.update({k: np.ascontiguousarray(np.asarray(inputs[v], np.float32))
              for k, v in B1_INMAP.items()})
    maps = [dict(w, x=np.ascontiguousarray(x[i])) for i in range(NCORES)]
    res = run_bass_kernel_spmd(nc, maps, core_ids=list(range(NCORES)))

    return np.stack([np.asarray(res.results[i]["out"], np.float32)
                     for i in range(NCORES)], axis=0)
